# version 20
# baseline (speedup 1.0000x reference)
import math
from contextlib import ExitStack

import numpy as np
import ml_dtypes

import concourse.bass as bass
import concourse.mybir as mybir
from concourse.bass_utils import run_bass_kernel_spmd
from concourse.bass_core_rust import IndirectOffsetOnAxis

F32 = mybir.dt.float32
BF16 = mybir.dt.bfloat16
I32 = mybir.dt.int32
U32 = mybir.dt.uint32
AF = mybir.ActivationFunctionType
ALU = mybir.AluOpType
AX = mybir.AxisListType

PE, ACT, DVE, POOL, SP = "PE", "ACT", "DVE", "POOL", "SP"
COMPUTE_ENGINES = (PE, ACT, DVE, POOL, SP)
N_DMA_SEMS = {SP: 8, POOL: 6}
N_CC_SEMS = 2

NEG_BIG = -1.0e30
EXP_SHIFT = 12.0


class Cfg:
    def __init__(self, B=4, T=4096, D=1024, F=1408, E=16, TOPK=2, CAPF=1.25,
                 HA=8, KV=2, HD=128, DLAT=32, HB=16):
        self.B, self.T, self.D, self.F, self.E = B, T, D, F, E
        self.TOPK, self.CAPF = TOPK, CAPF
        self.HA, self.KV, self.HD, self.DLAT = HA, KV, HD, DLAT
        self.HB = HB
        self.hdB = D // HB
        self.N = B * T
        self.C = int(math.ceil(CAPF * self.N * TOPK / E))
        self.NCORES = 2 * B
        self.NT = T // 2
        self.NSLOT = self.NT // 128
        self.GT = T // 128
        self.ND = D // 128
        self.NF = F // 128
        self.EPC = E // self.NCORES
        assert E % self.NCORES == 0 and T % 256 == 0 and D % 128 == 0 and F % 128 == 0
        assert self.C % 128 == 0, "capacity must be tile aligned for the gathers"
        self.CT = self.C // 128
        self.MYROWS = self.EPC * self.C



class _Op:
    __slots__ = ("eng", "fn", "seq", "idx", "kind", "inc", "sem", "sem_count",
                 "waits", "ksnap", "accum_key", "dbg", "inst_name")

    def __init__(self, eng, fn, seq, idx, kind, dbg):
        self.eng = eng
        self.fn = fn
        self.seq = seq
        self.idx = idx
        self.kind = kind
        self.inc = False
        self.sem = None
        self.sem_count = None
        self.waits = []
        self.ksnap = None
        self.accum_key = None
        self.dbg = dbg


class Emitter:
    def __init__(self, nc):
        self.nc = nc
        self.ops = []
        self.stream = {e: [] for e in COMPUTE_ENGINES}
        self.access = {}
        self.K = {e: {} for e in COMPUTE_ENGINES}
        self.dma_idx = {SP: 0, POOL: 0}
        self.cc_idx = 0
        self._es = ExitStack()
        self.engine_sems = {e: self._es.enter_context(nc.semaphore(f"es_{e}"))
                            for e in COMPUTE_ENGINES}
        self.dma_sems = {q: [self._es.enter_context(nc.semaphore(f"dq_{q}_{i}"))
                             for i in range(n)] for q, n in N_DMA_SEMS.items()}
        self.cc_sems = [self._es.enter_context(nc.semaphore(f"cc_{i}"))
                        for i in range(N_CC_SEMS)]
        self.psum_names = set()


    @staticmethod
    def _isect(a, b):
        return not (a[1] <= b[0] or b[1] <= a[0] or a[3] <= b[2] or b[3] <= a[2])

    @staticmethod
    def _covers(outer, inner):
        return (outer[0] <= inner[0] and outer[1] >= inner[1]
                and outer[2] <= inner[2] and outer[3] >= inner[3])

    def _knows(self, K, p):
        if p.kind == "compute":
            return K.get(("st", p.eng), -1) >= p.idx
        return K.get(p.sem, 0) >= p.sem_count

    def _merge_from(self, K, p):
        snap = p.ksnap
        if snap:
            for k, v in snap.items():
                if K.get(k, -1 if k[0] == "st" else 0) < v:
                    K[k] = v
        if p.kind == "compute":
            k = ("st", p.eng)
            if K.get(k, -1) < p.idx:
                K[k] = p.idx
        else:
            if K.get(p.sem, 0) < p.sem_count:
                K[p.sem] = p.sem_count


    def op(self, eng, fn, reads=(), writes=(), kind="compute", accum_key=None,
           inc=False, dbg=""):
        idx = len(self.stream[eng])
        o = _Op(eng, fn, len(self.ops), idx, kind, dbg)
        o.accum_key = accum_key
        o.inc = inc
        K = self.K[eng]
        if kind == "dma":
            i = self.dma_idx[eng]
            o.sem = ("dma", eng, i % N_DMA_SEMS[eng])
            o.sem_count = i // N_DMA_SEMS[eng] + 1
            o.inc = True
            self.dma_idx[eng] = i + 1
            if o.sem_count > 1 and K.get(o.sem, 0) < o.sem_count - 1:
                o.waits.append((o.sem, o.sem_count - 1))
                K[o.sem] = o.sem_count - 1
        elif kind == "coll":
            i = self.cc_idx
            o.sem = ("cc", POOL, i % N_CC_SEMS)
            o.sem_count = i // N_CC_SEMS + 1
            o.inc = True
            self.cc_idx = i + 1
            if o.sem_count > 1 and K.get(o.sem, 0) < o.sem_count - 1:
                o.waits.append((o.sem, o.sem_count - 1))
                K[o.sem] = o.sem_count - 1

        prods = []
        seen = set()
        for (name, rect) in reads:
            self._scan(name, rect, False, prods, seen)
        for (name, rect) in writes:
            self._scan(name, rect, True, prods, seen)
        prods.sort(key=lambda p: p.seq)
        for p in reversed(prods):
            if p is o or self._knows(K, p):
                continue
            if (accum_key is not None and p.eng == eng and
                    p.accum_key == accum_key):
                continue
            if p.eng == eng and p.kind == "compute" and kind in ("compute", "drain"):
                d = _Op(eng, (lambda e: e.drain()), len(self.ops),
                        len(self.stream[eng]), "drain", "drain")
                d.ksnap = None
                self.ops.append(d)
                self.stream[eng].append(d)
                K[("st", eng)] = d.idx - 1
                o.idx = len(self.stream[eng])
            elif p.kind == "compute":
                p.inc = True
                o.waits.append((("eng", p.eng), p))
                self._merge_from(K, p)
            else:
                if K.get(p.sem, 0) < p.sem_count:
                    o.waits.append((p.sem, p.sem_count))
                self._merge_from(K, p)
        BR = (0, 128, 0, 1)
        bank_names = set()
        if eng in (PE, ACT, DVE):
            for (name, rect) in list(reads) + list(writes):
                if name in self.psum_names:
                    bank_names.add(name)
        if bank_names:
            prods2 = []
            seen2 = set()
            for name in bank_names:
                self._scan(("B", name), BR, eng == PE, prods2, seen2)
            prods2.sort(key=lambda p: p.seq)
            for p in reversed(prods2):
                if p is o or p.eng == eng or self._knows(K, p):
                    continue
                if p.kind == "compute":
                    p.inc = True
                    o.waits.append((("eng", p.eng), p))
                    self._merge_from(K, p)
                else:
                    if K.get(p.sem, 0) < p.sem_count:
                        o.waits.append((p.sem, p.sem_count))
                    self._merge_from(K, p)
        o.ksnap = dict(K)
        self.ops.append(o)
        self.stream[eng].append(o)
        for (name, rect) in reads:
            self.access.setdefault(name, []).append((o, False, rect))
        for (name, rect) in writes:
            self.access.setdefault(name, []).append((o, True, rect))
        for name in bank_names:
            self.access.setdefault(("B", name), []).append((o, eng == PE, BR))
        return o

    def _scan(self, name, rect, is_write, prods, seen):
        log = self.access.get(name)
        if not log:
            return
        for (p, p_is_write, p_rect) in reversed(log):
            if not (p_is_write or is_write):
                continue
            if self._isect(rect, p_rect):
                if id(p) not in seen:
                    seen.add(id(p))
                    prods.append(p)
                if p_is_write and self._covers(p_rect, rect):
                    return

    def barrier(self):
        marks = {}
        for e in COMPUTE_ENGINES:
            for o in reversed(self.stream[e]):
                if o.kind == "compute":
                    o.inc = True
                    marks[e] = o
                    break
        sem_marks = []
        for q in N_DMA_SEMS:
            n = self.dma_idx[q]
            for i in range(min(n, N_DMA_SEMS[q])):
                cnt = (n - 1 - i) // N_DMA_SEMS[q] + 1
                sem_marks.append((("dma", q, (n - 1 - i) % N_DMA_SEMS[q]), cnt))

        for e in COMPUTE_ENGINES:
            d = _Op(e, (lambda eng: eng.drain()), len(self.ops),
                    len(self.stream[e]), "drain", "bar_drain")
            self.ops.append(d)
            self.stream[e].append(d)
            o = _Op(e, (lambda eng: eng.nop()), len(self.ops),
                    len(self.stream[e]), "compute", "barrier")
            K = self.K[e]
            K[("st", e)] = d.idx - 1
            for e2, m in marks.items():
                if e2 != e and not self._knows(K, m):
                    o.waits.append((("eng", e2), m))
                    self._merge_from(K, m)
            for (semkey, cnt) in sem_marks:
                if K.get(semkey, 0) < cnt:
                    o.waits.append((semkey, cnt))
                    K[semkey] = cnt
            o.ksnap = dict(K)
            self.ops.append(o)
            self.stream[e].append(o)
        kept = {}
        for name, log in self.access.items():
            k = [(p, w, r) for (p, w, r) in log if p.kind == "coll"]
            if k:
                kept[name] = k
        self.access = kept


    def flush(self):
        cnt = {e: 0 for e in COMPUTE_ENGINES}
        for e in COMPUTE_ENGINES:
            for o in self.stream[e]:
                if o.kind == "compute" and o.inc:
                    cnt[e] += 1
                    o.sem = ("eng", e)
                    o.sem_count = cnt[e]
        nc = self.nc

        def sem_of(key):
            if key[0] == "eng":
                return self.engine_sems[key[1]]
            if key[0] == "dma":
                return self.dma_sems[key[1]][key[2]]
            return self.cc_sems[key[2]]

        def make_runner(stream):
            def run(eng):
                vc = {}
                for o in stream:
                    for (key, val) in o.waits:
                        if isinstance(val, _Op):
                            if val.sem_count is None:
                                continue
                            key = ("eng", val.eng)
                            val = val.sem_count
                        else:
                            val = val * (16 if key[0] == "dma" else 1)
                        if vc.get(key, 0) >= val:
                            continue
                        eng.wait_ge(sem_of(key), val)
                        vc[key] = val
                    inst = o.fn(eng)
                    if o.inc and o.sem is not None:
                        amt = 16 if o.kind == "dma" else 1
                        inst.then_inc(sem_of(o.sem), amt)
            return run

        with nc.Block() as blk:
            deco = {PE: blk.tensor, ACT: blk.scalar, DVE: blk.vector,
                    POOL: blk.gpsimd, SP: blk.sync}
            for e in COMPUTE_ENGINES:
                deco[e](make_runner(self.stream[e]))
        nc.finalize()



class Buf:
    def __init__(self, name, handle, shape):
        self.name = name
        self.h = handle
        self.shape = shape

    def __getitem__(self, idx):
        return self.h[idx]

    def r(self, ps=None, fs=None):
        p0, p1 = (0, self.shape[0]) if ps is None else ps
        f0, f1 = (0, int(np.prod(self.shape[1:]))) if fs is None else fs
        return (self.name, (p0, p1, f0, f1))


class Builder:
    def __init__(self, nc, cfg):
        self.nc = nc
        self.cfg = cfg
        self.em = Emitter(nc)
        self.es = ExitStack()
        self.bufs = {}
        self.inputs = {}

    def sb(self, name, shape, dtype):
        cm = self.nc.sbuf_tensor(name, list(shape), dtype)
        h = cm.__enter__()
        b = Buf(name, h, tuple(shape))
        b._cm = cm
        self.bufs[name] = b
        return b

    def free(self, *bufs):
        for b in bufs:
            try:
                b._cm.__exit__(None, None, None)
                b._cm = None
            except AssertionError as e:
                live = [x for x in self.bufs
                        if getattr(self.bufs[x], "_cm", None) is not None]
                raise AssertionError(f"{e} (buf={b.name}; live={live})") from e


    def psum(self, name, shape, dtype):
        h = self.es.enter_context(self.nc.psum_tensor(name, list(shape), dtype))
        b = Buf(name, h, tuple(shape))
        self.bufs[name] = b
        self.em.psum_names.add(name)
        return b

    def dram(self, name, shape, dtype, kind="Internal"):
        h = self.nc.dram_tensor(name, list(shape), dtype, kind=kind)
        b = Buf(name, h.ap(), tuple(shape))
        self.bufs[name] = b
        return b



def make_consts(cfg, core):
    b, par = core // 2, core % 2
    C = {}
    ident = np.eye(128, dtype=np.float32)
    C["ident"] = ident.astype(ml_dtypes.bfloat16)
    C["negbig_ident"] = (NEG_BIG * ident).astype(ml_dtypes.bfloat16)
    tstrict = (np.arange(128)[:, None] < np.arange(128)[None, :])
    C["tstrict"] = tstrict.astype(ml_dtypes.bfloat16)
    tri_mask = (np.arange(128)[:, None] > np.arange(128)[None, :]).astype(np.float32)
    ones = np.ones((128, 128), np.float32)
    zeros = np.zeros((128, 128), np.float32)
    mp = np.stack([tri_mask, ones] if par == 0 else [zeros, tri_mask])
    C["maskpair"] = mp.astype(ml_dtypes.bfloat16)
    C["maskpair4"] = np.tile(mp, (1, 1, 4)).astype(ml_dtypes.bfloat16)
    C["tri"] = tri_mask.astype(ml_dtypes.bfloat16)
    hd = cfg.hdB
    pos = np.zeros((cfg.NT,), np.float64)
    for s in range(cfg.NSLOT):
        g = 2 * s + par
        pos[s * 128:(s + 1) * 128] = g * 128 + np.arange(128)
    d = np.arange(128) % hd
    freqs = 1.0 / (10000.0 ** (2 * (d // 2) / hd))
    ang = freqs[:, None] * pos[None, :]
    C["ropecos"] = np.cos(ang).astype(ml_dtypes.bfloat16)
    C["ropesin"] = np.sin(ang).astype(ml_dtypes.bfloat16)
    P = np.zeros((hd, hd), np.float32)
    for i in range(hd // 2):
        P[2 * i, 2 * i + 1] = -1.0
        P[2 * i + 1, 2 * i] = 1.0
    nb = 128 // hd
    PT = np.zeros((128, 128), np.float32)
    for k in range(nb):
        PT[k * hd:(k + 1) * hd, k * hd:(k + 1) * hd] = P.T
    C["ropeperm"] = PT.astype(ml_dtypes.bfloat16)
    NGT = cfg.B * cfg.GT
    pm = np.zeros((NGT, cfg.NSLOT), np.float32)
    for rp in range(cfg.NCORES):
        bp, parp = rp // 2, rp % 2
        for sp in range(cfg.NSLOT):
            gtp = bp * cfg.GT + 2 * sp + parp
            for s in range(cfg.NSLOT):
                gt = b * cfg.GT + 2 * s + par
                if gtp < gt:
                    pm[rp * cfg.NSLOT + sp, s] = 1.0
    C["prefmask"] = pm.astype(ml_dtypes.bfloat16)
    C["iota16"] = np.broadcast_to(np.arange(cfg.E, dtype=np.float32),
                                  (128, cfg.E)).copy()
    ev = np.arange(cfg.E)
    C["iota_rkE"] = np.broadcast_to(
        np.tile(ev // cfg.EPC, cfg.NSLOT).astype(np.float32),
        (128, cfg.NSLOT * cfg.E)).copy()
    C["iota_elE"] = np.broadcast_to(
        np.tile(ev % cfg.EPC, cfg.NSLOT).astype(np.float32),
        (128, cfg.NSLOT * cfg.E)).copy()
    C["identf"] = np.eye(128, dtype=np.float32)
    rv = np.zeros((128, cfg.NSLOT), np.float32)
    for s in range(cfg.NSLOT):
        rv[:, s] = core * cfg.NT + s * 128 + np.arange(128) + 1
    C["rowvals"] = rv
    return C


def shard_inputs(cfg, inputs):
    x = np.asarray(inputs["L0_attn_a_x"])
    maps = []
    wnames = {
        "wqa": "L0_attn_a_wq", "wkda": "L0_attn_a_wk_down",
        "wvda": "L0_attn_a_wv_down", "wkua": "L0_attn_a_wk_up",
        "wvua": "L0_attn_a_wv_up", "woa": "L0_attn_a_wo",
        "wqb": "L0_attn_b_wq", "wkb": "L0_attn_b_wk",
        "wvb": "L0_attn_b_wv", "wob": "L0_attn_b_wo",
        "rw": "L0_ffn_router_w",
        "swg": "L0_ffn_sw_gate", "swu": "L0_ffn_sw_up", "swd": "L0_ffn_sw_down",
    }
    rb = np.broadcast_to(np.asarray(inputs["L0_ffn_router_b"]),
                         (128, cfg.E)).copy()
    for core in range(cfg.NCORES):
        b, par = core // 2, core % 2
        m = {}
        xs = np.empty((cfg.NT, cfg.D), np.float32)
        for s in range(cfg.NSLOT):
            g = 2 * s + par
            xs[s * 128:(s + 1) * 128] = x[b, g * 128:(g + 1) * 128]
        m["x"] = xs
        for k, v in wnames.items():
            m[k] = np.ascontiguousarray(np.asarray(inputs[v]))
        m["rb_bc"] = rb
        e0 = core * cfg.EPC
        m["mywg"] = np.ascontiguousarray(np.asarray(inputs["L0_ffn_w_gate"])[e0:e0 + cfg.EPC])
        m["mywu"] = np.ascontiguousarray(np.asarray(inputs["L0_ffn_w_up"])[e0:e0 + cfg.EPC])
        m["mywd"] = np.ascontiguousarray(np.asarray(inputs["L0_ffn_w_down"])[e0:e0 + cfg.EPC])
        m["myblk"] = (core * cfg.MYROWS + 128 * np.arange(cfg.MYROWS // 128, dtype=np.int32))[:, None]
        m.update(make_consts(cfg, core))
        maps.append(m)
    return maps


def assemble_output(cfg, results):
    out = np.empty((cfg.B, cfg.T, cfg.D), np.float32)
    for core in range(cfg.NCORES):
        b, par = core // 2, core % 2
        o = results[core]["out"]
        for s in range(cfg.NSLOT):
            g = 2 * s + par
            out[b, g * 128:(g + 1) * 128] = o[s * 128:(s + 1) * 128]
    return out



CUT = 99
LN_SLOTS = 16
DO_TR = True
LOAD_WOA = True


def build_program(cfg, stages=("attn_a", "ffn1", "attn_b", "ffn2")):
    nc = bass.Bass("TRN2", target_bir_lowering=False)
    bld = Builder(nc, cfg)
    em = bld.em
    NT, D, ND, NSLOT, E = cfg.NT, cfg.D, cfg.ND, cfg.NSLOT, cfg.E

    x_in = bld.dram("x", (NT, D), F32, kind="ExternalInput")
    out_t = bld.dram("out", (NT, D), F32, kind="ExternalOutput")
    W = {}
    W["myblk"] = bld.dram("myblk", (cfg.MYROWS // 128, 1), I32, kind="ExternalInput")
    for name, shape in [
        ("wqa", (D, cfg.HA * cfg.HD)), ("wkda", (D, cfg.KV * cfg.DLAT)),
        ("wvda", (D, cfg.KV * cfg.DLAT)), ("wkua", (cfg.DLAT, cfg.HD)),
        ("wvua", (cfg.DLAT, cfg.HD)), ("woa", (cfg.HA * cfg.HD, D)),
        ("wqb", (D, D)), ("wkb", (D, D)), ("wvb", (D, D)), ("wob", (D, D)),
        ("rw", (D, E)), ("rb_bc", (128, E)),
        ("mywg", (cfg.EPC, D, cfg.F)), ("mywu", (cfg.EPC, D, cfg.F)),
        ("mywd", (cfg.EPC, cfg.F, D)),
        ("swg", (D, cfg.F)), ("swu", (D, cfg.F)), ("swd", (cfg.F, D)),
    ]:
        W[name] = bld.dram(name, shape, F32, kind="ExternalInput")
    CN = {}
    for name, shape, dt in [
        ("ident", (128, 128), BF16), ("negbig_ident", (128, 128), BF16),
        ("tstrict", (128, 128), BF16), ("maskpair", (2, 128, 128), BF16),
        ("maskpair4", (2, 128, 512), BF16), ("tri", (128, 128), BF16),
        ("ropecos", (128, NT), BF16), ("ropesin", (128, NT), BF16),
        ("ropeperm", (128, 128), BF16),
        ("prefmask", (cfg.B * cfg.GT, NSLOT), BF16),
        ("iota16", (128, E), F32), ("rowvals", (128, NSLOT), F32),
        ("identf", (128, 128), F32),
        ("iota_rkE", (128, NSLOT * E), F32), ("iota_elE", (128, NSLOT * E), F32),
    ]:
        CN[name] = bld.dram(name, shape, dt, kind="ExternalInput")

    DR = {}
    DR["spill_x"] = bld.dram("spill_x", (NT, D), F32)
    for n in ("wqb_bf", "wkb_bf", "wvb_bf", "wob_bf"):
        DR[n] = bld.dram(n, (D, D), BF16)
    DR["wg_bf"] = bld.dram("wg_bf", (cfg.EPC, D, cfg.F), BF16)
    DR["wu_bf"] = bld.dram("wu_bf", (cfg.EPC, D, cfg.F), BF16)
    DR["wd_bf"] = bld.dram("wd_bf", (cfg.EPC, cfg.F, D), BF16)
    DR["swg_bf"] = bld.dram("swg_bf", (D, cfg.F), BF16)
    DR["swu_bf"] = bld.dram("swu_bf", (D, cfg.F), BF16)
    DR["swd_bf"] = bld.dram("swd_bf", (cfg.F, D), BF16)
    DR["kta_st"] = bld.dram("kta_st", (cfg.KV * cfg.HD, NT), BF16)
    DR["kta_all"] = bld.dram("kta_all", (2 * cfg.KV * cfg.HD, NT), BF16)
    DR["va_st"] = bld.dram("va_st", (NT, cfg.KV * cfg.HD), BF16)
    DR["va_all"] = bld.dram("va_all", (2 * NT, cfg.KV * cfg.HD), BF16)
    DR["ktb_st"] = bld.dram("ktb_st", (D, NT), BF16)
    DR["ktb_all"] = bld.dram("ktb_all", (2 * D, NT), BF16)
    DR["vb_st"] = bld.dram("vb_st", (NT, D), BF16)
    DR["vb_all"] = bld.dram("vb_all", (2 * NT, D), BF16)
    DR["qtb_spill"] = bld.dram("qtb_spill", (D, NT), BF16)
    DR["xt_spill"] = bld.dram("xt_spill", (D, NT), BF16)
    DR["xn_st"] = bld.dram("xn_st", (NT, D), BF16)
    DR["xn_all"] = bld.dram("xn_all", (cfg.N, D), BF16)
    DR["cnt_st"] = bld.dram("cnt_st", (NSLOT, E), F32)
    DR["cnt_all"] = bld.dram("cnt_all", (cfg.B * cfg.GT, E), F32)
    DR["idxmap"] = bld.dram("idxmap", (E * cfg.C + 128, 1), F32)
    DR["idxmap_red"] = bld.dram("idxmap_red", (E * cfg.C + 128, 1), F32)
    DR["offrow"] = bld.dram("offrow", (1, NSLOT * E), F32)
    DR["myidx"] = bld.dram("myidx", (cfg.MYROWS, 1), F32)
    DR["ei_hbm"] = bld.dram("ei_hbm", (cfg.MYROWS, D), BF16)
    DR["eo_mine"] = bld.dram("eo_mine", (cfg.MYROWS, D), BF16)
    DR["eo_all0"] = bld.dram("eo_all0", (cfg.NCORES * cfg.C, D), BF16)
    DR["eo_all1a"] = bld.dram("eo_all1a", (cfg.NCORES * 1536, D), BF16)
    DR["eo_all1b"] = bld.dram("eo_all1b", (cfg.NCORES * (cfg.C - 1536), D), BF16)

    PB = [bld.psum(f"pb{i}", (128, 512), F32) for i in range(6)]
    PT = bld.psum("pbT", (128, 1024), BF16)
    PT2 = bld.psum("pbT2", (128, 1024), BF16)
    PTB = [PT, PT2]

    sb_ident = bld.sb("sb_ident", (128, 128), BF16)
    sb_nbig = bld.sb("sb_nbig", (128, 128), BF16)
    sb_tstrict = bld.sb("sb_tstrict", (128, 128), BF16)
    sb_maskp = bld.sb("sb_maskp", (128, 2 * 128), BF16)
    sb_maskp4 = bld.sb("sb_maskp4", (128, 2 * 512), BF16)
    sb_tri = bld.sb("sb_tri", (128, 128), BF16)
    sb_iota16 = bld.sb("sb_iota16", (128, E), F32)
    sb_rowvals = bld.sb("sb_rowvals", (128, NSLOT), F32)
    sb_prefm = bld.sb("sb_prefm", (cfg.B * cfg.GT, NSLOT), BF16)
    sb_onesrow = bld.sb("sb_onesrow", (1, 128), F32)
    sb_onescol = bld.sb("sb_onescol", (128, 1), BF16)
    sb_expbias = bld.sb("sb_expbias", (128, 1), F32)
    sb_eps = bld.sb("sb_eps", (128, 1), F32)
    sb_zero1 = bld.sb("sb_zero1", (128, 1), F32)
    scr = bld.sb("scr", (128, 64), F32)
    scr8 = bld.sb("scr8", (128, 8), F32)
    idx8 = bld.sb("idx8", (128, 8), U32)
    if "ffn1" in stages or "ffn2" in stages:
        NGT = cfg.B * cfg.GT
        NE = NSLOT * E
        CT, EPC, Ccap, NF, F = cfg.CT, cfg.EPC, cfg.C, cfg.NF, cfg.F
        RCW = 512 if Ccap % 512 == 0 else 128
        NRC = Ccap // RCW
        TCW = min(512, NT)
        NRW = cfg.MYROWS // 128
        DCW = min(512, D)
        sb_identf = bld.sb("sb_identf", (128, 128), F32)
        rw_sb = bld.sb("rw_sb", (128, ND * E), BF16)
        rb_sb = bld.sb("rb_sb", (128, E), F32)
        myblk_sb = bld.sb("myblk_sb", (NRW, 1), I32)
        lg = bld.sb("lg", (128, NE), F32)
        i01 = bld.sb("i01", (128, 2 * NSLOT), F32)
        rank01 = bld.sb("rank01", (128, 2 * NSLOT), F32)
        el01 = bld.sb("el01", (128, 2 * NSLOT), F32)
        dloc = bld.sb("dloc", (128, 2 * NSLOT), F32)
        dtmp = bld.sb("dtmp", (128, 2 * NSLOT), F32)
        ve0w = bld.sb("ve0w", (128, 2 * NSLOT), F32)
        ve1w = bld.sb("ve1w", (128, 2 * NSLOT), F32)
        ve1aw = bld.sb("ve1aw", (128, 2 * NSLOT), F32)
        ve1bw = bld.sb("ve1bw", (128, 2 * NSLOT), F32)
        dsti1a = bld.sb("dsti1a", (128, 2 * NSLOT), I32)
        dsti1b = bld.sb("dsti1b", (128, 2 * NSLOT), I32)
        hselw = bld.sb("hselw", (128, 2 * NSLOT), F32)
        iota_rk_sb = bld.sb("iota_rk_sb", (128, NSLOT * E), F32)
        iota_el_sb = bld.sb("iota_el_sb", (128, NSLOT * E), F32)
        v01 = bld.sb("v01", (128, 2 * NSLOT), F32)
        ve01 = bld.sb("ve01", (128, 2 * NSLOT), F32)
        pos01 = bld.sb("pos01", (128, 2 * NSLOT), F32)
        dstf = bld.sb("dstf", (128, 2 * NSLOT), F32)
        dsti = bld.sb("dsti", (128, 2 * NSLOT), I32)
        dsti2 = bld.sb("dsti2", (128, 2 * NSLOT), I32)
        gv = bld.sb("gv", (NRW, 128), F32)
        gtmp = bld.sb("gtmp", (NRW, 128), F32)
        gidx = bld.sb("gidx", (128, NRW), I32)
        tot_sb = bld.sb("tot_sb", (1, NE), F32)
        call_sb = bld.sb("call_sb", (NGT, E), F32)
        callb_sb = bld.sb("callb_sb", (NGT, E), BF16)
        offs_sb = bld.sb("offs_sb", (NSLOT, E), F32)
        offr_sb = bld.sb("offr_sb", (1, NE), F32)
        zb = bld.sb("zb", (128, 512), BF16)
        trash_col = bld.sb("trash_col", (128, 1), F32)

    x_res = bld.sb("x_res", (128, NSLOT * D), F32)
    HV = {"x": x_res}

    def alloc_ln_bufs(pfx):
        return (bld.sb(pfx + "xh", (128, NSLOT * D), BF16),
                bld.sb(pfx + "xT", (128, ND * NT), BF16),
                bld.sb(pfx + "xh32", (128, 2 * D), F32),
                bld.sb(pfx + "sq_scr", (128, 2 * D), BF16))

    def R(buf, ps=None, fs=None):
        return buf.r(ps, fs)


    def dma(queue, out_ap, in_ap, reads, writes, dbg="", **kw):
        return em.op(queue, (lambda e, o=out_ap, i=in_ap, kw=kw:
                             e.dma_start(o, i, **kw)),
                     reads=reads, writes=writes, kind="dma", dbg=dbg)

    def mm(out_ap, lhsT_ap, rhs_ap, reads, writes, start, stop, key=None, dbg=""):
        return em.op(PE, (lambda e, o=out_ap, l=lhsT_ap, r=rhs_ap, s=start,
                          p=stop: e.matmul(o, l, r, start=s, stop=p)),
                     reads=reads, writes=writes, accum_key=key, dbg=dbg)

    def transpose(out_ap, in_ap, reads, writes, dbg=""):
        return em.op(PE, (lambda e, o=out_ap, i=in_ap:
                          e.transpose(o, i, sb_ident[:, :])),
                     reads=list(reads) + [sb_ident.r()], writes=writes, dbg=dbg)

    def act(out_ap, in_ap, func, reads, writes, bias=0.0, scale=1.0,
            accum_out=None, dbg=""):
        def f(e, o=out_ap, i=in_ap, fn=func, b=bias, s=scale, a=accum_out):
            kw = {}
            if a is not None:
                kw["accum_out"] = a
            return e.activation(o, i, fn, bias=b, scale=s, **kw)
        return em.op(ACT, f, reads=reads, writes=writes, dbg=dbg)

    def vop(method, *args, reads=(), writes=(), dbg="", eng=DVE, **kw):
        def f(e, m=method, a=args, kw=kw):
            return getattr(e, m)(*a, **kw)
        return em.op(eng, f, reads=reads, writes=writes, dbg=dbg)


    dma(SP, x_res[:, :], x_in.h.rearrange("(s p) d -> p s d", p=128),
        [x_in.r()], [x_res.r()], dbg="load_x")
    for n, cc in [("ident", sb_ident), ("negbig_ident", sb_nbig),
                  ("tstrict", sb_tstrict), ("iota16", sb_iota16),
                  ("rowvals", sb_rowvals), ("prefmask", sb_prefm),
                  ("tri", sb_tri)]:
        dma(SP, cc[:, :], CN[n].h, [CN[n].r()], [cc.r()], dbg=f"load_{n}")
    dma(SP, sb_maskp[:, :], CN["maskpair"].h.rearrange("a p c -> p a c"),
        [CN["maskpair"].r()], [sb_maskp.r()], dbg="load_maskp")
    dma(SP, sb_maskp4[:, :], CN["maskpair4"].h.rearrange("a p c -> p a c"),
        [CN["maskpair4"].r()], [sb_maskp4.r()], dbg="load_maskp4")
    vop("memset", sb_onesrow[:, :], 1.0, writes=[sb_onesrow.r()])
    vop("memset", sb_onescol[:, :], 1.0, writes=[sb_onescol.r()])
    vop("memset", sb_expbias[:, :], -EXP_SHIFT, writes=[sb_expbias.r()])
    vop("memset", sb_eps[:, :], 1e-5, writes=[sb_eps.r()])
    vop("memset", sb_zero1[:, :], 0.0, writes=[sb_zero1.r()])

    def conv(dst, src, n_split=1, dbg=""):
        rows = src.shape[0]
        step = rows // n_split
        for i in range(n_split):
            sl = slice(i * step, (i + 1) * step)
            dma(POOL, dst.h[sl], src.h[sl], [src.r(ps=(i * step, (i + 1) * step))],
                [dst.r(ps=(i * step, (i + 1) * step))], dbg=f"conv_{dbg}_{i}")

    def emit_weight_convs():
        if "attn_b" in stages:
            for s_, d_ in [("wqb", "wqb_bf"), ("wkb", "wkb_bf"),
                           ("wvb", "wvb_bf"), ("wob", "wob_bf")]:
                conv(DR[d_], W[s_], n_split=1, dbg=s_)
        if "ffn1" in stages or "ffn2" in stages:
            em.op(POOL, (lambda e: e.iota(bld.bufs["trash_col"][:, :], [[0, 1]],
                                          base=cfg.E * cfg.C, channel_multiplier=1,
                                          allow_small_or_imprecise_dtypes=True)),
                  writes=[bld.bufs["trash_col"].r()], dbg="trashcol")
            for s_, d_ in [("swg", "swg_bf"), ("swu", "swu_bf"), ("swd", "swd_bf")]:
                conv(DR[d_], W[s_], n_split=1, dbg=s_)
            for s_, d_ in [("mywg", "wg_bf"), ("mywu", "wu_bf"), ("mywd", "wd_bf")]:
                conv(DR[d_], W[s_], n_split=cfg.EPC, dbg=s_)
            zb = bld.bufs["zb"]
            vop("memset", zb[:, :], 0.0, writes=[zb.r()])
            nz = (cfg.MYROWS * D) // (128 * 512)
            for i in range(nz):
                dma(POOL, DR["ei_hbm"].h.rearrange("r d -> (r d)")
                    .rearrange("(n p c) -> n p c", n=nz, p=128)[i],
                    zb[:, :], [zb.r()], [("ei_hbm", (0, cfg.MYROWS, i, i + 1))],
                    dbg=f"zei{i}")
    if "attn_a" not in stages or CUT <= 1:
        emit_weight_convs()


    def layernorm(src_buf, s, dst32_ap, dst32_w, dstbf_ap, dstbf_w, rot, dbg,
                  sq_scr=None):
        xr = [src_buf.r(fs=(s * D, (s + 1) * D))]
        c0 = rot * 8
        sums = scr[:, c0:c0 + 1]
        sumsq = scr[:, c0 + 1:c0 + 2]
        mean = scr[:, c0 + 2:c0 + 3]
        var = scr[:, c0 + 3:c0 + 4]
        rstd = scr[:, c0 + 4:c0 + 5]
        nmr = scr[:, c0 + 5:c0 + 6]
        sc = lambda i: ("scr", (0, 128, c0 + i, c0 + i + 1))
        vop("reduce_sum", sums, src_buf[:, s * D:(s + 1) * D], axis=AX.X,
            reads=xr, writes=[sc(0)], dbg=f"{dbg}_sum")
        act(sq_scr[:, rot * D:(rot + 1) * D], src_buf[:, s * D:(s + 1) * D],
            AF.Square, xr, [sq_scr.r(fs=(rot * D, (rot + 1) * D)), sc(1)],
            accum_out=sumsq, dbg=f"{dbg}_sq")
        del sq_scr
        m2 = scr[:, c0 + 6:c0 + 7]
        vop("tensor_scalar_mul", mean, sums, 1.0 / D,
            reads=[sc(0)], writes=[sc(2)], dbg=f"{dbg}_mean")
        vop("tensor_tensor", m2, mean, mean, ALU.mult,
            reads=[sc(2)], writes=[sc(6)], dbg=f"{dbg}_m2")
        vop("scalar_tensor_tensor", var, sumsq, 1.0 / D, m2, ALU.mult,
            ALU.subtract, reads=[sc(1), sc(6)], writes=[sc(3)], dbg=f"{dbg}_var")
        act(rstd, var, AF.Sqrt, [sc(3), ("sb_eps", (0, 128, 0, 1))], [sc(4)], bias=sb_eps[:, 0:1], scale=1.0,
            dbg=f"{dbg}_sd")
        vop("reciprocal", rstd, rstd, reads=[sc(4)], writes=[sc(4)],
            dbg=f"{dbg}_r")
        vop("scalar_tensor_tensor", nmr, mean, -1.0, rstd, ALU.mult, ALU.mult,
            reads=[sc(2), sc(4)], writes=[sc(5)], dbg=f"{dbg}_nmr")
        if dst32_ap is not None:
            act(dst32_ap, src_buf[:, s * D:(s + 1) * D], AF.Identity,
                xr + [sc(4), sc(5)], dst32_w, bias=nmr, scale=rstd,
                dbg=f"{dbg}_n32")
        act(dstbf_ap, src_buf[:, s * D:(s + 1) * D], AF.Identity,
            xr + [sc(4), sc(5)], dstbf_w, bias=nmr, scale=rstd,
            dbg=f"{dbg}_nbf")

    HA, KV, HD, DLAT = cfg.HA, cfg.KV, cfg.HD, cfg.DLAT
    TCH = min(512, NT)
    NTCH = NT // TCH
    RG_PAIRS = [[2 * i, 2 * i + 1] for i in range(cfg.B)]
    RG_ALL = [list(range(cfg.NCORES))]

    if "attn_a" in stages:
        woa_sb = bld.sb("woa_sb", (128, (HA * HD // 128) * D), BF16)
        qTa = bld.sb("qTa", (128, (HA * HD // 128) * NT), BF16)
        Ya = bld.sb("Ya", (128, 2 * HA * HD), BF16)
        ptb = [bld.sb(f"pta{i}", (128, 512), BF16) for i in range(3)]
        yT = bld.sb("yTa", (128, (HA * HD // 128) * 128), BF16)
        stg = bld.sb("stg_a", (128, 2 * 512), BF16)
        if LOAD_WOA:
            dma(POOL, woa_sb[:, :], W["woa"].h.rearrange("(t p) c -> p t c", p=128),
                [W["woa"].r()], [woa_sb.r()], dbg="ld_woa")
        xT = bld.sb("a_xT", (128, ND * NT), BF16)
        xh = bld.sb("a_xh", (128, 2 * D), BF16)
        xh32 = bld.sb("a_xh32", (128, 2 * D), F32)
        sq_scr = bld.sb("a_sq", (128, 2 * D), BF16)
        for s in range(min(NSLOT, LN_SLOTS)):
            rot = s % 2
            layernorm(x_res, s, None, None, xh[:, rot * D:(rot + 1) * D],
                      [xh.r(fs=(rot * D, (rot + 1) * D))], rot=rot,
                      dbg=f"ln1_{s}", sq_scr=sq_scr)
            for dt in range(ND if DO_TR else 0):
                po = (dt % 8) * 128
                PTX = PTB[dt % 2]
                transpose(PTX[:, po:po + 128],
                          xh[:, rot * D + dt * 128:rot * D + (dt + 1) * 128],
                          [xh.r(fs=(rot * D + dt * 128, rot * D + (dt + 1) * 128))],
                          [PTX.r(fs=(po, po + 128))], dbg=f"xt_{s}_{dt}")
                vop("tensor_copy", xT[:, dt * NT + s * 128:dt * NT + (s + 1) * 128],
                    PTX[:, po:po + 128],
                    reads=[PTX.r(fs=(po, po + 128))],
                    writes=[xT.r(fs=(dt * NT + s * 128, dt * NT + (s + 1) * 128))],
                    dbg=f"xte_{s}_{dt}")
        em.barrier()
        bld.free(sq_scr, xh32, xh)
        if CUT > 1:
            wqa_sb = bld.sb("wqa_sb", (128, 2 * HA * HD), BF16)
            wkda_sb = bld.sb("wkda_sb", (128, ND * KV * DLAT), BF16)
            wvda_sb = bld.sb("wvda_sb", (128, ND * KV * DLAT), BF16)
            wkua_sb = bld.sb("wkua_sb", (KV * DLAT, HD), BF16)
            wvua_sb = bld.sb("wvua_sb", (KV * DLAT, HD), BF16)
            klatT = bld.sb("klatT", (KV * DLAT, NT), BF16)
            vlatT = bld.sb("vlatT", (KV * DLAT, NT), BF16)
            dma(POOL, wkda_sb[:, :], W["wkda"].h.rearrange("(t p) c -> p t c", p=128),
                [W["wkda"].r()], [wkda_sb.r()], dbg="ld_wkda")
            dma(POOL, wvda_sb[:, :], W["wvda"].h.rearrange("(t p) c -> p t c", p=128),
                [W["wvda"].r()], [wvda_sb.r()], dbg="ld_wvda")
            for kv in range(KV):
                dma(POOL, wkua_sb[kv * DLAT:(kv + 1) * DLAT, :], W["wkua"].h,
                    [W["wkua"].r()], [wkua_sb.r(ps=(kv * DLAT, (kv + 1) * DLAT))],
                    dbg=f"ld_wkua{kv}")
                dma(POOL, wvua_sb[kv * DLAT:(kv + 1) * DLAT, :], W["wvua"].h,
                    [W["wvua"].r()], [wvua_sb.r(ps=(kv * DLAT, (kv + 1) * DLAT))],
                    dbg=f"ld_wvua{kv}")
            for nm, wsb, dst in [("kl", wkda_sb, klatT), ("vl", wvda_sb, vlatT)]:
                for tc in range(NTCH):
                    pb = PB[2 + tc % 2]
                    key = (nm, tc)
                    for dt in range(ND):
                        mm(pb[0:KV * DLAT, 0:TCH],
                           wsb[:, dt * KV * DLAT:(dt + 1) * KV * DLAT],
                           xT[:, dt * NT + tc * TCH:dt * NT + (tc + 1) * TCH],
                           [wsb.r(fs=(dt * KV * DLAT, (dt + 1) * KV * DLAT)),
                            xT.r(fs=(dt * NT + tc * TCH, dt * NT + (tc + 1) * TCH))],
                           [pb.r(ps=(0, KV * DLAT), fs=(0, TCH))],
                           start=(dt == 0), stop=(dt == ND - 1), key=key,
                           dbg=f"{nm}_{tc}_{dt}")
                    act(dst[:, tc * TCH:(tc + 1) * TCH], pb[0:KV * DLAT, 0:TCH],
                        AF.Copy, [pb.r(ps=(0, KV * DLAT), fs=(0, TCH))],
                        [dst.r(fs=(tc * TCH, (tc + 1) * TCH))], dbg=f"{nm}e_{tc}")
            for kv in range(KV):
                for tc in range(NTCH):
                    pb = PB[4 + tc % 2]
                    mm(pb[:, 0:TCH], wkua_sb[kv * DLAT:(kv + 1) * DLAT, :],
                       klatT[kv * DLAT:(kv + 1) * DLAT, tc * TCH:(tc + 1) * TCH],
                       [wkua_sb.r(ps=(kv * DLAT, (kv + 1) * DLAT)),
                        klatT.r(ps=(kv * DLAT, (kv + 1) * DLAT),
                                fs=(tc * TCH, (tc + 1) * TCH))],
                       [pb.r(fs=(0, TCH))], start=True, stop=True,
                       dbg=f"kta_{kv}_{tc}")
                    sl = (kv * NTCH + tc) % 2
                    act(stg[:, sl * 512:sl * 512 + TCH], pb[:, 0:TCH], AF.Copy,
                        [pb.r(fs=(0, TCH))], [stg.r(fs=(sl * 512, sl * 512 + TCH))],
                        dbg=f"ktae_{kv}_{tc}")
                    dma(SP, DR["kta_st"].h[kv * HD:(kv + 1) * HD,
                                           tc * TCH:(tc + 1) * TCH],
                        stg[:, sl * 512:sl * 512 + TCH],
                        [stg.r(fs=(sl * 512, sl * 512 + TCH))],
                        [DR["kta_st"].r(ps=(kv * HD, (kv + 1) * HD),
                                        fs=(tc * TCH, (tc + 1) * TCH))],
                        dbg=f"st_kta{kv}_{tc}")
            for s in range(NSLOT):
                for kv in range(KV):
                    pb = PB[0]
                    mm(pb[:, 0:HD],
                       vlatT[kv * DLAT:(kv + 1) * DLAT, s * 128:(s + 1) * 128],
                       wvua_sb[kv * DLAT:(kv + 1) * DLAT, :],
                       [vlatT.r(ps=(kv * DLAT, (kv + 1) * DLAT),
                                fs=(s * 128, (s + 1) * 128)),
                        wvua_sb.r(ps=(kv * DLAT, (kv + 1) * DLAT))],
                       [pb.r(fs=(0, HD))], start=True, stop=True,
                       dbg=f"va_{s}_{kv}")
                    sl = (s * KV + kv) % 2
                    act(stg[:, sl * 512:sl * 512 + HD], pb[:, 0:HD], AF.Copy,
                        [pb.r(fs=(0, HD))], [stg.r(fs=(sl * 512, sl * 512 + HD))],
                        dbg=f"vae_{s}_{kv}")
                    dma(SP, DR["va_st"].h[s * 128:(s + 1) * 128,
                                          kv * HD:(kv + 1) * HD],
                        stg[:, sl * 512:sl * 512 + HD],
                        [stg.r(fs=(sl * 512, sl * 512 + HD))],
                        [DR["va_st"].r(ps=(s * 128, (s + 1) * 128),
                                       fs=(kv * HD, (kv + 1) * HD))],
                        dbg=f"st_va{s}_{kv}")
            for dh_t in range(HA * HD // 128):
                wslot = dh_t % 2
                dma(POOL, wqa_sb[:, wslot * HA * HD:wslot * HA * HD + ND * 128],
                    W["wqa"].h.rearrange("(t p) c -> p t c", p=128)
                    [:, :, dh_t * 128:(dh_t + 1) * 128],
                    [W["wqa"].r(fs=(dh_t * 128, (dh_t + 1) * 128))],
                    [wqa_sb.r(fs=(wslot * HA * HD, wslot * HA * HD + ND * 128))],
                    dbg=f"ld_wqa{dh_t}")
                for tc in range(NTCH):
                    pb = PB[(dh_t * NTCH + tc) % 2]
                    key = ("qa", dh_t, tc)
                    for dt in range(ND):
                        wo_ = wslot * HA * HD + dt * 128
                        mm(pb[:, 0:TCH], wqa_sb[:, wo_:wo_ + 128],
                           xT[:, dt * NT + tc * TCH:dt * NT + (tc + 1) * TCH],
                           [wqa_sb.r(fs=(wo_, wo_ + 128)),
                            xT.r(fs=(dt * NT + tc * TCH, dt * NT + (tc + 1) * TCH))],
                           [pb.r(fs=(0, TCH))], start=(dt == 0), stop=(dt == ND - 1),
                           key=key, dbg=f"qa_{dh_t}_{tc}_{dt}")
                    kv_, r_ = dh_t // 4, dh_t % 4
                    qv = qTa.h[:, :].rearrange("p (kv s r t) -> p kv s r t",
                                               kv=KV, s=NSLOT, r=4)
                    act(qv[:, kv_, 4 * tc:4 * tc + 4, r_, :],
                        pb[:, 0:TCH], AF.Copy, [pb.r(fs=(0, TCH))],
                        [qTa.r(fs=(kv_ * NSLOT * 512 + 4 * tc * 512,
                                   kv_ * NSLOT * 512 + (4 * tc + 4) * 512))],
                        dbg=f"qae_{dh_t}_{tc}")
            em.barrier()
            bld.free(vlatT, klatT, wvua_sb, wkua_sb, wvda_sb, wkda_sb, wqa_sb, xT)
        if CUT > 2:
            em.op(POOL, (lambda e: e.collective_compute(
                "AllGather", ALU.bypass, RG_PAIRS,
                ins=[DR["kta_st"].h[:, :]], outs=[DR["kta_all"].h[:, :]])),
                reads=[DR["kta_st"].r()], writes=[DR["kta_all"].r()], kind="coll",
                dbg="ag_ka")
            em.op(POOL, (lambda e: e.collective_compute(
                "AllGather", ALU.bypass, RG_PAIRS,
                ins=[DR["va_st"].h[:, :]], outs=[DR["va_all"].h[:, :]])),
                reads=[DR["va_st"].r()], writes=[DR["va_all"].r()], kind="coll",
                dbg="ag_va")
            VAW = KV * (HD + 1)
            kta_sb = bld.sb("kta_sb", (128, KV * 2 * NT), BF16)
            va_sb = bld.sb("va_sb", (128, 2 * NSLOT * VAW), BF16)
            for kv in range(KV):
                for rank in range(2):
                    dma(SP, kta_sb[:, kv * 2 * NT + rank * NT:kv * 2 * NT + (rank + 1) * NT],
                        DR["kta_all"].h[rank * KV * HD + kv * HD:rank * KV * HD + (kv + 1) * HD, :],
                        [DR["kta_all"].r(ps=(rank * KV * HD + kv * HD,
                                             rank * KV * HD + (kv + 1) * HD))],
                        [kta_sb.r(fs=(kv * 2 * NT + rank * NT, kv * 2 * NT + (rank + 1) * NT))],
                        dbg=f"ld_kta_{kv}_{rank}")
            for jb in range(2 * NSLOT):
                rank, sl = jb // NSLOT, jb % NSLOT
                for kv in range(KV):
                    dma(SP, va_sb[:, jb * VAW + kv * (HD + 1):jb * VAW + kv * (HD + 1) + HD],
                        DR["va_all"].h[rank * NT + sl * 128:rank * NT + (sl + 1) * 128,
                                       kv * HD:(kv + 1) * HD],
                        [DR["va_all"].r(ps=(rank * NT + sl * 128, rank * NT + (sl + 1) * 128))],
                        [va_sb.r(fs=(jb * VAW + kv * (HD + 1), jb * VAW + kv * (HD + 1) + HD))],
                        dbg=f"ld_va_{jb}_{kv}")
                    vop("memset",
                        va_sb[:, jb * VAW + kv * (HD + 1) + HD:jb * VAW + (kv + 1) * (HD + 1)],
                        1.0,
                        writes=[va_sb.r(fs=(jb * VAW + kv * (HD + 1) + HD,
                                            jb * VAW + (kv + 1) * (HD + 1)))],
                        dbg=f"vone_{jb}_{kv}")
        if CUT > 3:
            scale_a = 1.0 / math.sqrt(HD)
            HW = HD + 1
            for s in range(NSLOT):
                yb = s % 2
                for kv in range(KV):
                    njt = 2 * s + 2
                    qoff = kv * NSLOT * 512 + s * 512
                    for j in range(njt + 1):
                        if j < njt:
                            sc_b = PB[2 + j % 2]
                            kcol = kv * 2 * NT + (j % 2) * NT + (j // 2) * 128
                            mm(sc_b[:, 0:512], kta_sb[:, kcol:kcol + 128],
                               qTa[:, qoff:qoff + 512],
                               [kta_sb.r(fs=(kcol, kcol + 128)),
                                qTa.r(fs=(qoff, qoff + 512))],
                               [sc_b.r(fs=(0, 512))], start=True,
                               stop=(j < 2 * s), dbg=f"sca_{s}_{kv}_{j}")
                            if j >= 2 * s:
                                jrel = j - 2 * s
                                mm(sc_b[:, 0:512], sb_nbig[:, :],
                                   sb_maskp4[:, jrel * 512:(jrel + 1) * 512],
                                   [sb_nbig.r(),
                                    sb_maskp4.r(fs=(jrel * 512, (jrel + 1) * 512))],
                                   [sc_b.r(fs=(0, 512))], start=False, stop=True,
                                   dbg=f"ma_{s}_{kv}_{j}")
                            pt = ptb[j % 3]
                            act(pt[:, :], sc_b[:, 0:512], AF.Exp,
                                [sc_b.r(fs=(0, 512)), sb_expbias.r()], [pt.r()],
                                bias=sb_expbias[:, 0:1], scale=scale_a,
                                dbg=f"expa_{s}_{kv}_{j}")
                        if j >= 1:
                            jj = j - 1
                            jb = (jj % 2) * NSLOT + jj // 2
                            pt = ptb[jj % 3]
                            for r in range(4):
                                av = PB[4 + r // 2]
                                ao = (r % 2) * HW
                                mm(av[:, ao:ao + HW], pt[:, r * 128:(r + 1) * 128],
                                   va_sb[:, jb * VAW + kv * HW:jb * VAW + (kv + 1) * HW],
                                   [pt.r(fs=(r * 128, (r + 1) * 128)),
                                    va_sb.r(fs=(jb * VAW + kv * HW,
                                                jb * VAW + (kv + 1) * HW))],
                                   [av.r(fs=(ao, ao + HW))],
                                   start=(jj == 0 and r % 2 == 0),
                                   stop=(jj == njt - 1 and r % 2 == 1),
                                   key=("ava", s, kv, r // 2),
                                   dbg=f"av_{s}_{kv}_{jj}_{r}")
                    for r in range(4):
                        h = kv * 4 + r
                        av = PB[4 + r // 2]
                        ao = (r % 2) * HW
                        rc = scr8[:, h % 8:h % 8 + 1]
                        rcr = ("scr8", (0, 128, h % 8, h % 8 + 1))
                        vop("reciprocal", rc, av[:, ao + HD:ao + HD + 1],
                            reads=[av.r(fs=(ao + HD, ao + HD + 1))], writes=[rcr],
                            dbg=f"rca_{s}_{h}")
                        act(Ya[:, yb * HA * HD + h * HD:yb * HA * HD + (h + 1) * HD],
                            av[:, ao:ao + HD], AF.Copy,
                            [av.r(fs=(ao, ao + HD)), rcr],
                            [Ya.r(fs=(yb * HA * HD + h * HD, yb * HA * HD + (h + 1) * HD))],
                            scale=rc, dbg=f"ye_{s}_{h}")
                for dh_t in range(HA * HD // 128):
                    po = (dh_t % 8) * 128
                    PTX = PTB[dh_t % 2]
                    transpose(PTX[:, po:po + 128],
                              Ya[:, yb * HA * HD + dh_t * 128:yb * HA * HD + (dh_t + 1) * 128],
                              [Ya.r(fs=(yb * HA * HD + dh_t * 128, yb * HA * HD + (dh_t + 1) * 128))],
                              [PTX.r(fs=(po, po + 128))], dbg=f"ytr_{s}_{dh_t}")
                    vop("tensor_copy", yT[:, dh_t * 128:(dh_t + 1) * 128],
                        PTX[:, po:po + 128], reads=[PTX.r(fs=(po, po + 128))],
                        writes=[yT.r(fs=(dh_t * 128, (dh_t + 1) * 128))],
                        dbg=f"ytre_{s}_{dh_t}")
                DCW = min(512, D)
                for dc in range(D // DCW):
                    pb = PB[dc % 2]
                    key = ("oa", s, dc)
                    for dh_t in range(HA * HD // 128):
                        mm(pb[:, 0:DCW], yT[:, dh_t * 128:(dh_t + 1) * 128],
                           woa_sb[:, dh_t * D + dc * DCW:dh_t * D + (dc + 1) * DCW],
                           [yT.r(fs=(dh_t * 128, (dh_t + 1) * 128)),
                            woa_sb.r(fs=(dh_t * D + dc * DCW, dh_t * D + (dc + 1) * DCW))],
                           [pb.r(fs=(0, DCW))], start=(dh_t == 0),
                           stop=(dh_t == HA * HD // 128 - 1), key=key,
                           dbg=f"oa_{s}_{dc}")
                    vop("tensor_tensor",
                        x_res[:, s * D + dc * DCW:s * D + (dc + 1) * DCW],
                        x_res[:, s * D + dc * DCW:s * D + (dc + 1) * DCW],
                        pb[:, 0:DCW], ALU.add,
                        reads=[x_res.r(fs=(s * D + dc * DCW, s * D + (dc + 1) * DCW)),
                               pb.r(fs=(0, DCW))],
                        writes=[x_res.r(fs=(s * D + dc * DCW, s * D + (dc + 1) * DCW))],
                        dbg=f"oadd_{s}_{dc}")
        if CUT > 1:
            emit_weight_convs()
        em.barrier()
        if CUT > 2:
            bld.free(va_sb, kta_sb)
        if CUT <= 1:
            bld.free(xT)
        bld.free(stg, yT, *reversed(ptb))
        bld.free(Ya, qTa, woa_sb)

    if "ffn1" in stages or "ffn2" in stages:
        NGT = cfg.B * cfg.GT
        NE = NSLOT * E
        CT, EPC, Ccap, NF, F = cfg.CT, cfg.EPC, cfg.C, cfg.NF, cfg.F
        RCW = 512 if Ccap % 512 == 0 else 128
        NRC = Ccap // RCW
        TCW = min(512, NT)
        NRW = cfg.MYROWS // 128
        DCW = min(512, D)
        dma(SP, sb_identf[:, :], CN["identf"].h, [CN["identf"].r()],
            [sb_identf.r()], dbg="ld_identf")
        dma(POOL, rw_sb[:, :], W["rw"].h.rearrange("(t p) c -> p t c", p=128),
            [W["rw"].r()], [rw_sb.r()], dbg="ld_rw")
        dma(SP, rb_sb[:, :], W["rb_bc"].h, [W["rb_bc"].r()], [rb_sb.r()],
            dbg="ld_rb")
        dma(SP, myblk_sb[:, :], W["myblk"].h, [W["myblk"].r()], [myblk_sb.r()],
            dbg="ld_myblk")
        dma(SP, iota_rk_sb[:, :], CN["iota_rkE"].h, [CN["iota_rkE"].r()],
            [iota_rk_sb.r()], dbg="ld_iork")
        dma(SP, iota_el_sb[:, :], CN["iota_elE"].h, [CN["iota_elE"].r()],
            [iota_el_sb.r()], dbg="ld_ioel")

    def emit_ffn(inst):
        t = f"F{inst}_"
        x_res = HV["x"]
        gu_chunk = bld.sb(t + "gu", (128, NF * 512), BF16)
        eo_sb = bld.sb(t + "eo_sb", (128, 2 * D), BF16)
        ret0 = bld.sb(t + "ret0", (128, D), BF16)
        ret1 = bld.sb(t + "ret1", (128, D), BF16)
        ret2 = bld.sb(t + "ret2", (128, D), BF16)
        ret3 = bld.sb(t + "ret3", (128, D), BF16)
        ret4 = bld.sb(t + "ret4", (128, D), BF16)
        ret5 = bld.sb(t + "ret5", (128, D), BF16)
        retf = bld.sb(t + "retf", (128, D), F32)
        zf = bld.sb(t + "zf", (128, 512), F32)
        vop("memset", zf[:, :], 0.0, writes=[zf.r()], dbg=f"{t}zf")
        xT = bld.sb(t + "xT", (128, ND * NT), BF16)
        xh = bld.sb(t + "xh", (128, 2 * D), BF16)
        xh32 = bld.sb(t + "xh32", (128, 2 * D), F32)
        sq_scr = bld.sb(t + "sq", (128, 2 * D), BF16)
        for s in range(NSLOT):
            rot = s % 2
            layernorm(x_res, s, xh32[:, rot * D:(rot + 1) * D],
                      [xh32.r(fs=(rot * D, (rot + 1) * D))],
                      xh[:, rot * D:(rot + 1) * D],
                      [xh.r(fs=(rot * D, (rot + 1) * D))], rot=rot,
                      dbg=f"{t}ln{s}", sq_scr=sq_scr)
            vop("tensor_tensor", x_res[:, s * D:(s + 1) * D],
                x_res[:, s * D:(s + 1) * D], xh32[:, rot * D:(rot + 1) * D],
                ALU.add,
                reads=[x_res.r(fs=(s * D, (s + 1) * D)),
                       xh32.r(fs=(rot * D, (rot + 1) * D))],
                writes=[x_res.r(fs=(s * D, (s + 1) * D))], dbg=f"{t}lnadd{s}")
            dma(SP, DR["xn_st"].h[s * 128:(s + 1) * 128, :],
                xh[:, rot * D:(rot + 1) * D],
                [xh.r(fs=(rot * D, (rot + 1) * D))],
                [DR["xn_st"].r(ps=(s * 128, (s + 1) * 128))], dbg=f"{t}st_xn{s}")
            for dt in range(ND):
                po = (dt % 8) * 128
                PTX = PTB[dt % 2]
                transpose(PTX[:, po:po + 128],
                          xh[:, rot * D + dt * 128:rot * D + (dt + 1) * 128],
                          [xh.r(fs=(rot * D + dt * 128, rot * D + (dt + 1) * 128))],
                          [PTX.r(fs=(po, po + 128))], dbg=f"{t}xt{s}_{dt}")
                vop("tensor_copy",
                    xT[:, dt * NT + s * 128:dt * NT + (s + 1) * 128],
                    PTX[:, po:po + 128], reads=[PTX.r(fs=(po, po + 128))],
                    writes=[xT.r(fs=(dt * NT + s * 128, dt * NT + (s + 1) * 128))],
                    dbg=f"{t}xte{s}_{dt}")
        em.op(POOL, (lambda e: e.collective_compute(
            "AllGather", ALU.bypass, RG_ALL,
            ins=[DR["xn_st"].h[:, :]], outs=[DR["xn_all"].h[:, :]])),
            reads=[DR["xn_st"].r()], writes=[DR["xn_all"].r()], kind="coll",
            dbg=f"{t}ag_xn")
        em.barrier()
        bld.free(sq_scr, xh32, xh)
        for s in range(NSLOT):
            pb = PB[s % 2]
            key = (t + "rt", s)
            for dt in range(ND):
                mm(pb[:, 0:E], xT[:, dt * NT + s * 128:dt * NT + (s + 1) * 128],
                   rw_sb[:, dt * E:(dt + 1) * E],
                   [xT.r(fs=(dt * NT + s * 128, dt * NT + (s + 1) * 128)),
                    rw_sb.r(fs=(dt * E, (dt + 1) * E))],
                   [pb.r(fs=(0, E))], start=(dt == 0), stop=(dt == ND - 1),
                   key=key, dbg=f"{t}rt{s}_{dt}")
            vop("tensor_tensor", lg[:, s * E:(s + 1) * E], pb[:, 0:E],
                rb_sb[:, :], ALU.add,
                reads=[pb.r(fs=(0, E)), rb_sb.r()],
                writes=[lg.r(fs=(s * E, (s + 1) * E))], dbg=f"{t}rte{s}")
        pr = bld.sb(t + "pr", (128, NE), F32)
        pr2 = bld.sb(t + "pr2", (128, NE), F32)
        O0 = bld.sb(t + "O0", (128, NE), F32)
        O1 = bld.sb(t + "O1", (128, NE), F32)
        s01b = bld.sb(t + "s01b", (128, NE), BF16)
        carrys = bld.sb(t + "carrys", (128, NE), F32)
        sumall = bld.sb(t + "sumall", (128, NE), F32)
        sumall1 = bld.sb(t + "sumall1", (128, NE), F32)
        tmpsel = bld.sb(t + "tmpsel", (128, NE), F32)
        for s in range(NSLOT):
            lgs = lg[:, s * E:(s + 1) * E]
            lgr = lg.r(fs=(s * E, (s + 1) * E))
            prs = pr[:, s * E:(s + 1) * E]
            prr = pr.r(fs=(s * E, (s + 1) * E))
            pr2s = pr2[:, s * E:(s + 1) * E]
            pr2r = pr2.r(fs=(s * E, (s + 1) * E))
            cN = lambda i: ("scr", (0, 128, 16 + i, 16 + i + 1))
            cA = lambda i: scr[:, 16 + i:16 + i + 1]
            vop("reduce_max", cA(0), lgs, axis=AX.X, negate=True,
                reads=[lgr], writes=[cN(0)], dbg=f"{t}sm{s}_m")
            act(prs, lgs, AF.Exp, [lgr, cN(0)], [prr, cN(1)],
                bias=cA(0), scale=1.0, accum_out=cA(1), dbg=f"{t}sm{s}_e")
            vop("reciprocal", cA(2), cA(1), reads=[cN(1)], writes=[cN(2)],
                dbg=f"{t}sm{s}_rz")
            vop("tensor_scalar_mul", prs, prs, cA(2),
                reads=[prr, cN(2)], writes=[prr], dbg=f"{t}sm{s}_p")
            vop("reduce_max", cA(3), prs, axis=AX.X,
                reads=[prr], writes=[cN(3)], dbg=f"{t}sm{s}_p1")
            vop("memset", scr8[:, 0:8], -1.0, writes=[scr8.r(fs=(0, 8))],
                dbg=f"{t}sm{s}_ms")
            vop("tensor_copy", scr8[:, 0:1], cA(3),
                reads=[cN(3)], writes=[scr8.r(fs=(0, 1))], dbg=f"{t}sm{s}_c1")
            vop("max_index", idx8[:, :], scr8[:, 0:8], prs,
                reads=[scr8.r(fs=(0, 8)), prr], writes=[idx8.r()],
                dbg=f"{t}sm{s}_i1")
            vop("tensor_copy", i01[:, s:s + 1], idx8[:, 0:1],
                reads=[idx8.r(fs=(0, 1))], writes=[i01.r(fs=(s, s + 1))],
                dbg=f"{t}sm{s}_i1c")
            vop("match_replace", pr2s, scr8[:, 0:8], prs, -1.0,
                reads=[scr8.r(fs=(0, 8)), prr], writes=[pr2r],
                dbg=f"{t}sm{s}_mr")
            vop("reduce_max", cA(4), pr2s, axis=AX.X,
                reads=[pr2r], writes=[cN(4)], dbg=f"{t}sm{s}_p2")
            vop("memset", scr8[:, 0:8], -1.0, writes=[scr8.r(fs=(0, 8))],
                dbg=f"{t}sm{s}_ms2")
            vop("tensor_copy", scr8[:, 0:1], cA(4),
                reads=[cN(4)], writes=[scr8.r(fs=(0, 1))], dbg=f"{t}sm{s}_c2")
            vop("max_index", idx8[:, :], scr8[:, 0:8], pr2s,
                reads=[scr8.r(fs=(0, 8)), pr2r], writes=[idx8.r()],
                dbg=f"{t}sm{s}_i2")
            vop("tensor_copy", i01[:, NSLOT + s:NSLOT + s + 1], idx8[:, 0:1],
                reads=[idx8.r(fs=(0, 1))],
                writes=[i01.r(fs=(NSLOT + s, NSLOT + s + 1))],
                dbg=f"{t}sm{s}_i2c")
            vop("tensor_tensor", cA(5), cA(3), cA(4), ALU.add,
                reads=[cN(3), cN(4)], writes=[cN(5)], dbg=f"{t}sm{s}_s")
            vop("tensor_scalar_add", cA(5), cA(5), 1e-9,
                reads=[cN(5)], writes=[cN(5)], dbg=f"{t}sm{s}_se")
            vop("reciprocal", cA(6), cA(5), reads=[cN(5)], writes=[cN(6)],
                dbg=f"{t}sm{s}_sr")
            vop("tensor_tensor", v01[:, s:s + 1], cA(3), cA(6), ALU.mult,
                reads=[cN(3), cN(6)], writes=[v01.r(fs=(s, s + 1))],
                dbg=f"{t}sm{s}_v1")
            vop("tensor_tensor", v01[:, NSLOT + s:NSLOT + s + 1], cA(4), cA(6),
                ALU.mult, reads=[cN(4), cN(6)],
                writes=[v01.r(fs=(NSLOT + s, NSLOT + s + 1))],
                dbg=f"{t}sm{s}_v2")
        for s in range(NSLOT):
            for k, Ok in ((0, O0), (1, O1)):
                vop("tensor_scalar", Ok[:, s * E:(s + 1) * E], sb_iota16[:, :],
                    i01[:, k * NSLOT + s:k * NSLOT + s + 1], None, ALU.is_equal,
                    reads=[sb_iota16.r(),
                           i01.r(fs=(k * NSLOT + s, k * NSLOT + s + 1))],
                    writes=[Ok.r(fs=(s * E, (s + 1) * E))], dbg=f"{t}oh{s}_{k}")
        vop("tensor_tensor", tmpsel[:, :], O0[:, :], O1[:, :], ALU.add,
            reads=[O0.r(), O1.r()], writes=[tmpsel.r()], dbg=f"{t}s01")
        vop("tensor_copy", s01b[:, :], tmpsel[:, :], reads=[tmpsel.r()],
            writes=[s01b.r()], dbg=f"{t}s01b")
        def routing_a():
            mm(PB[2][:, 0:NE], sb_tstrict[:, :], s01b[:, :],
               [sb_tstrict.r(), s01b.r()], [PB[2].r(fs=(0, NE))],
               start=True, stop=True, dbg=f"{t}carry")
            act(carrys[:, :], PB[2][:, 0:NE], AF.Copy, [PB[2].r(fs=(0, NE))],
                [carrys.r()], dbg=f"{t}carrye")
            mm(PB[3][0:1, 0:NE], sb_onescol[:, :], s01b[:, :],
               [sb_onescol.r(), s01b.r()], [PB[3].r(ps=(0, 1), fs=(0, NE))],
               start=True, stop=True, dbg=f"{t}tot")
            act(tot_sb[0:1, :], PB[3][0:1, 0:NE], AF.Copy,
                [PB[3].r(ps=(0, 1), fs=(0, NE))], [tot_sb.r()], dbg=f"{t}tote")
            dma(SP, DR["cnt_st"].h.rearrange("s e -> (s e)"), tot_sb[0:1, :],
                [tot_sb.r()], [DR["cnt_st"].r()], dbg=f"{t}st_cnt")
        def routing_b():
            em.op(POOL, (lambda e: e.collective_compute(
                "AllGather", ALU.bypass, RG_ALL,
                ins=[DR["cnt_st"].h[:, :]], outs=[DR["cnt_all"].h[:, :]])),
                reads=[DR["cnt_st"].r()], writes=[DR["cnt_all"].r()], kind="coll",
                dbg=f"{t}ag_cnt")
            dma(SP, call_sb[:, :], DR["cnt_all"].h, [DR["cnt_all"].r()],
                [call_sb.r()], dbg=f"{t}ld_cnt")
            vop("tensor_copy", callb_sb[:, :], call_sb[:, :], reads=[call_sb.r()],
                writes=[callb_sb.r()], dbg=f"{t}cntb")
            mm(PB[4][0:NSLOT, 0:E], sb_prefm[:, :], callb_sb[:, :],
               [sb_prefm.r(), callb_sb.r()], [PB[4].r(ps=(0, NSLOT), fs=(0, E))],
               start=True, stop=True, dbg=f"{t}offs")
            act(offs_sb[:, :], PB[4][0:NSLOT, 0:E], AF.Copy,
                [PB[4].r(ps=(0, NSLOT), fs=(0, E))], [offs_sb.r()], dbg=f"{t}offse")
            dma(SP, DR["offrow"].h.rearrange("o (s e) -> o s e", s=NSLOT),
                offs_sb[:, :], [offs_sb.r()], [DR["offrow"].r()], dbg=f"{t}st_off")
        def routing_c():
            dma(SP, offr_sb[:, :], DR["offrow"].h, [DR["offrow"].r()],
                [offr_sb.r()], dbg=f"{t}ld_off")
            mm(PB[5][:, 0:NE], sb_onesrow[:, :], offr_sb[0:1, :],
               [sb_onesrow.r(), offr_sb.r()], [PB[5].r(fs=(0, NE))],
               start=True, stop=True, dbg=f"{t}offb")
            vop("tensor_tensor", sumall[:, :], carrys[:, :], PB[5][:, 0:NE],
                ALU.add, reads=[carrys.r(), PB[5].r(fs=(0, NE))],
                writes=[sumall.r()], dbg=f"{t}sumall")
            vop("tensor_tensor", sumall1[:, :], sumall[:, :], O0[:, :], ALU.add,
                reads=[sumall.r(), O0.r()], writes=[sumall1.r()], dbg=f"{t}sumall1")
            for k, Ok, SS in ((0, O0, sumall), (1, O1, sumall1)):
                vop("tensor_tensor", tmpsel[:, :], SS[:, :], Ok[:, :], ALU.mult,
                    reads=[SS.r(), Ok.r()], writes=[tmpsel.r()], dbg=f"{t}sel{k}")
                vop("tensor_reduce",
                    pos01[:, k * NSLOT:(k + 1) * NSLOT],
                    tmpsel.h[:, :].rearrange("p (s e) -> p s e", e=E),
                    axis=AX.X, op=ALU.add,
                    reads=[tmpsel.r()],
                    writes=[pos01.r(fs=(k * NSLOT, (k + 1) * NSLOT))],
                    dbg=f"{t}pos{k}")
            vop("tensor_scalar", tmpsel[:, 0:2 * NSLOT], pos01[:, :], float(Ccap),
                None, ALU.is_lt, reads=[pos01.r()],
                writes=[tmpsel.r(fs=(0, 2 * NSLOT))], dbg=f"{t}win")
            vop("tensor_tensor", ve01[:, :], tmpsel[:, 0:2 * NSLOT], v01[:, :],
                ALU.mult, reads=[tmpsel.r(fs=(0, 2 * NSLOT)), v01.r()],
                writes=[ve01.r()], dbg=f"{t}ve")
            for k, Ok in ((0, O0), (1, O1)):
                for srcb, dstb, nm in ((iota_rk_sb, rank01, "rk"),
                                       (iota_el_sb, el01, "el")):
                    vop("tensor_tensor", pr2[:, :], srcb[:, :], Ok[:, :], ALU.mult,
                        reads=[srcb.r(), Ok.r()], writes=[pr2.r()],
                        dbg=f"{t}{nm}m{k}")
                    vop("tensor_reduce", dstb[:, k * NSLOT:(k + 1) * NSLOT],
                        pr2.h[:, :].rearrange("p (s e) -> p s e", e=E),
                        axis=AX.X, op=ALU.add, reads=[pr2.r()],
                        writes=[dstb.r(fs=(k * NSLOT, (k + 1) * NSLOT))],
                        dbg=f"{t}{nm}r{k}")
            vop("scalar_tensor_tensor", dstf[:, :], i01[:, :], float(Ccap),
                pos01[:, :], ALU.mult, ALU.add,
                reads=[i01.r(), pos01.r()], writes=[dstf.r()], dbg=f"{t}dst0")
            vop("tensor_tensor", dstf[:, :], dstf[:, :], tmpsel[:, 0:2 * NSLOT],
                ALU.mult, reads=[dstf.r(), tmpsel.r(fs=(0, 2 * NSLOT))],
                writes=[dstf.r()], dbg=f"{t}dst1")
            vop("scalar_tensor_tensor", dloc[:, :], rank01[:, :], float(Ccap),
                pos01[:, :], ALU.mult, ALU.add,
                reads=[rank01.r(), pos01.r()], writes=[dloc.r()], dbg=f"{t}dl0")
            vop("tensor_tensor", dloc[:, :], dloc[:, :], tmpsel[:, 0:2 * NSLOT],
                ALU.mult, reads=[dloc.r(), tmpsel.r(fs=(0, 2 * NSLOT))],
                writes=[dloc.r()], dbg=f"{t}dl1")
            vop("tensor_copy", dsti[:, :], dloc[:, :], reads=[dloc.r()],
                writes=[dsti.r()], dbg=f"{t}dsti")
            vop("tensor_scalar", hselw[:, :], pos01[:, :], 1536.0, None,
                ALU.is_lt, reads=[pos01.r()], writes=[hselw.r()], dbg=f"{t}hsel")
            vop("scalar_tensor_tensor", dtmp[:, :], rank01[:, :], 1536.0,
                pos01[:, :], ALU.mult, ALU.add,
                reads=[rank01.r(), pos01.r()], writes=[dtmp.r()], dbg=f"{t}d1a0")
            vop("tensor_tensor", dtmp[:, :], dtmp[:, :], tmpsel[:, 0:2 * NSLOT],
                ALU.mult, reads=[dtmp.r(), tmpsel.r(fs=(0, 2 * NSLOT))],
                writes=[dtmp.r()], dbg=f"{t}d1a1")
            vop("tensor_tensor", dtmp[:, :], dtmp[:, :], hselw[:, :],
                ALU.mult, reads=[dtmp.r(), hselw.r()],
                writes=[dtmp.r()], dbg=f"{t}d1a2")
            vop("tensor_copy", dsti1a[:, :], dtmp[:, :], reads=[dtmp.r()],
                writes=[dsti1a.r()], dbg=f"{t}d1ai")
            vop("scalar_tensor_tensor", dtmp[:, :], rank01[:, :], 1024.0,
                pos01[:, :], ALU.mult, ALU.add,
                reads=[rank01.r(), pos01.r()], writes=[dtmp.r()], dbg=f"{t}d1b0")
            vop("tensor_scalar", dtmp[:, :], dtmp[:, :], -1536.0, 0.0,
                ALU.add, ALU.max, reads=[dtmp.r()], writes=[dtmp.r()],
                dbg=f"{t}d1b1")
            vop("tensor_tensor", dtmp[:, :], dtmp[:, :], tmpsel[:, 0:2 * NSLOT],
                ALU.mult, reads=[dtmp.r(), tmpsel.r(fs=(0, 2 * NSLOT))],
                writes=[dtmp.r()], dbg=f"{t}d1b2")
            vop("tensor_copy", dsti1b[:, :], dtmp[:, :], reads=[dtmp.r()],
                writes=[dsti1b.r()], dbg=f"{t}d1bi")
            vop("tensor_scalar", dtmp[:, :], el01[:, :], -1.0, 1.0,
                ALU.mult, ALU.add, reads=[el01.r()], writes=[dtmp.r()],
                dbg=f"{t}el0s")
            vop("tensor_tensor", ve0w[:, :], ve01[:, :], dtmp[:, :], ALU.mult,
                reads=[ve01.r(), dtmp.r()], writes=[ve0w.r()], dbg=f"{t}ve0w")
            vop("tensor_tensor", ve1w[:, :], ve01[:, :], el01[:, :], ALU.mult,
                reads=[ve01.r(), el01.r()], writes=[ve1w.r()], dbg=f"{t}ve1w")
            vop("tensor_tensor", ve1aw[:, :], ve1w[:, :], hselw[:, :], ALU.mult,
                reads=[ve1w.r(), hselw.r()], writes=[ve1aw.r()], dbg=f"{t}ve1aw")
            vop("tensor_tensor", ve1bw[:, :], ve1w[:, :], ve1aw[:, :],
                ALU.subtract, reads=[ve1w.r(), ve1aw.r()], writes=[ve1bw.r()],
                dbg=f"{t}ve1bw")
            vop("tensor_scalar", tmpsel[:, 0:2 * NSLOT], tmpsel[:, 0:2 * NSLOT],
                -1.0, 1.0, ALU.mult, ALU.add,
                reads=[tmpsel.r(fs=(0, 2 * NSLOT))],
                writes=[tmpsel.r(fs=(0, 2 * NSLOT))], dbg=f"{t}wneg")
            vop("tensor_scalar", tmpsel[:, 0:2 * NSLOT], tmpsel[:, 0:2 * NSLOT],
                bld.bufs["trash_col"][:, 0:1], None, ALU.mult,
                reads=[tmpsel.r(fs=(0, 2 * NSLOT)), bld.bufs["trash_col"].r()],
                writes=[tmpsel.r(fs=(0, 2 * NSLOT))], dbg=f"{t}wtr")
            vop("tensor_tensor", tmpsel[:, 0:2 * NSLOT], tmpsel[:, 0:2 * NSLOT],
                dstf[:, :], ALU.add,
                reads=[tmpsel.r(fs=(0, 2 * NSLOT)), dstf.r()],
                writes=[tmpsel.r(fs=(0, 2 * NSLOT))], dbg=f"{t}dsc")
            vop("tensor_copy", dsti2[:, :], tmpsel[:, 0:2 * NSLOT],
                reads=[tmpsel.r(fs=(0, 2 * NSLOT))], writes=[dsti2.r()],
                dbg=f"{t}dsti2")
        def routing_d():
            nzc = (E * Ccap + 128) // 128
            dma(POOL, DR["idxmap"].h.rearrange("(p c) o -> p c o", p=128),
                zf[:, 0:nzc], [zf.r(fs=(0, nzc))], [DR["idxmap"].r()],
                dbg=f"{t}zidx")
            for k in range(2):
                for s in range(NSLOT):
                    col = k * NSLOT + s
                    em.op(POOL, (lambda e, col=col, s=s: e.indirect_dma_start(
                        DR["idxmap"].h[:, :],
                        IndirectOffsetOnAxis(dsti2[:, col:col + 1], 0),
                        sb_rowvals[:, s:s + 1], None)),
                        reads=[dsti2.r(fs=(col, col + 1)),
                               sb_rowvals.r(fs=(s, s + 1))],
                        writes=[DR["idxmap"].r()], kind="dma", dbg=f"{t}sct{k}_{s}")
            em.op(POOL, (lambda e: e.collective_compute(
                "AllReduce", ALU.add, RG_ALL,
                ins=[DR["idxmap"].h[:, :]], outs=[DR["idxmap_red"].h[:, :]])),
                reads=[DR["idxmap"].r()], writes=[DR["idxmap_red"].r()],
                kind="coll", dbg=f"{t}ar_idx")
        def mlp_chunk(tag, rc, rcw_, lhs_gu, rhs_act, wd_get, out_add):
            ntt_ = rcw_ // 128
            for ft in range(NF):
                pg = PB[ft % 2]
                pu = PB[2 + ft % 2]
                kg = (tag, "g", rc, ft)
                ku = (tag, "u", rc, ft)
                for dt in range(ND):
                    la, lr = lhs_gu(0, dt, ft)
                    ra, rr = rhs_act(dt, rc)
                    mm(pg[:, 0:rcw_], la, ra, [lr, rr], [pg.r(fs=(0, rcw_))],
                       start=(dt == 0), stop=(dt == ND - 1), key=kg,
                       dbg=f"{tag}g{rc}_{ft}_{dt}")
                    la2, lr2 = lhs_gu(1, dt, ft)
                    mm(pu[:, 0:rcw_], la2, ra, [lr2, rr], [pu.r(fs=(0, rcw_))],
                       start=(dt == 0), stop=(dt == ND - 1), key=ku,
                       dbg=f"{tag}u{rc}_{ft}_{dt}")
                gur = gu_chunk.r(fs=(ft * 512, ft * 512 + rcw_))
                act(gu_chunk[:, ft * 512:ft * 512 + rcw_], pg[:, 0:rcw_],
                    AF.Silu, [pg.r(fs=(0, rcw_))], [gur], dbg=f"{tag}sil{rc}_{ft}")
                vop("tensor_tensor", gu_chunk[:, ft * 512:ft * 512 + rcw_],
                    gu_chunk[:, ft * 512:ft * 512 + rcw_], pu[:, 0:rcw_],
                    ALU.mult, reads=[gur, pu.r(fs=(0, rcw_))], writes=[gur],
                    dbg=f"{tag}gu{rc}_{ft}")
            for tt in range(ntt_):
                for dc in range(D // DCW):
                    pd = PB[4 + (tt * (D // DCW) + dc) % 2]
                    kd = (tag, "d", rc, tt, dc)
                    for ft in range(NF):
                        wda, wdr = wd_get(ft, dc)
                        mm(pd[:, 0:DCW],
                           gu_chunk[:, ft * 512 + tt * 128:ft * 512 + (tt + 1) * 128],
                           wda,
                           [gu_chunk.r(fs=(ft * 512 + tt * 128,
                                           ft * 512 + (tt + 1) * 128)), wdr],
                           [pd.r(fs=(0, DCW))], start=(ft == 0),
                           stop=(ft == NF - 1), key=kd,
                           dbg=f"{tag}d{rc}_{tt}_{dc}_{ft}")
                    out_add(rc, tt, dc, pd[:, 0:DCW], pd.r(fs=(0, DCW)))

        swgu_sb = bld.sb(t + "swgu", (128, 4 * ND * 128), BF16)
        swd_sb = bld.sb(t + "swd", (128, NF * D), BF16)
        dma(POOL, swd_sb[:, :], DR["swd_bf"].h.rearrange("(t p) c -> p t c", p=128),
            [DR["swd_bf"].r()], [swd_sb.r()], dbg=f"{t}ld_swd")
        sw_loaded = {}

        def sh_lhs(gi, dt, ft):
            sl = (2 * (ft % 2) + gi) * ND * 128
            if (gi, ft) not in sw_loaded:
                srcb = DR["swg_bf"] if gi == 0 else DR["swu_bf"]
                dma(POOL, swgu_sb[:, sl:sl + ND * 128],
                    srcb.h.rearrange("(t p) c -> p t c", p=128)
                    [:, :, ft * 128:(ft + 1) * 128],
                    [srcb.r(fs=(ft * 128, (ft + 1) * 128))],
                    [swgu_sb.r(fs=(sl, sl + ND * 128))], dbg=f"{t}ldsw{gi}_{ft}")
                sw_loaded[(gi, ft)] = True
            off = sl + dt * 128
            return (swgu_sb[:, off:off + 128], swgu_sb.r(fs=(off, off + 128)))

        def sh_rhs(dt, rc):
            return (xT[:, dt * NT + rc * TCW:dt * NT + (rc + 1) * TCW],
                    xT.r(fs=(dt * NT + rc * TCW, dt * NT + (rc + 1) * TCW)))

        def sh_wd(ft, dc):
            return (swd_sb[:, ft * D + dc * DCW:ft * D + (dc + 1) * DCW],
                    swd_sb.r(fs=(ft * D + dc * DCW, ft * D + (dc + 1) * DCW)))

        def sh_out(rc, tt, dc, pap, prd):
            s = (rc * TCW) // 128 + tt
            fs = (s * D + dc * DCW, s * D + (dc + 1) * DCW)
            vop("tensor_tensor", x_res[:, fs[0]:fs[1]], x_res[:, fs[0]:fs[1]],
                pap, ALU.add, reads=[x_res.r(fs=fs), prd],
                writes=[x_res.r(fs=fs)], dbg=f"{t}shadd{rc}_{tt}_{dc}")

        def routing_bc():
            routing_b()
            routing_c()
        routing_cb = {0: routing_a, 1: routing_bc, 2: routing_d}
        for rc in range(NT // TCW):
            sw_loaded.clear()
            mlp_chunk(t + "sh", rc, TCW, sh_lhs, sh_rhs, sh_wd, sh_out)
            if rc in routing_cb:
                routing_cb[rc]()
        em.op(POOL, (lambda e: e.indirect_dma_start(
            gv[:, :], None, DR["idxmap_red"].h[:, :],
            IndirectOffsetOnAxis(myblk_sb[:, :], 0))),
            reads=[DR["idxmap_red"].r(), myblk_sb.r()], writes=[gv.r()],
            kind="dma", dbg=f"{t}myblk")
        vop("tensor_scalar", gv[:, :], gv[:, :], -1.0, 0.0, ALU.add, ALU.max,
            reads=[gv.r()], writes=[gv.r()], dbg=f"{t}gm1")
        em.op(PE, (lambda e: e.transpose(PB[5][:, 0:NRW], gv[:, :],
                                         sb_identf[0:NRW, 0:NRW])),
              reads=[gv.r(), sb_identf.r()],
              writes=[PB[5].r(fs=(0, NRW))], dbg=f"{t}gtr")
        vop("tensor_copy", gidx[:, :], PB[5][:, 0:NRW],
            reads=[PB[5].r(fs=(0, NRW))], writes=[gidx.r()], dbg=f"{t}gidx")
        eib = bld.sb(t + "eib", (128, 2 * D), BF16)
        for i in range(NRW):
            sl = (i % 2) * D
            em.op(POOL, (lambda e, i=i, sl=sl: e.indirect_dma_start(
                eib[:, sl:sl + D], None,
                DR["xn_all"].h[:, :],
                IndirectOffsetOnAxis(gidx[:, i:i + 1], 0))),
                reads=[DR["xn_all"].r(), gidx.r(fs=(i, i + 1))],
                writes=[eib.r(fs=(sl, sl + D))],
                kind="dma", dbg=f"{t}gei{i}")
            dma(SP, DR["ei_hbm"].h[i * 128:(i + 1) * 128, :], eib[:, sl:sl + D],
                [eib.r(fs=(sl, sl + D))],
                [DR["ei_hbm"].r(ps=(i * 128, (i + 1) * 128))],
                dbg=f"{t}sei{i}")

        em.barrier()
        bld.free(eib, swd_sb, swgu_sb)
        bld.free(tmpsel, sumall1, sumall, carrys, s01b, O1, O0, pr2, pr)
        bld.free(xT)
        wgwu_sb = bld.sb(t + "wgwu", (128, 2 * ND * F), BF16)
        wdx_sb = bld.sb(t + "wdx", (128, NF * D), BF16)
        eiT = bld.sb(t + "eiT", (128, 2 * ND * RCW), BF16)

        def emit_eo_ag(el):
            em.op(POOL, (lambda e, el=el: e.collective_compute(
                "AllGather", ALU.bypass, RG_ALL,
                ins=[DR["eo_mine"].h[el * Ccap:(el + 1) * Ccap, :]],
                outs=[DR[f"eo_all{el}"].h[:, :]])),
                reads=[DR["eo_mine"].r(ps=(el * Ccap, (el + 1) * Ccap))],
                writes=[DR[f"eo_all{el}"].r()], kind="coll",
                dbg=f"{t}ag_eo{el}")
        for el in range(EPC):
            dma(POOL, wgwu_sb[:, 0:ND * F],
                DR["wg_bf"].h[el].rearrange("(t p) c -> p t c", p=128),
                [DR["wg_bf"].r(ps=(el, el + 1))], [wgwu_sb.r(fs=(0, ND * F))],
                dbg=f"{t}ldwg{el}")
            dma(POOL, wgwu_sb[:, ND * F:2 * ND * F],
                DR["wu_bf"].h[el].rearrange("(t p) c -> p t c", p=128),
                [DR["wu_bf"].r(ps=(el, el + 1))],
                [wgwu_sb.r(fs=(ND * F, 2 * ND * F))], dbg=f"{t}ldwu{el}")
            dma(POOL, wdx_sb[:, :],
                DR["wd_bf"].h[el].rearrange("(t p) c -> p t c", p=128),
                [DR["wd_bf"].r(ps=(el, el + 1))], [wdx_sb.r()],
                dbg=f"{t}ldwd{el}")
            if el > 0:
                emit_eo_ag(el - 1)
            def ex_lhs(gi, dt, ft, el=el):
                off = gi * ND * F + dt * F + ft * 128
                return (wgwu_sb[:, off:off + 128],
                        wgwu_sb.r(fs=(off, off + 128)))

            def ex_rhs(dt, rc, el=el):
                sl = (rc % 2) * ND * RCW + dt * RCW
                return (eiT[:, sl:sl + RCW], eiT.r(fs=(sl, sl + RCW)))

            def ex_wd(ft, dc, el=el):
                return (wdx_sb[:, ft * D + dc * DCW:ft * D + (dc + 1) * DCW],
                        wdx_sb.r(fs=(ft * D + dc * DCW, ft * D + (dc + 1) * DCW)))

            def ex_out(rc, tt, dc, pap, prd, el=el):
                r0 = rc * RCW + tt * 128
                act(eo_sb[:, dc * DCW:(dc + 1) * DCW], pap, AF.Copy, [prd],
                    [eo_sb.r(fs=(dc * DCW, (dc + 1) * DCW))],
                    dbg=f"{t}eoe{el}_{rc}_{tt}_{dc}")
                if dc == D // DCW - 1:
                    dma(SP, DR["eo_mine"].h[el * Ccap + r0:el * Ccap + r0 + 128, :],
                        eo_sb[:, 0:D], [eo_sb.r(fs=(0, D))],
                        [DR["eo_mine"].r(ps=(el * Ccap + r0, el * Ccap + r0 + 128))],
                        dbg=f"{t}eost{el}_{rc}_{tt}")

            for rc in range(NRC):
                sl = (rc % 2) * ND * RCW
                for dt in range(ND):
                    dma(SP, eiT[:, sl + dt * RCW:sl + (dt + 1) * RCW],
                        DR["ei_hbm"].h[el * Ccap + rc * RCW:el * Ccap + (rc + 1) * RCW,
                                       dt * 128:(dt + 1) * 128],
                        [DR["ei_hbm"].r(ps=(el * Ccap + rc * RCW,
                                            el * Ccap + (rc + 1) * RCW))],
                        [eiT.r(fs=(sl + dt * RCW, sl + (dt + 1) * RCW))],
                        dbg=f"{t}eit{el}_{rc}_{dt}", transpose=True)
                mlp_chunk(t + f"ex{el}", rc, RCW, ex_lhs, ex_rhs, ex_wd, ex_out)
                if el == EPC - 1 and rc == 2:
                    em.op(POOL, (lambda e: e.collective_compute(
                        "AllGather", ALU.bypass, RG_ALL,
                        ins=[DR["eo_mine"].h[(EPC - 1) * Ccap:(EPC - 1) * Ccap + 1536, :]],
                        outs=[DR["eo_all1a"].h[:, :]])),
                        reads=[DR["eo_mine"].r(ps=((EPC - 1) * Ccap,
                                                   (EPC - 1) * Ccap + 1536))],
                        writes=[DR["eo_all1a"].r()], kind="coll",
                        dbg=f"{t}ag_eo1a")
        em.op(POOL, (lambda e: e.collective_compute(
            "AllGather", ALU.bypass, RG_ALL,
            ins=[DR["eo_mine"].h[(EPC - 1) * Ccap + 1536:EPC * Ccap, :]],
            outs=[DR["eo_all1b"].h[:, :]])),
            reads=[DR["eo_mine"].r(ps=((EPC - 1) * Ccap + 1536, EPC * Ccap))],
            writes=[DR["eo_all1b"].r()], kind="coll", dbg=f"{t}ag_eo1b")
        em.barrier()
        bld.free(eiT, wdx_sb, wgwu_sb)
        for s in range(NSLOT):
            for k, rb_, rb2_, rb3_ in ((0, ret0, ret2, ret4),
                                       (1, ret1, ret3, ret5)):
                col = k * NSLOT + s
                em.op(POOL, (lambda e, col=col, rb_=rb_: e.indirect_dma_start(
                    rb_[:, :], None, DR["eo_all0"].h[:, :],
                    IndirectOffsetOnAxis(dsti[:, col:col + 1], 0))),
                    reads=[DR["eo_all0"].r(), dsti.r(fs=(col, col + 1))],
                    writes=[rb_.r()], kind="dma", dbg=f"{t}ret{k}_{s}")
                em.op(POOL, (lambda e, col=col, rb3_=rb3_: e.indirect_dma_start(
                    rb3_[:, :], None, DR["eo_all1b"].h[:, :],
                    IndirectOffsetOnAxis(dsti1b[:, col:col + 1], 0))),
                    reads=[DR["eo_all1b"].r(), dsti1b.r(fs=(col, col + 1))],
                    writes=[rb3_.r()], kind="dma", dbg=f"{t}retb{k}_{s}")
                em.op(POOL, (lambda e, col=col, rb2_=rb2_: e.indirect_dma_start(
                    rb2_[:, :], None, DR["eo_all1a"].h[:, :],
                    IndirectOffsetOnAxis(dsti1a[:, col:col + 1], 0))),
                    reads=[DR["eo_all1a"].r(), dsti1a.r(fs=(col, col + 1))],
                    writes=[rb2_.r()], kind="dma", dbg=f"{t}reta{k}_{s}")
            vop("tensor_scalar_mul", retf[:, :], ret0[:, :],
                ve0w[:, s:s + 1],
                reads=[ret0.r(), ve0w.r(fs=(s, s + 1))], writes=[retf.r()],
                dbg=f"{t}rv0_{s}")
            vop("scalar_tensor_tensor", retf[:, :], ret2[:, :],
                ve1aw[:, s:s + 1], retf[:, :], ALU.mult, ALU.add,
                reads=[ret2.r(), ve1aw.r(fs=(s, s + 1)), retf.r()],
                writes=[retf.r()], dbg=f"{t}rva_{s}")
            vop("scalar_tensor_tensor", retf[:, :], ret4[:, :],
                ve1bw[:, s:s + 1], retf[:, :], ALU.mult, ALU.add,
                reads=[ret4.r(), ve1bw.r(fs=(s, s + 1)), retf.r()],
                writes=[retf.r()], dbg=f"{t}rvb_{s}")
            vop("scalar_tensor_tensor", retf[:, :], ret1[:, :],
                ve0w[:, NSLOT + s:NSLOT + s + 1], retf[:, :], ALU.mult,
                ALU.add,
                reads=[ret1.r(), ve0w.r(fs=(NSLOT + s, NSLOT + s + 1)),
                       retf.r()],
                writes=[retf.r()], dbg=f"{t}rv1_{s}")
            vop("scalar_tensor_tensor", retf[:, :], ret3[:, :],
                ve1aw[:, NSLOT + s:NSLOT + s + 1], retf[:, :], ALU.mult,
                ALU.add,
                reads=[ret3.r(), ve1aw.r(fs=(NSLOT + s, NSLOT + s + 1)),
                       retf.r()],
                writes=[retf.r()], dbg=f"{t}rv1a_{s}")
            vop("scalar_tensor_tensor", retf[:, :], ret5[:, :],
                ve1bw[:, NSLOT + s:NSLOT + s + 1], retf[:, :], ALU.mult,
                ALU.add,
                reads=[ret5.r(), ve1bw.r(fs=(NSLOT + s, NSLOT + s + 1)),
                       retf.r()],
                writes=[retf.r()], dbg=f"{t}rv1b_{s}")
            vop("tensor_tensor", x_res[:, s * D:(s + 1) * D],
                x_res[:, s * D:(s + 1) * D], retf[:, :], ALU.add,
                reads=[x_res.r(fs=(s * D, (s + 1) * D)), retf.r()],
                writes=[x_res.r(fs=(s * D, (s + 1) * D))], dbg=f"{t}radd{s}")
        em.barrier()
        bld.free(zf, retf, ret5, ret4, ret3, ret2, ret1, ret0, eo_sb, gu_chunk)

    if "ffn1" in stages:
        emit_ffn(0)

    if "attn_b" in stages:
        x_res_b = HV["x"]
        hd = cfg.hdB
        HPT = 128 // hd
        NHT = cfg.HB // HPT
        VBW = cfg.HB * (hd + 1)
        scale_b = 1.0 / math.sqrt(hd)
        xT = bld.sb("b_xT", (128, ND * NT), BF16)
        xh = bld.sb("b_xh", (128, 2 * D), BF16)
        xh32 = bld.sb("b_xh32", (128, 2 * D), F32)
        sq_scr = bld.sb("b_sq", (128, 2 * D), BF16)
        for s in range(NSLOT):
            rot = s % 2
            layernorm(x_res_b, s, None, None, xh[:, rot * D:(rot + 1) * D],
                      [xh.r(fs=(rot * D, (rot + 1) * D))], rot=rot,
                      dbg=f"b_ln{s}", sq_scr=sq_scr)
            for dt in range(ND):
                po = (dt % 8) * 128
                PTX = PTB[dt % 2]
                transpose(PTX[:, po:po + 128],
                          xh[:, rot * D + dt * 128:rot * D + (dt + 1) * 128],
                          [xh.r(fs=(rot * D + dt * 128, rot * D + (dt + 1) * 128))],
                          [PTX.r(fs=(po, po + 128))], dbg=f"b_xt{s}_{dt}")
                vop("tensor_copy",
                    xT[:, dt * NT + s * 128:dt * NT + (s + 1) * 128],
                    PTX[:, po:po + 128], reads=[PTX.r(fs=(po, po + 128))],
                    writes=[xT.r(fs=(dt * NT + s * 128, dt * NT + (s + 1) * 128))],
                    dbg=f"b_xte{s}_{dt}")
        dma(SP, DR["spill_x"].h.rearrange("(s p) d -> p s d", p=128),
            x_res_b[:, :], [x_res_b.r()], [DR["spill_x"].r()], dbg="b_spx")
        dma(SP, DR["xt_spill"].h.rearrange("(t p) c -> p t c", p=128),
            xT[:, :], [xT.r()], [DR["xt_spill"].r()], dbg="b_spxt")
        em.barrier()
        bld.free(sq_scr, xh32, xh, xT, x_res_b)
        xT2 = bld.sb("b_xT2", (128, ND * NT), BF16)
        dma(SP, xT2[:, :], DR["xt_spill"].h.rearrange("(t p) c -> p t c", p=128),
            [DR["xt_spill"].r()], [xT2.r()], dbg="b_ldxt")
        wT_sb = bld.sb("b_wT", (128, 2 * ND * D), BF16)
        pT_sb = bld.sb("b_pT", (128, 2 * ND * NT), BF16)
        rope_c = bld.sb("b_ropec", (128, NT), BF16)
        rope_s = bld.sb("b_ropes", (128, NT), BF16)
        rope_t = bld.sb("b_ropet", (128, 2 * 512), BF16)
        vstg = bld.sb("b_vstg", (128, 2 * 512), BF16)
        dma(SP, rope_c[:, :], CN["ropecos"].h, [CN["ropecos"].r()],
            [rope_c.r()], dbg="b_ldrc")
        dma(SP, rope_s[:, :], CN["ropesin"].h, [CN["ropesin"].r()],
            [rope_s.r()], dbg="b_ldrs")
        ropeperm_sb = bld.sb("b_rpm", (128, 128), BF16)
        dma(SP, ropeperm_sb[:, :], CN["ropeperm"].h, [CN["ropeperm"].r()],
            [ropeperm_sb.r()], dbg="b_ldrp")

        def load_wT(wname, dbg, wb):
            wo0 = wb * ND * D
            for jt in range(ND):
                dma(SP, wT_sb[:, wo0 + jt * D:wo0 + (jt + 1) * D],
                    DR[wname].h[:, jt * 128:(jt + 1) * 128],
                    [DR[wname].r()], [wT_sb.r(fs=(wo0 + jt * D, wo0 + (jt + 1) * D))],
                    dbg=f"{dbg}{jt}", transpose=True)

        def project_fm(tag, wb, pbx):
            wo0 = wb * ND * D
            po0 = pbx * ND * NT
            for it in range(ND):
                for tc in range(NTCH):
                    pb = PB[(it * NTCH + tc) % 2]
                    key = (tag, it, tc)
                    for jt in range(ND):
                        mm(pb[:, 0:TCH],
                           wT_sb[:, wo0 + jt * D + it * 128:wo0 + jt * D + (it + 1) * 128],
                           xT2[:, jt * NT + tc * TCH:jt * NT + (tc + 1) * TCH],
                           [wT_sb.r(fs=(wo0 + jt * D + it * 128, wo0 + jt * D + (it + 1) * 128)),
                            xT2.r(fs=(jt * NT + tc * TCH, jt * NT + (tc + 1) * TCH))],
                           [pb.r(fs=(0, TCH))], start=(jt == 0),
                           stop=(jt == ND - 1), key=key, dbg=f"{tag}_{it}_{tc}")
                    act(pT_sb[:, po0 + it * NT + tc * TCH:po0 + it * NT + (tc + 1) * TCH],
                        pb[:, 0:TCH], AF.Copy, [pb.r(fs=(0, TCH))],
                        [pT_sb.r(fs=(po0 + it * NT + tc * TCH, po0 + it * NT + (tc + 1) * TCH))],
                        dbg=f"{tag}e_{it}_{tc}")

        def rope_fm(tag, pbx):
            po0 = pbx * ND * NT
            for it in range(ND):
                for tc in range(NT // 512 if NT >= 512 else 1):
                    W512 = min(512, NT)
                    c0 = po0 + it * NT + tc * W512
                    pb = PB[2 + tc % 2]
                    mm(pb[:, 0:W512], ropeperm_sb[:, :],
                       pT_sb[:, c0:c0 + W512],
                       [ropeperm_sb.r(), pT_sb.r(fs=(c0, c0 + W512))],
                       [pb.r(fs=(0, W512))], start=True, stop=True,
                       dbg=f"{tag}r_{it}_{tc}")
                    sl = (tc % 2) * 512
                    vop("tensor_tensor", rope_t[:, sl:sl + W512],
                        pT_sb[:, c0:c0 + W512],
                        rope_c[:, tc * W512:tc * W512 + W512], ALU.mult,
                        reads=[pT_sb.r(fs=(c0, c0 + W512)),
                               rope_c.r(fs=(tc * W512, tc * W512 + W512))],
                        writes=[rope_t.r(fs=(sl, sl + W512))],
                        dbg=f"{tag}c_{it}_{tc}")
                    vop("tensor_tensor", pT_sb[:, c0:c0 + W512],
                        pb[:, 0:W512],
                        rope_s[:, tc * W512:tc * W512 + W512], ALU.mult,
                        reads=[pb.r(fs=(0, W512)),
                               rope_s.r(fs=(tc * W512, tc * W512 + W512))],
                        writes=[pT_sb.r(fs=(c0, c0 + W512))],
                        dbg=f"{tag}s_{it}_{tc}")
                    vop("tensor_tensor", pT_sb[:, c0:c0 + W512],
                        pT_sb[:, c0:c0 + W512], rope_t[:, sl:sl + W512],
                        ALU.add,
                        reads=[pT_sb.r(fs=(c0, c0 + W512)),
                               rope_t.r(fs=(sl, sl + W512))],
                        writes=[pT_sb.r(fs=(c0, c0 + W512))],
                        dbg=f"{tag}a_{it}_{tc}")

        load_wT("wqb_bf", "b_lwq", 0)
        load_wT("wkb_bf", "b_lwk", 1)
        project_fm("b_q", 0, 0)
        rope_fm("b_q", 0)
        dma(SP, DR["qtb_spill"].h.rearrange("(t p) c -> p t c", p=128),
            pT_sb[:, 0:ND * NT], [pT_sb.r(fs=(0, ND * NT))],
            [DR["qtb_spill"].r()], dbg="b_spq")
        project_fm("b_k", 1, 1)
        rope_fm("b_k", 1)
        dma(SP, DR["ktb_st"].h.rearrange("(t p) c -> p t c", p=128),
            pT_sb[:, ND * NT:2 * ND * NT], [pT_sb.r(fs=(ND * NT, 2 * ND * NT))],
            [DR["ktb_st"].r()], dbg="b_spk")
        load_wT("wvb_bf", "b_lwv", 0)
        for s in range(NSLOT):
            for dc in range(D // 512 if D >= 512 else 1):
                W512 = min(512, D)
                pb = PB[4 + dc % 2]
                key = ("b_v", s, dc)
                for jt in range(ND):
                    mm(pb[:, 0:W512],
                       xT2[:, jt * NT + s * 128:jt * NT + (s + 1) * 128],
                       wT_sb[:, jt * D + dc * W512:jt * D + (dc + 1) * W512],
                       [xT2.r(fs=(jt * NT + s * 128, jt * NT + (s + 1) * 128)),
                        wT_sb.r(fs=(jt * D + dc * W512, jt * D + (dc + 1) * W512))],
                       [pb.r(fs=(0, W512))], start=(jt == 0),
                       stop=(jt == ND - 1), key=key, dbg=f"b_v{s}_{dc}")
                sl = (dc % 2) * 512
                act(vstg[:, sl:sl + W512], pb[:, 0:W512], AF.Copy,
                    [pb.r(fs=(0, W512))], [vstg.r(fs=(sl, sl + W512))],
                    dbg=f"b_ve{s}_{dc}")
                dma(SP, DR["vb_st"].h[s * 128:(s + 1) * 128,
                                      dc * W512:(dc + 1) * W512],
                    vstg[:, sl:sl + W512], [vstg.r(fs=(sl, sl + W512))],
                    [DR["vb_st"].r(ps=(s * 128, (s + 1) * 128),
                                   fs=(dc * W512, (dc + 1) * W512))],
                    dbg=f"b_vst{s}_{dc}")
        em.op(POOL, (lambda e: e.collective_compute(
            "AllGather", ALU.bypass, RG_PAIRS,
            ins=[DR["ktb_st"].h[:, :]], outs=[DR["ktb_all"].h[:, :]])),
            reads=[DR["ktb_st"].r()], writes=[DR["ktb_all"].r()], kind="coll",
            dbg="b_ag_k")
        em.op(POOL, (lambda e: e.collective_compute(
            "AllGather", ALU.bypass, RG_PAIRS,
            ins=[DR["vb_st"].h[:, :]], outs=[DR["vb_all"].h[:, :]])),
            reads=[DR["vb_st"].r()], writes=[DR["vb_all"].r()], kind="coll",
            dbg="b_ag_v")
        em.barrier()
        bld.free(ropeperm_sb, vstg, rope_t, rope_s, rope_c, pT_sb, wT_sb, xT2)
        ktb_sb = bld.sb("b_K", (128, ND * 2 * NT), BF16)
        vb_sb = bld.sb("b_V", (128, 2 * NSLOT * VBW), BF16)
        wobT_sb = bld.sb("b_woT", (128, ND * D), BF16)
        qs_sb = bld.sb("b_qs", (128, 2 * ND * 512), BF16)
        Yb = bld.sb("b_Y", (128, 4 * D), BF16)
        ptb2 = [bld.sb(f"b_pt{i}", (128, 512), BF16) for i in range(3)]
        yT2 = bld.sb("b_yT", (128, ND * 128), BF16)
        xadd = bld.sb("b_xadd", (128, 2 * D), F32)
        vint = bld.sb("b_vint", (128, 2 * D), BF16)
        for jt in range(ND):
            dma(SP, wobT_sb[:, jt * D:(jt + 1) * D],
                DR["wob_bf"].h[:, jt * 128:(jt + 1) * 128],
                [DR["wob_bf"].r()], [wobT_sb.r(fs=(jt * D, (jt + 1) * D))],
                dbg=f"b_lwo{jt}", transpose=True)
        for dt in range(ND):
            for rank in range(2):
                dma(SP, ktb_sb[:, dt * 2 * NT + rank * NT:dt * 2 * NT + (rank + 1) * NT],
                    DR["ktb_all"].h[rank * D + dt * 128:rank * D + (dt + 1) * 128, :],
                    [DR["ktb_all"].r(ps=(rank * D + dt * 128,
                                         rank * D + (dt + 1) * 128))],
                    [ktb_sb.r(fs=(dt * 2 * NT + rank * NT,
                                  dt * 2 * NT + (rank + 1) * NT))],
                    dbg=f"b_ldk{dt}_{rank}")
        for jb in range(2 * NSLOT):
            rank, sl_ = jb // NSLOT, jb % NSLOT
            vs = (jb % 2) * D
            dma(SP, vint[:, vs:vs + D],
                DR["vb_all"].h[rank * NT + sl_ * 128:rank * NT + (sl_ + 1) * 128, :],
                [DR["vb_all"].r(ps=(rank * NT + sl_ * 128,
                                    rank * NT + (sl_ + 1) * 128))],
                [vint.r(fs=(vs, vs + D))], dbg=f"b_ldv{jb}")
            vop("tensor_copy",
                vb_sb.h[:, :].rearrange("p (j h c) -> p j h c", j=2 * NSLOT,
                                        h=cfg.HB)[:, jb, :, 0:hd],
                vint.h[:, :].rearrange("p (r h c) -> p r h c", r=2,
                                       h=cfg.HB)[:, jb % 2, :, :],
                reads=[vint.r(fs=(vs, vs + D))],
                writes=[vb_sb.r(fs=(jb * VBW, (jb + 1) * VBW))],
                dbg=f"b_vi{jb}")
            vop("memset",
                vb_sb.h[:, :].rearrange("p (j h c) -> p j h c", j=2 * NSLOT,
                                        h=cfg.HB)[:, jb, :, hd:hd + 1], 1.0,
                writes=[vb_sb.r(fs=(jb * VBW, (jb + 1) * VBW))],
                dbg=f"b_vo{jb}")
        HW2 = hd + 1
        NG = NSLOT // 4
        for t in range(NG):
            qsl = (t % 2) * ND * 512
            dma(SP, qs_sb[:, qsl:qsl + ND * 512],
                DR["qtb_spill"].h[:, t * 512:(t + 1) * 512]
                .rearrange("(t p) c -> p t c", p=128),
                [DR["qtb_spill"].r()],
                [qs_sb.r(fs=(qsl, qsl + ND * 512))], dbg=f"b_lq{t}")
            njt = 8 * t + 8
            for h in range(cfg.HB):
                dt = h // HPT
                base = (h % HPT) * hd
                av = PB[4 + h % 2]
                avk = ("avb", t, h)
                for j in range(njt + 1):
                    if j < njt:
                        jb = (j % 2) * NSLOT + j // 2
                        kcol = dt * 2 * NT + (j % 2) * NT + (j // 2) * 128
                        sc_b = PB[2 + j % 2]
                        jrel = j - 8 * t
                        mm(sc_b[:, 0:512],
                           ktb_sb[base:base + hd, kcol:kcol + 128],
                           qs_sb[base:base + hd, qsl + dt * 512:qsl + (dt + 1) * 512],
                           [ktb_sb.r(ps=(base, base + hd), fs=(kcol, kcol + 128)),
                            qs_sb.r(ps=(base, base + hd),
                                    fs=(qsl + dt * 512, qsl + (dt + 1) * 512))],
                           [sc_b.r(fs=(0, 512))], start=True, stop=(jrel < 0),
                           dbg=f"b_sc{t}_{h}_{j}")
                        if jrel >= 0:
                            r_ = jrel // 2
                            mp = jrel % 2
                            mm(sc_b[:, r_ * 128:(r_ + 1) * 128], sb_nbig[:, :],
                               sb_maskp[:, mp * 128:(mp + 1) * 128],
                               [sb_nbig.r(), sb_maskp.r(fs=(mp * 128, (mp + 1) * 128))],
                               [sc_b.r(fs=(r_ * 128, (r_ + 1) * 128))],
                               start=False, stop=True, dbg=f"b_ma{t}_{h}_{j}")
                        pt = ptb2[j % 3]
                        act(pt[:, :], sc_b[:, 0:512], AF.Exp,
                            [sc_b.r(fs=(0, 512)), sb_expbias.r()], [pt.r()],
                            bias=sb_expbias[:, 0:1], scale=scale_b,
                            dbg=f"b_ex{t}_{h}_{j}")
                    if j >= 1:
                        jj = j - 1
                        jb = (jj % 2) * NSLOT + jj // 2
                        pt = ptb2[jj % 3]
                        for r in range(4):
                            if jj > 8 * t + 2 * r + 1:
                                continue
                            ao = r * HW2
                            mm(av[:, ao:ao + HW2], pt[:, r * 128:(r + 1) * 128],
                               vb_sb[:, jb * VBW + h * HW2:jb * VBW + (h + 1) * HW2],
                               [pt.r(fs=(r * 128, (r + 1) * 128)),
                                vb_sb.r(fs=(jb * VBW + h * HW2,
                                            jb * VBW + (h + 1) * HW2))],
                               [av.r(fs=(ao, ao + HW2))],
                               start=(jj == 0 and r == 0),
                               stop=(jj == njt - 1 and r == 3),
                               key=avk, dbg=f"b_av{t}_{h}_{jj}_{r}")
                for r in range(4):
                    ao = r * HW2
                    rs = (h % 2) * 4 + r
                    rc = scr8[:, rs:rs + 1]
                    rcr = ("scr8", (0, 128, rs, rs + 1))
                    vop("reciprocal", rc, av[:, ao + hd:ao + hd + 1],
                        reads=[av.r(fs=(ao + hd, ao + hd + 1))], writes=[rcr],
                        dbg=f"b_rc{t}_{h}_{r}")
                    act(Yb[:, r * D + h * hd:r * D + (h + 1) * hd],
                        av[:, ao:ao + hd], AF.Copy,
                        [av.r(fs=(ao, ao + hd)), rcr],
                        [Yb.r(fs=(r * D + h * hd, r * D + (h + 1) * hd))],
                        scale=rc, dbg=f"b_ye{t}_{h}_{r}")
            DCW = min(512, D)
            for r in range(4):
                s = 4 * t + r
                for dh_t in range(ND):
                    po = (dh_t % 8) * 128
                    PTX = PTB[dh_t % 2]
                    transpose(PTX[:, po:po + 128],
                              Yb[:, r * D + dh_t * 128:r * D + (dh_t + 1) * 128],
                              [Yb.r(fs=(r * D + dh_t * 128, r * D + (dh_t + 1) * 128))],
                              [PTX.r(fs=(po, po + 128))], dbg=f"b_ytr{s}_{dh_t}")
                    vop("tensor_copy", yT2[:, dh_t * 128:(dh_t + 1) * 128],
                        PTX[:, po:po + 128], reads=[PTX.r(fs=(po, po + 128))],
                        writes=[yT2.r(fs=(dh_t * 128, (dh_t + 1) * 128))],
                        dbg=f"b_ytre{s}_{dh_t}")
                xsl = (s % 2) * D
                dma(SP, xadd[:, xsl:xsl + D],
                    DR["spill_x"].h[s * 128:(s + 1) * 128, :],
                    [DR["spill_x"].r(ps=(s * 128, (s + 1) * 128))],
                    [xadd.r(fs=(xsl, xsl + D))], dbg=f"b_lx{s}")
                for dc in range(D // DCW):
                    pb = PB[dc % 2]
                    key = ("ob", s, dc)
                    for dh_t in range(ND):
                        mm(pb[:, 0:DCW], yT2[:, dh_t * 128:(dh_t + 1) * 128],
                           wobT_sb[:, dh_t * D + dc * DCW:dh_t * D + (dc + 1) * DCW],
                           [yT2.r(fs=(dh_t * 128, (dh_t + 1) * 128)),
                            wobT_sb.r(fs=(dh_t * D + dc * DCW, dh_t * D + (dc + 1) * DCW))],
                           [pb.r(fs=(0, DCW))], start=(dh_t == 0),
                           stop=(dh_t == ND - 1), key=key, dbg=f"b_o{s}_{dc}")
                    vop("tensor_tensor",
                        xadd[:, xsl + dc * DCW:xsl + (dc + 1) * DCW],
                        xadd[:, xsl + dc * DCW:xsl + (dc + 1) * DCW],
                        pb[:, 0:DCW], ALU.add,
                        reads=[xadd.r(fs=(xsl + dc * DCW, xsl + (dc + 1) * DCW)),
                               pb.r(fs=(0, DCW))],
                        writes=[xadd.r(fs=(xsl + dc * DCW, xsl + (dc + 1) * DCW))],
                        dbg=f"b_oa{s}_{dc}")
                dma(SP, DR["spill_x"].h[s * 128:(s + 1) * 128, :],
                    xadd[:, xsl:xsl + D], [xadd.r(fs=(xsl, xsl + D))],
                    [DR["spill_x"].r(ps=(s * 128, (s + 1) * 128))],
                    dbg=f"b_sx{s}")
        em.barrier()
        bld.free(vint, xadd, yT2, *reversed(ptb2))
        bld.free(Yb, qs_sb, wobT_sb, vb_sb, ktb_sb)
        x_res2 = bld.sb("x_res2", (128, NSLOT * D), F32)
        HV["x"] = x_res2
        dma(SP, x_res2[:, :],
            DR["spill_x"].h.rearrange("(s p) d -> p s d", p=128),
            [DR["spill_x"].r()], [x_res2.r()], dbg="b_rex")
        em.barrier()

    if "ffn2" in stages:
        emit_ffn(1)

    x_out = HV["x"]
    dma(SP, out_t.h.rearrange("(s p) d -> p s d", p=128), x_out[:, :],
        [x_out.r()], [out_t.r()], dbg="store_out")
    em.barrier()
    em.flush()
    return nc


def kernel(**inputs):
    cfg = Cfg()
    nc = build_program(cfg)
    in_maps = shard_inputs(cfg, inputs)
    res = run_bass_kernel_spmd(nc, in_maps, core_ids=list(range(cfg.NCORES)))
    return assemble_output(cfg, [r for r in res.results])


# version 21
# speedup vs baseline: 1.0103x; 1.0103x over previous
import math
from contextlib import ExitStack

import numpy as np
import ml_dtypes

import concourse.bass as bass
import concourse.mybir as mybir
from concourse.bass_utils import run_bass_kernel_spmd
from concourse.bass_core_rust import IndirectOffsetOnAxis

F32 = mybir.dt.float32
BF16 = mybir.dt.bfloat16
I32 = mybir.dt.int32
U32 = mybir.dt.uint32
AF = mybir.ActivationFunctionType
ALU = mybir.AluOpType
AX = mybir.AxisListType

PE, ACT, DVE, POOL, SP = "PE", "ACT", "DVE", "POOL", "SP"
COMPUTE_ENGINES = (PE, ACT, DVE, POOL, SP)
N_DMA_SEMS = {SP: 8, POOL: 6}
N_CC_SEMS = 2

NEG_BIG = -1.0e30
EXP_SHIFT = 12.0


class Cfg:
    def __init__(self, B=4, T=4096, D=1024, F=1408, E=16, TOPK=2, CAPF=1.25,
                 HA=8, KV=2, HD=128, DLAT=32, HB=16):
        self.B, self.T, self.D, self.F, self.E = B, T, D, F, E
        self.TOPK, self.CAPF = TOPK, CAPF
        self.HA, self.KV, self.HD, self.DLAT = HA, KV, HD, DLAT
        self.HB = HB
        self.hdB = D // HB
        self.N = B * T
        self.C = int(math.ceil(CAPF * self.N * TOPK / E))
        self.NCORES = 2 * B
        self.NT = T // 2
        self.NSLOT = self.NT // 128
        self.GT = T // 128
        self.ND = D // 128
        self.NF = F // 128
        self.EPC = E // self.NCORES
        assert E % self.NCORES == 0 and T % 256 == 0 and D % 128 == 0 and F % 128 == 0
        assert self.C % 128 == 0, "capacity must be tile aligned for the gathers"
        self.CT = self.C // 128
        self.MYROWS = self.EPC * self.C



class _Op:
    __slots__ = ("eng", "fn", "seq", "idx", "kind", "inc", "sem", "sem_count",
                 "waits", "ksnap", "accum_key", "dbg", "inst_name")

    def __init__(self, eng, fn, seq, idx, kind, dbg):
        self.eng = eng
        self.fn = fn
        self.seq = seq
        self.idx = idx
        self.kind = kind
        self.inc = False
        self.sem = None
        self.sem_count = None
        self.waits = []
        self.ksnap = None
        self.accum_key = None
        self.dbg = dbg


class Emitter:
    def __init__(self, nc):
        self.nc = nc
        self.ops = []
        self.stream = {e: [] for e in COMPUTE_ENGINES}
        self.access = {}
        self.K = {e: {} for e in COMPUTE_ENGINES}
        self.dma_idx = {SP: 0, POOL: 0}
        self.cc_idx = 0
        self._es = ExitStack()
        self.engine_sems = {e: self._es.enter_context(nc.semaphore(f"es_{e}"))
                            for e in COMPUTE_ENGINES}
        self.dma_sems = {q: [self._es.enter_context(nc.semaphore(f"dq_{q}_{i}"))
                             for i in range(n)] for q, n in N_DMA_SEMS.items()}
        self.cc_sems = [self._es.enter_context(nc.semaphore(f"cc_{i}"))
                        for i in range(N_CC_SEMS)]
        self.psum_names = set()


    @staticmethod
    def _isect(a, b):
        return not (a[1] <= b[0] or b[1] <= a[0] or a[3] <= b[2] or b[3] <= a[2])

    @staticmethod
    def _covers(outer, inner):
        return (outer[0] <= inner[0] and outer[1] >= inner[1]
                and outer[2] <= inner[2] and outer[3] >= inner[3])

    def _knows(self, K, p):
        if p.kind == "compute":
            return K.get(("st", p.eng), -1) >= p.idx
        return K.get(p.sem, 0) >= p.sem_count

    def _merge_from(self, K, p):
        snap = p.ksnap
        if snap:
            for k, v in snap.items():
                if K.get(k, -1 if k[0] == "st" else 0) < v:
                    K[k] = v
        if p.kind == "compute":
            k = ("st", p.eng)
            if K.get(k, -1) < p.idx:
                K[k] = p.idx
        else:
            if K.get(p.sem, 0) < p.sem_count:
                K[p.sem] = p.sem_count


    def op(self, eng, fn, reads=(), writes=(), kind="compute", accum_key=None,
           inc=False, dbg=""):
        idx = len(self.stream[eng])
        o = _Op(eng, fn, len(self.ops), idx, kind, dbg)
        o.accum_key = accum_key
        o.inc = inc
        K = self.K[eng]
        if kind == "dma":
            i = self.dma_idx[eng]
            o.sem = ("dma", eng, i % N_DMA_SEMS[eng])
            o.sem_count = i // N_DMA_SEMS[eng] + 1
            o.inc = True
            self.dma_idx[eng] = i + 1
            if o.sem_count > 1 and K.get(o.sem, 0) < o.sem_count - 1:
                o.waits.append((o.sem, o.sem_count - 1))
                K[o.sem] = o.sem_count - 1
        elif kind == "coll":
            i = self.cc_idx
            o.sem = ("cc", POOL, i % N_CC_SEMS)
            o.sem_count = i // N_CC_SEMS + 1
            o.inc = True
            self.cc_idx = i + 1
            if o.sem_count > 1 and K.get(o.sem, 0) < o.sem_count - 1:
                o.waits.append((o.sem, o.sem_count - 1))
                K[o.sem] = o.sem_count - 1

        prods = []
        seen = set()
        for (name, rect) in reads:
            self._scan(name, rect, False, prods, seen)
        for (name, rect) in writes:
            self._scan(name, rect, True, prods, seen)
        prods.sort(key=lambda p: p.seq)
        for p in reversed(prods):
            if p is o or self._knows(K, p):
                continue
            if (accum_key is not None and p.eng == eng and
                    p.accum_key == accum_key):
                continue
            if p.eng == eng and p.kind == "compute" and kind in ("compute", "drain"):
                d = _Op(eng, (lambda e: e.drain()), len(self.ops),
                        len(self.stream[eng]), "drain", "drain")
                d.ksnap = None
                self.ops.append(d)
                self.stream[eng].append(d)
                K[("st", eng)] = d.idx - 1
                o.idx = len(self.stream[eng])
            elif p.kind == "compute":
                p.inc = True
                o.waits.append((("eng", p.eng), p))
                self._merge_from(K, p)
            else:
                if K.get(p.sem, 0) < p.sem_count:
                    o.waits.append((p.sem, p.sem_count))
                self._merge_from(K, p)
        BR = (0, 128, 0, 1)
        bank_names = set()
        if eng in (PE, ACT, DVE):
            for (name, rect) in list(reads) + list(writes):
                if name in self.psum_names:
                    bank_names.add(name)
        if bank_names:
            prods2 = []
            seen2 = set()
            for name in bank_names:
                self._scan(("B", name), BR, eng == PE, prods2, seen2)
            prods2.sort(key=lambda p: p.seq)
            for p in reversed(prods2):
                if p is o or p.eng == eng or self._knows(K, p):
                    continue
                if p.kind == "compute":
                    p.inc = True
                    o.waits.append((("eng", p.eng), p))
                    self._merge_from(K, p)
                else:
                    if K.get(p.sem, 0) < p.sem_count:
                        o.waits.append((p.sem, p.sem_count))
                    self._merge_from(K, p)
        o.ksnap = dict(K)
        self.ops.append(o)
        self.stream[eng].append(o)
        for (name, rect) in reads:
            self.access.setdefault(name, []).append((o, False, rect))
        for (name, rect) in writes:
            self.access.setdefault(name, []).append((o, True, rect))
        for name in bank_names:
            self.access.setdefault(("B", name), []).append((o, eng == PE, BR))
        return o

    def _scan(self, name, rect, is_write, prods, seen):
        log = self.access.get(name)
        if not log:
            return
        for (p, p_is_write, p_rect) in reversed(log):
            if not (p_is_write or is_write):
                continue
            if self._isect(rect, p_rect):
                if id(p) not in seen:
                    seen.add(id(p))
                    prods.append(p)
                if p_is_write and self._covers(p_rect, rect):
                    return

    def barrier(self):
        marks = {}
        for e in COMPUTE_ENGINES:
            for o in reversed(self.stream[e]):
                if o.kind == "compute":
                    o.inc = True
                    marks[e] = o
                    break
        sem_marks = []
        for q in N_DMA_SEMS:
            n = self.dma_idx[q]
            for i in range(min(n, N_DMA_SEMS[q])):
                cnt = (n - 1 - i) // N_DMA_SEMS[q] + 1
                sem_marks.append((("dma", q, (n - 1 - i) % N_DMA_SEMS[q]), cnt))

        for e in COMPUTE_ENGINES:
            d = _Op(e, (lambda eng: eng.drain()), len(self.ops),
                    len(self.stream[e]), "drain", "bar_drain")
            self.ops.append(d)
            self.stream[e].append(d)
            o = _Op(e, (lambda eng: eng.nop()), len(self.ops),
                    len(self.stream[e]), "compute", "barrier")
            K = self.K[e]
            K[("st", e)] = d.idx - 1
            for e2, m in marks.items():
                if e2 != e and not self._knows(K, m):
                    o.waits.append((("eng", e2), m))
                    self._merge_from(K, m)
            for (semkey, cnt) in sem_marks:
                if K.get(semkey, 0) < cnt:
                    o.waits.append((semkey, cnt))
                    K[semkey] = cnt
            o.ksnap = dict(K)
            self.ops.append(o)
            self.stream[e].append(o)
        kept = {}
        for name, log in self.access.items():
            k = [(p, w, r) for (p, w, r) in log if p.kind == "coll"]
            if k:
                kept[name] = k
        self.access = kept


    def flush(self):
        cnt = {e: 0 for e in COMPUTE_ENGINES}
        for e in COMPUTE_ENGINES:
            for o in self.stream[e]:
                if o.kind == "compute" and o.inc:
                    cnt[e] += 1
                    o.sem = ("eng", e)
                    o.sem_count = cnt[e]
        nc = self.nc

        def sem_of(key):
            if key[0] == "eng":
                return self.engine_sems[key[1]]
            if key[0] == "dma":
                return self.dma_sems[key[1]][key[2]]
            return self.cc_sems[key[2]]

        def make_runner(stream):
            def run(eng):
                vc = {}
                for o in stream:
                    for (key, val) in o.waits:
                        if isinstance(val, _Op):
                            if val.sem_count is None:
                                continue
                            key = ("eng", val.eng)
                            val = val.sem_count
                        else:
                            val = val * (16 if key[0] == "dma" else 1)
                        if vc.get(key, 0) >= val:
                            continue
                        eng.wait_ge(sem_of(key), val)
                        vc[key] = val
                    inst = o.fn(eng)
                    if o.inc and o.sem is not None:
                        amt = 16 if o.kind == "dma" else 1
                        inst.then_inc(sem_of(o.sem), amt)
            return run

        with nc.Block() as blk:
            deco = {PE: blk.tensor, ACT: blk.scalar, DVE: blk.vector,
                    POOL: blk.gpsimd, SP: blk.sync}
            for e in COMPUTE_ENGINES:
                deco[e](make_runner(self.stream[e]))
        nc.finalize()



class Buf:
    def __init__(self, name, handle, shape):
        self.name = name
        self.h = handle
        self.shape = shape

    def __getitem__(self, idx):
        return self.h[idx]

    def r(self, ps=None, fs=None):
        p0, p1 = (0, self.shape[0]) if ps is None else ps
        f0, f1 = (0, int(np.prod(self.shape[1:]))) if fs is None else fs
        return (self.name, (p0, p1, f0, f1))


class Builder:
    def __init__(self, nc, cfg):
        self.nc = nc
        self.cfg = cfg
        self.em = Emitter(nc)
        self.es = ExitStack()
        self.bufs = {}
        self.inputs = {}

    def sb(self, name, shape, dtype):
        cm = self.nc.sbuf_tensor(name, list(shape), dtype)
        h = cm.__enter__()
        b = Buf(name, h, tuple(shape))
        b._cm = cm
        self.bufs[name] = b
        return b

    def free(self, *bufs):
        for b in bufs:
            try:
                b._cm.__exit__(None, None, None)
                b._cm = None
            except AssertionError as e:
                live = [x for x in self.bufs
                        if getattr(self.bufs[x], "_cm", None) is not None]
                raise AssertionError(f"{e} (buf={b.name}; live={live})") from e


    def psum(self, name, shape, dtype):
        h = self.es.enter_context(self.nc.psum_tensor(name, list(shape), dtype))
        b = Buf(name, h, tuple(shape))
        self.bufs[name] = b
        self.em.psum_names.add(name)
        return b

    def dram(self, name, shape, dtype, kind="Internal"):
        h = self.nc.dram_tensor(name, list(shape), dtype, kind=kind)
        b = Buf(name, h.ap(), tuple(shape))
        self.bufs[name] = b
        return b



def make_consts(cfg, core):
    b, par = core // 2, core % 2
    C = {}
    ident = np.eye(128, dtype=np.float32)
    C["ident"] = ident.astype(ml_dtypes.bfloat16)
    C["negbig_ident"] = (NEG_BIG * ident).astype(ml_dtypes.bfloat16)
    tstrict = (np.arange(128)[:, None] < np.arange(128)[None, :])
    C["tstrict"] = tstrict.astype(ml_dtypes.bfloat16)
    tri_mask = (np.arange(128)[:, None] > np.arange(128)[None, :]).astype(np.float32)
    ones = np.ones((128, 128), np.float32)
    zeros = np.zeros((128, 128), np.float32)
    mp = np.stack([tri_mask, ones] if par == 0 else [zeros, tri_mask])
    C["maskpair"] = mp.astype(ml_dtypes.bfloat16)
    C["maskpair4"] = np.tile(mp, (1, 1, 4)).astype(ml_dtypes.bfloat16)
    C["tri"] = tri_mask.astype(ml_dtypes.bfloat16)
    hd = cfg.hdB
    pos = np.zeros((cfg.NT,), np.float64)
    for s in range(cfg.NSLOT):
        g = 2 * s + par
        pos[s * 128:(s + 1) * 128] = g * 128 + np.arange(128)
    d = np.arange(128) % hd
    freqs = 1.0 / (10000.0 ** (2 * (d // 2) / hd))
    ang = freqs[:, None] * pos[None, :]
    C["ropecos"] = np.cos(ang).astype(ml_dtypes.bfloat16)
    C["ropesin"] = np.sin(ang).astype(ml_dtypes.bfloat16)
    P = np.zeros((hd, hd), np.float32)
    for i in range(hd // 2):
        P[2 * i, 2 * i + 1] = -1.0
        P[2 * i + 1, 2 * i] = 1.0
    nb = 128 // hd
    PT = np.zeros((128, 128), np.float32)
    for k in range(nb):
        PT[k * hd:(k + 1) * hd, k * hd:(k + 1) * hd] = P.T
    C["ropeperm"] = PT.astype(ml_dtypes.bfloat16)
    NGT = cfg.B * cfg.GT
    pm = np.zeros((NGT, cfg.NSLOT), np.float32)
    for rp in range(cfg.NCORES):
        bp, parp = rp // 2, rp % 2
        for sp in range(cfg.NSLOT):
            gtp = bp * cfg.GT + 2 * sp + parp
            for s in range(cfg.NSLOT):
                gt = b * cfg.GT + 2 * s + par
                if gtp < gt:
                    pm[rp * cfg.NSLOT + sp, s] = 1.0
    C["prefmask"] = pm.astype(ml_dtypes.bfloat16)
    C["iota16"] = np.broadcast_to(np.arange(cfg.E, dtype=np.float32),
                                  (128, cfg.E)).copy()
    ev = np.arange(cfg.E)
    C["iota_rkE"] = np.broadcast_to(
        np.tile(ev // cfg.EPC, cfg.NSLOT).astype(np.float32),
        (128, cfg.NSLOT * cfg.E)).copy()
    C["iota_elE"] = np.broadcast_to(
        np.tile(ev % cfg.EPC, cfg.NSLOT).astype(np.float32),
        (128, cfg.NSLOT * cfg.E)).copy()
    C["identf"] = np.eye(128, dtype=np.float32)
    rv = np.zeros((128, cfg.NSLOT), np.float32)
    for s in range(cfg.NSLOT):
        rv[:, s] = core * cfg.NT + s * 128 + np.arange(128) + 1
    C["rowvals"] = rv
    return C


def shard_inputs(cfg, inputs):
    x = np.asarray(inputs["L0_attn_a_x"])
    maps = []
    wnames = {
        "wqa": "L0_attn_a_wq", "wkda": "L0_attn_a_wk_down",
        "wvda": "L0_attn_a_wv_down", "wkua": "L0_attn_a_wk_up",
        "wvua": "L0_attn_a_wv_up", "woa": "L0_attn_a_wo",
        "wqb": "L0_attn_b_wq", "wkb": "L0_attn_b_wk",
        "wvb": "L0_attn_b_wv", "wob": "L0_attn_b_wo",
        "rw": "L0_ffn_router_w",
        "swg": "L0_ffn_sw_gate", "swu": "L0_ffn_sw_up", "swd": "L0_ffn_sw_down",
    }
    rb = np.broadcast_to(np.asarray(inputs["L0_ffn_router_b"]),
                         (128, cfg.E)).copy()
    for core in range(cfg.NCORES):
        b, par = core // 2, core % 2
        m = {}
        xs = np.empty((cfg.NT, cfg.D), np.float32)
        for s in range(cfg.NSLOT):
            g = 2 * s + par
            xs[s * 128:(s + 1) * 128] = x[b, g * 128:(g + 1) * 128]
        m["x"] = xs
        for k, v in wnames.items():
            m[k] = np.ascontiguousarray(np.asarray(inputs[v]))
        m["rb_bc"] = rb
        e0 = core * cfg.EPC
        m["mywg"] = np.ascontiguousarray(np.asarray(inputs["L0_ffn_w_gate"])[e0:e0 + cfg.EPC])
        m["mywu"] = np.ascontiguousarray(np.asarray(inputs["L0_ffn_w_up"])[e0:e0 + cfg.EPC])
        m["mywd"] = np.ascontiguousarray(np.asarray(inputs["L0_ffn_w_down"])[e0:e0 + cfg.EPC])
        m["myblk"] = (core * cfg.MYROWS + 128 * np.arange(cfg.MYROWS // 128, dtype=np.int32))[:, None]
        m.update(make_consts(cfg, core))
        maps.append(m)
    return maps


def assemble_output(cfg, results):
    out = np.empty((cfg.B, cfg.T, cfg.D), np.float32)
    for core in range(cfg.NCORES):
        b, par = core // 2, core % 2
        o = results[core]["out"]
        for s in range(cfg.NSLOT):
            g = 2 * s + par
            out[b, g * 128:(g + 1) * 128] = o[s * 128:(s + 1) * 128]
    return out



CUT = 99
LN_SLOTS = 16
DO_TR = True
LOAD_WOA = True


def build_program(cfg, stages=("attn_a", "ffn1", "attn_b", "ffn2")):
    nc = bass.Bass("TRN2", target_bir_lowering=False)
    bld = Builder(nc, cfg)
    em = bld.em
    NT, D, ND, NSLOT, E = cfg.NT, cfg.D, cfg.ND, cfg.NSLOT, cfg.E

    x_in = bld.dram("x", (NT, D), F32, kind="ExternalInput")
    out_t = bld.dram("out", (NT, D), F32, kind="ExternalOutput")
    W = {}
    W["myblk"] = bld.dram("myblk", (cfg.MYROWS // 128, 1), I32, kind="ExternalInput")
    for name, shape in [
        ("wqa", (D, cfg.HA * cfg.HD)), ("wkda", (D, cfg.KV * cfg.DLAT)),
        ("wvda", (D, cfg.KV * cfg.DLAT)), ("wkua", (cfg.DLAT, cfg.HD)),
        ("wvua", (cfg.DLAT, cfg.HD)), ("woa", (cfg.HA * cfg.HD, D)),
        ("wqb", (D, D)), ("wkb", (D, D)), ("wvb", (D, D)), ("wob", (D, D)),
        ("rw", (D, E)), ("rb_bc", (128, E)),
        ("mywg", (cfg.EPC, D, cfg.F)), ("mywu", (cfg.EPC, D, cfg.F)),
        ("mywd", (cfg.EPC, cfg.F, D)),
        ("swg", (D, cfg.F)), ("swu", (D, cfg.F)), ("swd", (cfg.F, D)),
    ]:
        W[name] = bld.dram(name, shape, F32, kind="ExternalInput")
    CN = {}
    for name, shape, dt in [
        ("ident", (128, 128), BF16), ("negbig_ident", (128, 128), BF16),
        ("tstrict", (128, 128), BF16), ("maskpair", (2, 128, 128), BF16),
        ("maskpair4", (2, 128, 512), BF16), ("tri", (128, 128), BF16),
        ("ropecos", (128, NT), BF16), ("ropesin", (128, NT), BF16),
        ("ropeperm", (128, 128), BF16),
        ("prefmask", (cfg.B * cfg.GT, NSLOT), BF16),
        ("iota16", (128, E), F32), ("rowvals", (128, NSLOT), F32),
        ("identf", (128, 128), F32),
        ("iota_rkE", (128, NSLOT * E), F32), ("iota_elE", (128, NSLOT * E), F32),
    ]:
        CN[name] = bld.dram(name, shape, dt, kind="ExternalInput")

    DR = {}
    DR["spill_x"] = bld.dram("spill_x", (NT, D), F32)
    for n in ("wqb_bf", "wkb_bf", "wvb_bf", "wob_bf"):
        DR[n] = bld.dram(n, (D, D), BF16)
    DR["wg_bf"] = bld.dram("wg_bf", (cfg.EPC, D, cfg.F), BF16)
    DR["wu_bf"] = bld.dram("wu_bf", (cfg.EPC, D, cfg.F), BF16)
    DR["wd_bf"] = bld.dram("wd_bf", (cfg.EPC, cfg.F, D), BF16)
    DR["swg_bf"] = bld.dram("swg_bf", (D, cfg.F), BF16)
    DR["swu_bf"] = bld.dram("swu_bf", (D, cfg.F), BF16)
    DR["swd_bf"] = bld.dram("swd_bf", (cfg.F, D), BF16)
    DR["kta_st"] = bld.dram("kta_st", (cfg.KV * cfg.HD, NT), BF16)
    DR["kta_all"] = bld.dram("kta_all", (2 * cfg.KV * cfg.HD, NT), BF16)
    DR["va_st"] = bld.dram("va_st", (NT, cfg.KV * cfg.HD), BF16)
    DR["va_all"] = bld.dram("va_all", (2 * NT, cfg.KV * cfg.HD), BF16)
    DR["ktb_st"] = bld.dram("ktb_st", (D, NT), BF16)
    DR["ktb_all"] = bld.dram("ktb_all", (2 * D, NT), BF16)
    DR["vb_st"] = bld.dram("vb_st", (NT, D), BF16)
    DR["vb_all"] = bld.dram("vb_all", (2 * NT, D), BF16)
    DR["qtb_spill"] = bld.dram("qtb_spill", (D, NT), BF16)
    DR["xt_spill"] = bld.dram("xt_spill", (D, NT), BF16)
    DR["xn_st"] = bld.dram("xn_st", (NT, D), BF16)
    DR["xn_all"] = bld.dram("xn_all", (cfg.N, D), BF16)
    DR["cnt_st"] = bld.dram("cnt_st", (NSLOT, E), F32)
    DR["cnt_all"] = bld.dram("cnt_all", (cfg.B * cfg.GT, E), F32)
    DR["idxmap"] = bld.dram("idxmap", (E * cfg.C + 128, 1), F32)
    DR["idxmap_red"] = bld.dram("idxmap_red", (E * cfg.C + 128, 1), F32)
    DR["offrow"] = bld.dram("offrow", (1, NSLOT * E), F32)
    DR["myidx"] = bld.dram("myidx", (cfg.MYROWS, 1), F32)
    DR["ei_hbm"] = bld.dram("ei_hbm", (cfg.MYROWS, D), BF16)
    DR["eo_mine"] = bld.dram("eo_mine", (cfg.MYROWS, D), BF16)
    DR["eo_all0"] = bld.dram("eo_all0", (cfg.NCORES * cfg.C, D), BF16)
    DR["eo_all1a"] = bld.dram("eo_all1a", (cfg.NCORES * 1536, D), BF16)
    DR["eo_all1b"] = bld.dram("eo_all1b", (cfg.NCORES * (cfg.C - 1536), D), BF16)

    PB = [bld.psum(f"pb{i}", (128, 512), F32) for i in range(6)]
    PT = bld.psum("pbT", (128, 1024), BF16)
    PT2 = bld.psum("pbT2", (128, 1024), BF16)
    PTB = [PT, PT2]

    sb_ident = bld.sb("sb_ident", (128, 128), BF16)
    sb_nbig = bld.sb("sb_nbig", (128, 128), BF16)
    sb_tstrict = bld.sb("sb_tstrict", (128, 128), BF16)
    sb_maskp = bld.sb("sb_maskp", (128, 2 * 128), BF16)
    sb_maskp4 = bld.sb("sb_maskp4", (128, 2 * 512), BF16)
    sb_tri = bld.sb("sb_tri", (128, 128), BF16)
    sb_iota16 = bld.sb("sb_iota16", (128, E), F32)
    sb_rowvals = bld.sb("sb_rowvals", (128, NSLOT), F32)
    sb_prefm = bld.sb("sb_prefm", (cfg.B * cfg.GT, NSLOT), BF16)
    sb_onesrow = bld.sb("sb_onesrow", (1, 128), F32)
    sb_onescol = bld.sb("sb_onescol", (128, 1), BF16)
    sb_expbias = bld.sb("sb_expbias", (128, 1), F32)
    sb_eps = bld.sb("sb_eps", (128, 1), F32)
    sb_zero1 = bld.sb("sb_zero1", (128, 1), F32)
    scr = bld.sb("scr", (128, 64), F32)
    scr8 = bld.sb("scr8", (128, 8), F32)
    idx8 = bld.sb("idx8", (128, 8), U32)
    if "ffn1" in stages or "ffn2" in stages:
        NGT = cfg.B * cfg.GT
        NE = NSLOT * E
        CT, EPC, Ccap, NF, F = cfg.CT, cfg.EPC, cfg.C, cfg.NF, cfg.F
        RCW = 512 if Ccap % 512 == 0 else 128
        NRC = Ccap // RCW
        TCW = min(512, NT)
        NRW = cfg.MYROWS // 128
        DCW = min(512, D)
        sb_identf = bld.sb("sb_identf", (128, 128), F32)
        rw_sb = bld.sb("rw_sb", (128, ND * E), BF16)
        rb_sb = bld.sb("rb_sb", (128, E), F32)
        myblk_sb = bld.sb("myblk_sb", (NRW, 1), I32)
        lg = bld.sb("lg", (128, NE), F32)
        i01 = bld.sb("i01", (128, 2 * NSLOT), F32)
        rank01 = bld.sb("rank01", (128, 2 * NSLOT), F32)
        el01 = bld.sb("el01", (128, 2 * NSLOT), F32)
        dloc = bld.sb("dloc", (128, 2 * NSLOT), F32)
        dtmp = bld.sb("dtmp", (128, 2 * NSLOT), F32)
        ve0w = bld.sb("ve0w", (128, 2 * NSLOT), F32)
        ve1w = bld.sb("ve1w", (128, 2 * NSLOT), F32)
        ve1aw = bld.sb("ve1aw", (128, 2 * NSLOT), F32)
        ve1bw = bld.sb("ve1bw", (128, 2 * NSLOT), F32)
        dsti1a = bld.sb("dsti1a", (128, 2 * NSLOT), I32)
        dsti1b = bld.sb("dsti1b", (128, 2 * NSLOT), I32)
        hselw = bld.sb("hselw", (128, 2 * NSLOT), F32)
        iota_rk_sb = bld.sb("iota_rk_sb", (128, NSLOT * E), F32)
        iota_el_sb = bld.sb("iota_el_sb", (128, NSLOT * E), F32)
        v01 = bld.sb("v01", (128, 2 * NSLOT), F32)
        ve01 = bld.sb("ve01", (128, 2 * NSLOT), F32)
        pos01 = bld.sb("pos01", (128, 2 * NSLOT), F32)
        dstf = bld.sb("dstf", (128, 2 * NSLOT), F32)
        dsti = bld.sb("dsti", (128, 2 * NSLOT), I32)
        dsti2 = bld.sb("dsti2", (128, 2 * NSLOT), I32)
        gv = bld.sb("gv", (NRW, 128), F32)
        gtmp = bld.sb("gtmp", (NRW, 128), F32)
        gidx = bld.sb("gidx", (128, NRW), I32)
        tot_sb = bld.sb("tot_sb", (1, NE), F32)
        call_sb = bld.sb("call_sb", (NGT, E), F32)
        callb_sb = bld.sb("callb_sb", (NGT, E), BF16)
        offs_sb = bld.sb("offs_sb", (NSLOT, E), F32)
        offr_sb = bld.sb("offr_sb", (1, NE), F32)
        zb = bld.sb("zb", (128, 512), BF16)
        trash_col = bld.sb("trash_col", (128, 1), F32)

    x_res = bld.sb("x_res", (128, NSLOT * D), F32)
    HV = {"x": x_res}

    def alloc_ln_bufs(pfx):
        return (bld.sb(pfx + "xh", (128, NSLOT * D), BF16),
                bld.sb(pfx + "xT", (128, ND * NT), BF16),
                bld.sb(pfx + "xh32", (128, 2 * D), F32),
                bld.sb(pfx + "sq_scr", (128, 2 * D), BF16))

    def R(buf, ps=None, fs=None):
        return buf.r(ps, fs)


    def dma(queue, out_ap, in_ap, reads, writes, dbg="", **kw):
        return em.op(queue, (lambda e, o=out_ap, i=in_ap, kw=kw:
                             e.dma_start(o, i, **kw)),
                     reads=reads, writes=writes, kind="dma", dbg=dbg)

    def mm(out_ap, lhsT_ap, rhs_ap, reads, writes, start, stop, key=None, dbg=""):
        return em.op(PE, (lambda e, o=out_ap, l=lhsT_ap, r=rhs_ap, s=start,
                          p=stop: e.matmul(o, l, r, start=s, stop=p)),
                     reads=reads, writes=writes, accum_key=key, dbg=dbg)

    def transpose(out_ap, in_ap, reads, writes, dbg=""):
        return em.op(PE, (lambda e, o=out_ap, i=in_ap:
                          e.transpose(o, i, sb_ident[:, :])),
                     reads=list(reads) + [sb_ident.r()], writes=writes, dbg=dbg)

    def act(out_ap, in_ap, func, reads, writes, bias=0.0, scale=1.0,
            accum_out=None, dbg=""):
        def f(e, o=out_ap, i=in_ap, fn=func, b=bias, s=scale, a=accum_out):
            kw = {}
            if a is not None:
                kw["accum_out"] = a
            return e.activation(o, i, fn, bias=b, scale=s, **kw)
        return em.op(ACT, f, reads=reads, writes=writes, dbg=dbg)

    def vop(method, *args, reads=(), writes=(), dbg="", eng=DVE, **kw):
        def f(e, m=method, a=args, kw=kw):
            return getattr(e, m)(*a, **kw)
        return em.op(eng, f, reads=reads, writes=writes, dbg=dbg)


    dma(SP, x_res[:, :], x_in.h.rearrange("(s p) d -> p s d", p=128),
        [x_in.r()], [x_res.r()], dbg="load_x")
    for n, cc in [("ident", sb_ident), ("negbig_ident", sb_nbig),
                  ("tstrict", sb_tstrict), ("iota16", sb_iota16),
                  ("rowvals", sb_rowvals), ("prefmask", sb_prefm),
                  ("tri", sb_tri)]:
        dma(SP, cc[:, :], CN[n].h, [CN[n].r()], [cc.r()], dbg=f"load_{n}")
    dma(SP, sb_maskp[:, :], CN["maskpair"].h.rearrange("a p c -> p a c"),
        [CN["maskpair"].r()], [sb_maskp.r()], dbg="load_maskp")
    dma(SP, sb_maskp4[:, :], CN["maskpair4"].h.rearrange("a p c -> p a c"),
        [CN["maskpair4"].r()], [sb_maskp4.r()], dbg="load_maskp4")
    vop("memset", sb_onesrow[:, :], 1.0, writes=[sb_onesrow.r()])
    vop("memset", sb_onescol[:, :], 1.0, writes=[sb_onescol.r()])
    vop("memset", sb_expbias[:, :], -EXP_SHIFT, writes=[sb_expbias.r()])
    vop("memset", sb_eps[:, :], 1e-5, writes=[sb_eps.r()])
    vop("memset", sb_zero1[:, :], 0.0, writes=[sb_zero1.r()])

    def conv(dst, src, n_split=1, dbg=""):
        rows = src.shape[0]
        step = rows // n_split
        for i in range(n_split):
            sl = slice(i * step, (i + 1) * step)
            dma(POOL, dst.h[sl], src.h[sl], [src.r(ps=(i * step, (i + 1) * step))],
                [dst.r(ps=(i * step, (i + 1) * step))], dbg=f"conv_{dbg}_{i}")

    def emit_weight_convs():
        if "attn_b" in stages:
            for s_, d_ in [("wqb", "wqb_bf"), ("wkb", "wkb_bf"),
                           ("wvb", "wvb_bf"), ("wob", "wob_bf")]:
                conv(DR[d_], W[s_], n_split=1, dbg=s_)
        if "ffn1" in stages or "ffn2" in stages:
            em.op(POOL, (lambda e: e.iota(bld.bufs["trash_col"][:, :], [[0, 1]],
                                          base=cfg.E * cfg.C, channel_multiplier=1,
                                          allow_small_or_imprecise_dtypes=True)),
                  writes=[bld.bufs["trash_col"].r()], dbg="trashcol")
            for s_, d_ in [("swg", "swg_bf"), ("swu", "swu_bf"), ("swd", "swd_bf")]:
                conv(DR[d_], W[s_], n_split=1, dbg=s_)
            for s_, d_ in [("mywg", "wg_bf"), ("mywu", "wu_bf"), ("mywd", "wd_bf")]:
                conv(DR[d_], W[s_], n_split=cfg.EPC, dbg=s_)
            zb = bld.bufs["zb"]
            vop("memset", zb[:, :], 0.0, writes=[zb.r()])
            nz = (cfg.MYROWS * D) // (128 * 512)
            for i in range(nz):
                dma(POOL, DR["ei_hbm"].h.rearrange("r d -> (r d)")
                    .rearrange("(n p c) -> n p c", n=nz, p=128)[i],
                    zb[:, :], [zb.r()], [("ei_hbm", (0, cfg.MYROWS, i, i + 1))],
                    dbg=f"zei{i}")
    if "attn_a" not in stages or CUT <= 1:
        emit_weight_convs()


    def layernorm(src_buf, s, dst32_ap, dst32_w, dstbf_ap, dstbf_w, rot, dbg,
                  sq_scr=None):
        xr = [src_buf.r(fs=(s * D, (s + 1) * D))]
        c0 = rot * 8
        sums = scr[:, c0:c0 + 1]
        sumsq = scr[:, c0 + 1:c0 + 2]
        mean = scr[:, c0 + 2:c0 + 3]
        var = scr[:, c0 + 3:c0 + 4]
        rstd = scr[:, c0 + 4:c0 + 5]
        nmr = scr[:, c0 + 5:c0 + 6]
        sc = lambda i: ("scr", (0, 128, c0 + i, c0 + i + 1))
        vop("reduce_sum", sums, src_buf[:, s * D:(s + 1) * D], axis=AX.X,
            reads=xr, writes=[sc(0)], dbg=f"{dbg}_sum")
        act(sq_scr[:, rot * D:(rot + 1) * D], src_buf[:, s * D:(s + 1) * D],
            AF.Square, xr, [sq_scr.r(fs=(rot * D, (rot + 1) * D)), sc(1)],
            accum_out=sumsq, dbg=f"{dbg}_sq")
        del sq_scr
        m2 = scr[:, c0 + 6:c0 + 7]
        vop("tensor_scalar_mul", mean, sums, 1.0 / D,
            reads=[sc(0)], writes=[sc(2)], dbg=f"{dbg}_mean")
        vop("tensor_tensor", m2, mean, mean, ALU.mult,
            reads=[sc(2)], writes=[sc(6)], dbg=f"{dbg}_m2")
        vop("scalar_tensor_tensor", var, sumsq, 1.0 / D, m2, ALU.mult,
            ALU.subtract, reads=[sc(1), sc(6)], writes=[sc(3)], dbg=f"{dbg}_var")
        act(rstd, var, AF.Sqrt, [sc(3), ("sb_eps", (0, 128, 0, 1))], [sc(4)], bias=sb_eps[:, 0:1], scale=1.0,
            dbg=f"{dbg}_sd")
        vop("reciprocal", rstd, rstd, reads=[sc(4)], writes=[sc(4)],
            dbg=f"{dbg}_r")
        vop("scalar_tensor_tensor", nmr, mean, -1.0, rstd, ALU.mult, ALU.mult,
            reads=[sc(2), sc(4)], writes=[sc(5)], dbg=f"{dbg}_nmr")
        if dst32_ap is not None:
            act(dst32_ap, src_buf[:, s * D:(s + 1) * D], AF.Identity,
                xr + [sc(4), sc(5)], dst32_w, bias=nmr, scale=rstd,
                dbg=f"{dbg}_n32")
        act(dstbf_ap, src_buf[:, s * D:(s + 1) * D], AF.Identity,
            xr + [sc(4), sc(5)], dstbf_w, bias=nmr, scale=rstd,
            dbg=f"{dbg}_nbf")

    HA, KV, HD, DLAT = cfg.HA, cfg.KV, cfg.HD, cfg.DLAT
    TCH = min(512, NT)
    NTCH = NT // TCH
    RG_PAIRS = [[2 * i, 2 * i + 1] for i in range(cfg.B)]
    RG_ALL = [list(range(cfg.NCORES))]

    if "attn_a" in stages:
        woa_sb = bld.sb("woa_sb", (128, (HA * HD // 128) * D), BF16)
        qTa = bld.sb("qTa", (128, (HA * HD // 128) * NT), BF16)
        Ya = bld.sb("Ya", (128, 2 * HA * HD), BF16)
        ptb = [bld.sb(f"pta{i}", (128, 512), BF16) for i in range(3)]
        yT = bld.sb("yTa", (128, (HA * HD // 128) * 128), BF16)
        stg = bld.sb("stg_a", (128, 2 * 512), BF16)
        if LOAD_WOA:
            dma(POOL, woa_sb[:, :], W["woa"].h.rearrange("(t p) c -> p t c", p=128),
                [W["woa"].r()], [woa_sb.r()], dbg="ld_woa")
        xT = bld.sb("a_xT", (128, ND * NT), BF16)
        xh = bld.sb("a_xh", (128, 2 * D), BF16)
        xh32 = bld.sb("a_xh32", (128, 2 * D), F32)
        sq_scr = bld.sb("a_sq", (128, 2 * D), BF16)
        for s in range(min(NSLOT, LN_SLOTS)):
            rot = s % 2
            layernorm(x_res, s, None, None, xh[:, rot * D:(rot + 1) * D],
                      [xh.r(fs=(rot * D, (rot + 1) * D))], rot=rot,
                      dbg=f"ln1_{s}", sq_scr=sq_scr)
            for dt in range(ND if DO_TR else 0):
                po = (dt % 8) * 128
                PTX = PTB[dt % 2]
                transpose(PTX[:, po:po + 128],
                          xh[:, rot * D + dt * 128:rot * D + (dt + 1) * 128],
                          [xh.r(fs=(rot * D + dt * 128, rot * D + (dt + 1) * 128))],
                          [PTX.r(fs=(po, po + 128))], dbg=f"xt_{s}_{dt}")
                vop("tensor_copy", xT[:, dt * NT + s * 128:dt * NT + (s + 1) * 128],
                    PTX[:, po:po + 128],
                    reads=[PTX.r(fs=(po, po + 128))],
                    writes=[xT.r(fs=(dt * NT + s * 128, dt * NT + (s + 1) * 128))],
                    dbg=f"xte_{s}_{dt}")
        em.barrier()
        bld.free(sq_scr, xh32, xh)
        if CUT > 1:
            wqa_sb = bld.sb("wqa_sb", (128, 2 * HA * HD), BF16)
            wkda_sb = bld.sb("wkda_sb", (128, ND * KV * DLAT), BF16)
            wvda_sb = bld.sb("wvda_sb", (128, ND * KV * DLAT), BF16)
            wkua_sb = bld.sb("wkua_sb", (KV * DLAT, HD), BF16)
            wvua_sb = bld.sb("wvua_sb", (KV * DLAT, HD), BF16)
            klatT = bld.sb("klatT", (KV * DLAT, NT), BF16)
            vlatT = bld.sb("vlatT", (KV * DLAT, NT), BF16)
            dma(POOL, wkda_sb[:, :], W["wkda"].h.rearrange("(t p) c -> p t c", p=128),
                [W["wkda"].r()], [wkda_sb.r()], dbg="ld_wkda")
            dma(POOL, wvda_sb[:, :], W["wvda"].h.rearrange("(t p) c -> p t c", p=128),
                [W["wvda"].r()], [wvda_sb.r()], dbg="ld_wvda")
            for kv in range(KV):
                dma(POOL, wkua_sb[kv * DLAT:(kv + 1) * DLAT, :], W["wkua"].h,
                    [W["wkua"].r()], [wkua_sb.r(ps=(kv * DLAT, (kv + 1) * DLAT))],
                    dbg=f"ld_wkua{kv}")
                dma(POOL, wvua_sb[kv * DLAT:(kv + 1) * DLAT, :], W["wvua"].h,
                    [W["wvua"].r()], [wvua_sb.r(ps=(kv * DLAT, (kv + 1) * DLAT))],
                    dbg=f"ld_wvua{kv}")
            for nm, wsb, dst in [("kl", wkda_sb, klatT), ("vl", wvda_sb, vlatT)]:
                for tc in range(NTCH):
                    pb = PB[2 + tc % 2]
                    key = (nm, tc)
                    for dt in range(ND):
                        mm(pb[0:KV * DLAT, 0:TCH],
                           wsb[:, dt * KV * DLAT:(dt + 1) * KV * DLAT],
                           xT[:, dt * NT + tc * TCH:dt * NT + (tc + 1) * TCH],
                           [wsb.r(fs=(dt * KV * DLAT, (dt + 1) * KV * DLAT)),
                            xT.r(fs=(dt * NT + tc * TCH, dt * NT + (tc + 1) * TCH))],
                           [pb.r(ps=(0, KV * DLAT), fs=(0, TCH))],
                           start=(dt == 0), stop=(dt == ND - 1), key=key,
                           dbg=f"{nm}_{tc}_{dt}")
                    act(dst[:, tc * TCH:(tc + 1) * TCH], pb[0:KV * DLAT, 0:TCH],
                        AF.Copy, [pb.r(ps=(0, KV * DLAT), fs=(0, TCH))],
                        [dst.r(fs=(tc * TCH, (tc + 1) * TCH))], dbg=f"{nm}e_{tc}")
            for kv in range(KV):
                for tc in range(NTCH):
                    pb = PB[4 + tc % 2]
                    mm(pb[:, 0:TCH], wkua_sb[kv * DLAT:(kv + 1) * DLAT, :],
                       klatT[kv * DLAT:(kv + 1) * DLAT, tc * TCH:(tc + 1) * TCH],
                       [wkua_sb.r(ps=(kv * DLAT, (kv + 1) * DLAT)),
                        klatT.r(ps=(kv * DLAT, (kv + 1) * DLAT),
                                fs=(tc * TCH, (tc + 1) * TCH))],
                       [pb.r(fs=(0, TCH))], start=True, stop=True,
                       dbg=f"kta_{kv}_{tc}")
                    sl = (kv * NTCH + tc) % 2
                    act(stg[:, sl * 512:sl * 512 + TCH], pb[:, 0:TCH], AF.Copy,
                        [pb.r(fs=(0, TCH))], [stg.r(fs=(sl * 512, sl * 512 + TCH))],
                        dbg=f"ktae_{kv}_{tc}")
                    dma(SP, DR["kta_st"].h[kv * HD:(kv + 1) * HD,
                                           tc * TCH:(tc + 1) * TCH],
                        stg[:, sl * 512:sl * 512 + TCH],
                        [stg.r(fs=(sl * 512, sl * 512 + TCH))],
                        [DR["kta_st"].r(ps=(kv * HD, (kv + 1) * HD),
                                        fs=(tc * TCH, (tc + 1) * TCH))],
                        dbg=f"st_kta{kv}_{tc}")
            for s in range(NSLOT):
                for kv in range(KV):
                    pb = PB[0]
                    mm(pb[:, 0:HD],
                       vlatT[kv * DLAT:(kv + 1) * DLAT, s * 128:(s + 1) * 128],
                       wvua_sb[kv * DLAT:(kv + 1) * DLAT, :],
                       [vlatT.r(ps=(kv * DLAT, (kv + 1) * DLAT),
                                fs=(s * 128, (s + 1) * 128)),
                        wvua_sb.r(ps=(kv * DLAT, (kv + 1) * DLAT))],
                       [pb.r(fs=(0, HD))], start=True, stop=True,
                       dbg=f"va_{s}_{kv}")
                    sl = (s * KV + kv) % 2
                    act(stg[:, sl * 512:sl * 512 + HD], pb[:, 0:HD], AF.Copy,
                        [pb.r(fs=(0, HD))], [stg.r(fs=(sl * 512, sl * 512 + HD))],
                        dbg=f"vae_{s}_{kv}")
                    dma(SP, DR["va_st"].h[s * 128:(s + 1) * 128,
                                          kv * HD:(kv + 1) * HD],
                        stg[:, sl * 512:sl * 512 + HD],
                        [stg.r(fs=(sl * 512, sl * 512 + HD))],
                        [DR["va_st"].r(ps=(s * 128, (s + 1) * 128),
                                       fs=(kv * HD, (kv + 1) * HD))],
                        dbg=f"st_va{s}_{kv}")
            for dh_t in range(HA * HD // 128):
                wslot = dh_t % 2
                dma(POOL, wqa_sb[:, wslot * HA * HD:wslot * HA * HD + ND * 128],
                    W["wqa"].h.rearrange("(t p) c -> p t c", p=128)
                    [:, :, dh_t * 128:(dh_t + 1) * 128],
                    [W["wqa"].r(fs=(dh_t * 128, (dh_t + 1) * 128))],
                    [wqa_sb.r(fs=(wslot * HA * HD, wslot * HA * HD + ND * 128))],
                    dbg=f"ld_wqa{dh_t}")
                for tc in range(NTCH):
                    pb = PB[(dh_t * NTCH + tc) % 2]
                    key = ("qa", dh_t, tc)
                    for dt in range(ND):
                        wo_ = wslot * HA * HD + dt * 128
                        mm(pb[:, 0:TCH], wqa_sb[:, wo_:wo_ + 128],
                           xT[:, dt * NT + tc * TCH:dt * NT + (tc + 1) * TCH],
                           [wqa_sb.r(fs=(wo_, wo_ + 128)),
                            xT.r(fs=(dt * NT + tc * TCH, dt * NT + (tc + 1) * TCH))],
                           [pb.r(fs=(0, TCH))], start=(dt == 0), stop=(dt == ND - 1),
                           key=key, dbg=f"qa_{dh_t}_{tc}_{dt}")
                    kv_, r_ = dh_t // 4, dh_t % 4
                    qv = qTa.h[:, :].rearrange("p (kv s r t) -> p kv s r t",
                                               kv=KV, s=NSLOT, r=4)
                    act(qv[:, kv_, 4 * tc:4 * tc + 4, r_, :],
                        pb[:, 0:TCH], AF.Copy, [pb.r(fs=(0, TCH))],
                        [qTa.r(fs=(kv_ * NSLOT * 512 + 4 * tc * 512,
                                   kv_ * NSLOT * 512 + (4 * tc + 4) * 512))],
                        dbg=f"qae_{dh_t}_{tc}")
            em.barrier()
            bld.free(vlatT, klatT, wvua_sb, wkua_sb, wvda_sb, wkda_sb, wqa_sb, xT)
        if CUT > 2:
            em.op(POOL, (lambda e: e.collective_compute(
                "AllGather", ALU.bypass, RG_PAIRS,
                ins=[DR["kta_st"].h[:, :]], outs=[DR["kta_all"].h[:, :]])),
                reads=[DR["kta_st"].r()], writes=[DR["kta_all"].r()], kind="coll",
                dbg="ag_ka")
            em.op(POOL, (lambda e: e.collective_compute(
                "AllGather", ALU.bypass, RG_PAIRS,
                ins=[DR["va_st"].h[:, :]], outs=[DR["va_all"].h[:, :]])),
                reads=[DR["va_st"].r()], writes=[DR["va_all"].r()], kind="coll",
                dbg="ag_va")
            VAW = KV * (HD + 1)
            kta_sb = bld.sb("kta_sb", (128, KV * 2 * NT), BF16)
            va_sb = bld.sb("va_sb", (128, 2 * NSLOT * VAW), BF16)
            for kv in range(KV):
                for rank in range(2):
                    dma(SP, kta_sb[:, kv * 2 * NT + rank * NT:kv * 2 * NT + (rank + 1) * NT],
                        DR["kta_all"].h[rank * KV * HD + kv * HD:rank * KV * HD + (kv + 1) * HD, :],
                        [DR["kta_all"].r(ps=(rank * KV * HD + kv * HD,
                                             rank * KV * HD + (kv + 1) * HD))],
                        [kta_sb.r(fs=(kv * 2 * NT + rank * NT, kv * 2 * NT + (rank + 1) * NT))],
                        dbg=f"ld_kta_{kv}_{rank}")
            for jb in range(2 * NSLOT):
                rank, sl = jb // NSLOT, jb % NSLOT
                for kv in range(KV):
                    dma(SP, va_sb[:, jb * VAW + kv * (HD + 1):jb * VAW + kv * (HD + 1) + HD],
                        DR["va_all"].h[rank * NT + sl * 128:rank * NT + (sl + 1) * 128,
                                       kv * HD:(kv + 1) * HD],
                        [DR["va_all"].r(ps=(rank * NT + sl * 128, rank * NT + (sl + 1) * 128))],
                        [va_sb.r(fs=(jb * VAW + kv * (HD + 1), jb * VAW + kv * (HD + 1) + HD))],
                        dbg=f"ld_va_{jb}_{kv}")
                    vop("memset",
                        va_sb[:, jb * VAW + kv * (HD + 1) + HD:jb * VAW + (kv + 1) * (HD + 1)],
                        1.0,
                        writes=[va_sb.r(fs=(jb * VAW + kv * (HD + 1) + HD,
                                            jb * VAW + (kv + 1) * (HD + 1)))],
                        dbg=f"vone_{jb}_{kv}")
        if CUT > 3:
            scale_a = 1.0 / math.sqrt(HD)
            HW = HD + 1
            for s in range(NSLOT):
                yb = s % 2
                for kv in range(KV):
                    njt = 2 * s + 2
                    qoff = kv * NSLOT * 512 + s * 512
                    for j in range(njt + 1):
                        if j < njt:
                            sc_b = PB[2 + j % 2]
                            kcol = kv * 2 * NT + (j % 2) * NT + (j // 2) * 128
                            mm(sc_b[:, 0:512], kta_sb[:, kcol:kcol + 128],
                               qTa[:, qoff:qoff + 512],
                               [kta_sb.r(fs=(kcol, kcol + 128)),
                                qTa.r(fs=(qoff, qoff + 512))],
                               [sc_b.r(fs=(0, 512))], start=True,
                               stop=(j < 2 * s), dbg=f"sca_{s}_{kv}_{j}")
                            if j >= 2 * s:
                                jrel = j - 2 * s
                                mm(sc_b[:, 0:512], sb_nbig[:, :],
                                   sb_maskp4[:, jrel * 512:(jrel + 1) * 512],
                                   [sb_nbig.r(),
                                    sb_maskp4.r(fs=(jrel * 512, (jrel + 1) * 512))],
                                   [sc_b.r(fs=(0, 512))], start=False, stop=True,
                                   dbg=f"ma_{s}_{kv}_{j}")
                            pt = ptb[j % 3]
                            act(pt[:, :], sc_b[:, 0:512], AF.Exp,
                                [sc_b.r(fs=(0, 512)), sb_expbias.r()], [pt.r()],
                                bias=sb_expbias[:, 0:1], scale=scale_a,
                                dbg=f"expa_{s}_{kv}_{j}")
                        if j >= 1:
                            jj = j - 1
                            jb = (jj % 2) * NSLOT + jj // 2
                            pt = ptb[jj % 3]
                            for r in range(4):
                                av = PB[4 + r // 2]
                                ao = (r % 2) * HW
                                mm(av[:, ao:ao + HW], pt[:, r * 128:(r + 1) * 128],
                                   va_sb[:, jb * VAW + kv * HW:jb * VAW + (kv + 1) * HW],
                                   [pt.r(fs=(r * 128, (r + 1) * 128)),
                                    va_sb.r(fs=(jb * VAW + kv * HW,
                                                jb * VAW + (kv + 1) * HW))],
                                   [av.r(fs=(ao, ao + HW))],
                                   start=(jj == 0 and r % 2 == 0),
                                   stop=(jj == njt - 1 and r % 2 == 1),
                                   key=("ava", s, kv, r // 2),
                                   dbg=f"av_{s}_{kv}_{jj}_{r}")
                    for r in range(4):
                        h = kv * 4 + r
                        av = PB[4 + r // 2]
                        ao = (r % 2) * HW
                        rc = scr8[:, h % 8:h % 8 + 1]
                        rcr = ("scr8", (0, 128, h % 8, h % 8 + 1))
                        vop("reciprocal", rc, av[:, ao + HD:ao + HD + 1],
                            reads=[av.r(fs=(ao + HD, ao + HD + 1))], writes=[rcr],
                            dbg=f"rca_{s}_{h}")
                        act(Ya[:, yb * HA * HD + h * HD:yb * HA * HD + (h + 1) * HD],
                            av[:, ao:ao + HD], AF.Copy,
                            [av.r(fs=(ao, ao + HD)), rcr],
                            [Ya.r(fs=(yb * HA * HD + h * HD, yb * HA * HD + (h + 1) * HD))],
                            scale=rc, dbg=f"ye_{s}_{h}")
                for dh_t in range(HA * HD // 128):
                    po = (dh_t % 8) * 128
                    PTX = PTB[dh_t % 2]
                    transpose(PTX[:, po:po + 128],
                              Ya[:, yb * HA * HD + dh_t * 128:yb * HA * HD + (dh_t + 1) * 128],
                              [Ya.r(fs=(yb * HA * HD + dh_t * 128, yb * HA * HD + (dh_t + 1) * 128))],
                              [PTX.r(fs=(po, po + 128))], dbg=f"ytr_{s}_{dh_t}")
                    vop("tensor_copy", yT[:, dh_t * 128:(dh_t + 1) * 128],
                        PTX[:, po:po + 128], reads=[PTX.r(fs=(po, po + 128))],
                        writes=[yT.r(fs=(dh_t * 128, (dh_t + 1) * 128))],
                        dbg=f"ytre_{s}_{dh_t}")
                DCW = min(512, D)
                for dc in range(D // DCW):
                    pb = PB[dc % 2]
                    key = ("oa", s, dc)
                    for dh_t in range(HA * HD // 128):
                        mm(pb[:, 0:DCW], yT[:, dh_t * 128:(dh_t + 1) * 128],
                           woa_sb[:, dh_t * D + dc * DCW:dh_t * D + (dc + 1) * DCW],
                           [yT.r(fs=(dh_t * 128, (dh_t + 1) * 128)),
                            woa_sb.r(fs=(dh_t * D + dc * DCW, dh_t * D + (dc + 1) * DCW))],
                           [pb.r(fs=(0, DCW))], start=(dh_t == 0),
                           stop=(dh_t == HA * HD // 128 - 1), key=key,
                           dbg=f"oa_{s}_{dc}")
                    vop("tensor_tensor",
                        x_res[:, s * D + dc * DCW:s * D + (dc + 1) * DCW],
                        x_res[:, s * D + dc * DCW:s * D + (dc + 1) * DCW],
                        pb[:, 0:DCW], ALU.add,
                        reads=[x_res.r(fs=(s * D + dc * DCW, s * D + (dc + 1) * DCW)),
                               pb.r(fs=(0, DCW))],
                        writes=[x_res.r(fs=(s * D + dc * DCW, s * D + (dc + 1) * DCW))],
                        dbg=f"oadd_{s}_{dc}")
        if CUT > 1:
            emit_weight_convs()
        em.barrier()
        if CUT > 2:
            bld.free(va_sb, kta_sb)
        if CUT <= 1:
            bld.free(xT)
        bld.free(stg, yT, *reversed(ptb))
        bld.free(Ya, qTa, woa_sb)

    if "ffn1" in stages or "ffn2" in stages:
        NGT = cfg.B * cfg.GT
        NE = NSLOT * E
        CT, EPC, Ccap, NF, F = cfg.CT, cfg.EPC, cfg.C, cfg.NF, cfg.F
        RCW = 512 if Ccap % 512 == 0 else 128
        NRC = Ccap // RCW
        TCW = min(512, NT)
        NRW = cfg.MYROWS // 128
        DCW = min(512, D)
        dma(SP, sb_identf[:, :], CN["identf"].h, [CN["identf"].r()],
            [sb_identf.r()], dbg="ld_identf")
        dma(POOL, rw_sb[:, :], W["rw"].h.rearrange("(t p) c -> p t c", p=128),
            [W["rw"].r()], [rw_sb.r()], dbg="ld_rw")
        dma(SP, rb_sb[:, :], W["rb_bc"].h, [W["rb_bc"].r()], [rb_sb.r()],
            dbg="ld_rb")
        dma(SP, myblk_sb[:, :], W["myblk"].h, [W["myblk"].r()], [myblk_sb.r()],
            dbg="ld_myblk")
        dma(SP, iota_rk_sb[:, :], CN["iota_rkE"].h, [CN["iota_rkE"].r()],
            [iota_rk_sb.r()], dbg="ld_iork")
        dma(SP, iota_el_sb[:, :], CN["iota_elE"].h, [CN["iota_elE"].r()],
            [iota_el_sb.r()], dbg="ld_ioel")

    def emit_ffn(inst):
        t = f"F{inst}_"
        x_res = HV["x"]
        gu_chunk = bld.sb(t + "gu", (128, NF * 512), BF16)
        eo_sb = bld.sb(t + "eo_sb", (128, 2 * D), BF16)
        ret0 = bld.sb(t + "ret0", (128, D), BF16)
        ret1 = bld.sb(t + "ret1", (128, D), BF16)
        ret2 = bld.sb(t + "ret2", (128, D), BF16)
        ret3 = bld.sb(t + "ret3", (128, D), BF16)
        ret4 = bld.sb(t + "ret4", (128, D), BF16)
        ret5 = bld.sb(t + "ret5", (128, D), BF16)
        retf = bld.sb(t + "retf", (128, D), F32)
        zf = bld.sb(t + "zf", (128, 512), F32)
        vop("memset", zf[:, :], 0.0, writes=[zf.r()], dbg=f"{t}zf")
        xT = bld.sb(t + "xT", (128, ND * NT), BF16)
        xh = bld.sb(t + "xh", (128, 2 * D), BF16)
        xh32 = bld.sb(t + "xh32", (128, 2 * D), F32)
        sq_scr = bld.sb(t + "sq", (128, 2 * D), BF16)
        for s in range(NSLOT):
            rot = s % 2
            layernorm(x_res, s, xh32[:, rot * D:(rot + 1) * D],
                      [xh32.r(fs=(rot * D, (rot + 1) * D))],
                      xh[:, rot * D:(rot + 1) * D],
                      [xh.r(fs=(rot * D, (rot + 1) * D))], rot=rot,
                      dbg=f"{t}ln{s}", sq_scr=sq_scr)
            vop("tensor_tensor", x_res[:, s * D:(s + 1) * D],
                x_res[:, s * D:(s + 1) * D], xh32[:, rot * D:(rot + 1) * D],
                ALU.add,
                reads=[x_res.r(fs=(s * D, (s + 1) * D)),
                       xh32.r(fs=(rot * D, (rot + 1) * D))],
                writes=[x_res.r(fs=(s * D, (s + 1) * D))], dbg=f"{t}lnadd{s}")
            dma(SP, DR["xn_st"].h[s * 128:(s + 1) * 128, :],
                xh[:, rot * D:(rot + 1) * D],
                [xh.r(fs=(rot * D, (rot + 1) * D))],
                [DR["xn_st"].r(ps=(s * 128, (s + 1) * 128))], dbg=f"{t}st_xn{s}")
            for dt in range(ND):
                po = (dt % 8) * 128
                PTX = PTB[dt % 2]
                transpose(PTX[:, po:po + 128],
                          xh[:, rot * D + dt * 128:rot * D + (dt + 1) * 128],
                          [xh.r(fs=(rot * D + dt * 128, rot * D + (dt + 1) * 128))],
                          [PTX.r(fs=(po, po + 128))], dbg=f"{t}xt{s}_{dt}")
                vop("tensor_copy",
                    xT[:, dt * NT + s * 128:dt * NT + (s + 1) * 128],
                    PTX[:, po:po + 128], reads=[PTX.r(fs=(po, po + 128))],
                    writes=[xT.r(fs=(dt * NT + s * 128, dt * NT + (s + 1) * 128))],
                    dbg=f"{t}xte{s}_{dt}")
        em.op(POOL, (lambda e: e.collective_compute(
            "AllGather", ALU.bypass, RG_ALL,
            ins=[DR["xn_st"].h[:, :]], outs=[DR["xn_all"].h[:, :]])),
            reads=[DR["xn_st"].r()], writes=[DR["xn_all"].r()], kind="coll",
            dbg=f"{t}ag_xn")
        em.barrier()
        bld.free(sq_scr, xh32, xh)
        for s in range(NSLOT):
            pb = PB[s % 2]
            key = (t + "rt", s)
            for dt in range(ND):
                mm(pb[:, 0:E], xT[:, dt * NT + s * 128:dt * NT + (s + 1) * 128],
                   rw_sb[:, dt * E:(dt + 1) * E],
                   [xT.r(fs=(dt * NT + s * 128, dt * NT + (s + 1) * 128)),
                    rw_sb.r(fs=(dt * E, (dt + 1) * E))],
                   [pb.r(fs=(0, E))], start=(dt == 0), stop=(dt == ND - 1),
                   key=key, dbg=f"{t}rt{s}_{dt}")
            vop("tensor_tensor", lg[:, s * E:(s + 1) * E], pb[:, 0:E],
                rb_sb[:, :], ALU.add,
                reads=[pb.r(fs=(0, E)), rb_sb.r()],
                writes=[lg.r(fs=(s * E, (s + 1) * E))], dbg=f"{t}rte{s}")
        pr = bld.sb(t + "pr", (128, NE), F32)
        pr2 = bld.sb(t + "pr2", (128, NE), F32)
        O0 = bld.sb(t + "O0", (128, NE), F32)
        O1 = bld.sb(t + "O1", (128, NE), F32)
        s01b = bld.sb(t + "s01b", (128, NE), BF16)
        carrys = bld.sb(t + "carrys", (128, NE), F32)
        sumall = bld.sb(t + "sumall", (128, NE), F32)
        sumall1 = bld.sb(t + "sumall1", (128, NE), F32)
        tmpsel = bld.sb(t + "tmpsel", (128, NE), F32)
        for s in range(NSLOT):
            lgs = lg[:, s * E:(s + 1) * E]
            lgr = lg.r(fs=(s * E, (s + 1) * E))
            prs = pr[:, s * E:(s + 1) * E]
            prr = pr.r(fs=(s * E, (s + 1) * E))
            pr2s = pr2[:, s * E:(s + 1) * E]
            pr2r = pr2.r(fs=(s * E, (s + 1) * E))
            cN = lambda i: ("scr", (0, 128, 16 + i, 16 + i + 1))
            cA = lambda i: scr[:, 16 + i:16 + i + 1]
            vop("reduce_max", cA(0), lgs, axis=AX.X, negate=True,
                reads=[lgr], writes=[cN(0)], dbg=f"{t}sm{s}_m")
            act(prs, lgs, AF.Exp, [lgr, cN(0)], [prr, cN(1)],
                bias=cA(0), scale=1.0, accum_out=cA(1), dbg=f"{t}sm{s}_e")
            vop("reciprocal", cA(2), cA(1), reads=[cN(1)], writes=[cN(2)],
                dbg=f"{t}sm{s}_rz")
            vop("tensor_scalar_mul", prs, prs, cA(2),
                reads=[prr, cN(2)], writes=[prr], dbg=f"{t}sm{s}_p")
            vop("reduce_max", cA(3), prs, axis=AX.X,
                reads=[prr], writes=[cN(3)], dbg=f"{t}sm{s}_p1")
            vop("memset", scr8[:, 0:8], -1.0, writes=[scr8.r(fs=(0, 8))],
                dbg=f"{t}sm{s}_ms")
            vop("tensor_copy", scr8[:, 0:1], cA(3),
                reads=[cN(3)], writes=[scr8.r(fs=(0, 1))], dbg=f"{t}sm{s}_c1")
            vop("max_index", idx8[:, :], scr8[:, 0:8], prs,
                reads=[scr8.r(fs=(0, 8)), prr], writes=[idx8.r()],
                dbg=f"{t}sm{s}_i1")
            vop("tensor_copy", i01[:, s:s + 1], idx8[:, 0:1],
                reads=[idx8.r(fs=(0, 1))], writes=[i01.r(fs=(s, s + 1))],
                dbg=f"{t}sm{s}_i1c")
            vop("match_replace", pr2s, scr8[:, 0:8], prs, -1.0,
                reads=[scr8.r(fs=(0, 8)), prr], writes=[pr2r],
                dbg=f"{t}sm{s}_mr")
            vop("reduce_max", cA(4), pr2s, axis=AX.X,
                reads=[pr2r], writes=[cN(4)], dbg=f"{t}sm{s}_p2")
            vop("memset", scr8[:, 0:8], -1.0, writes=[scr8.r(fs=(0, 8))],
                dbg=f"{t}sm{s}_ms2")
            vop("tensor_copy", scr8[:, 0:1], cA(4),
                reads=[cN(4)], writes=[scr8.r(fs=(0, 1))], dbg=f"{t}sm{s}_c2")
            vop("max_index", idx8[:, :], scr8[:, 0:8], pr2s,
                reads=[scr8.r(fs=(0, 8)), pr2r], writes=[idx8.r()],
                dbg=f"{t}sm{s}_i2")
            vop("tensor_copy", i01[:, NSLOT + s:NSLOT + s + 1], idx8[:, 0:1],
                reads=[idx8.r(fs=(0, 1))],
                writes=[i01.r(fs=(NSLOT + s, NSLOT + s + 1))],
                dbg=f"{t}sm{s}_i2c")
            vop("tensor_tensor", cA(5), cA(3), cA(4), ALU.add,
                reads=[cN(3), cN(4)], writes=[cN(5)], dbg=f"{t}sm{s}_s")
            vop("tensor_scalar_add", cA(5), cA(5), 1e-9,
                reads=[cN(5)], writes=[cN(5)], dbg=f"{t}sm{s}_se")
            vop("reciprocal", cA(6), cA(5), reads=[cN(5)], writes=[cN(6)],
                dbg=f"{t}sm{s}_sr")
            vop("tensor_tensor", v01[:, s:s + 1], cA(3), cA(6), ALU.mult,
                reads=[cN(3), cN(6)], writes=[v01.r(fs=(s, s + 1))],
                dbg=f"{t}sm{s}_v1")
            vop("tensor_tensor", v01[:, NSLOT + s:NSLOT + s + 1], cA(4), cA(6),
                ALU.mult, reads=[cN(4), cN(6)],
                writes=[v01.r(fs=(NSLOT + s, NSLOT + s + 1))],
                dbg=f"{t}sm{s}_v2")
        for s in range(NSLOT):
            for k, Ok in ((0, O0), (1, O1)):
                vop("tensor_scalar", Ok[:, s * E:(s + 1) * E], sb_iota16[:, :],
                    i01[:, k * NSLOT + s:k * NSLOT + s + 1], None, ALU.is_equal,
                    reads=[sb_iota16.r(),
                           i01.r(fs=(k * NSLOT + s, k * NSLOT + s + 1))],
                    writes=[Ok.r(fs=(s * E, (s + 1) * E))], dbg=f"{t}oh{s}_{k}")
        vop("tensor_tensor", tmpsel[:, :], O0[:, :], O1[:, :], ALU.add,
            reads=[O0.r(), O1.r()], writes=[tmpsel.r()], dbg=f"{t}s01")
        vop("tensor_copy", s01b[:, :], tmpsel[:, :], reads=[tmpsel.r()],
            writes=[s01b.r()], dbg=f"{t}s01b")
        def routing_a():
            mm(PB[2][:, 0:NE], sb_tstrict[:, :], s01b[:, :],
               [sb_tstrict.r(), s01b.r()], [PB[2].r(fs=(0, NE))],
               start=True, stop=True, dbg=f"{t}carry")
            act(carrys[:, :], PB[2][:, 0:NE], AF.Copy, [PB[2].r(fs=(0, NE))],
                [carrys.r()], dbg=f"{t}carrye")
            mm(PB[3][0:1, 0:NE], sb_onescol[:, :], s01b[:, :],
               [sb_onescol.r(), s01b.r()], [PB[3].r(ps=(0, 1), fs=(0, NE))],
               start=True, stop=True, dbg=f"{t}tot")
            act(tot_sb[0:1, :], PB[3][0:1, 0:NE], AF.Copy,
                [PB[3].r(ps=(0, 1), fs=(0, NE))], [tot_sb.r()], dbg=f"{t}tote")
            dma(SP, DR["cnt_st"].h.rearrange("s e -> (s e)"), tot_sb[0:1, :],
                [tot_sb.r()], [DR["cnt_st"].r()], dbg=f"{t}st_cnt")
        def routing_b():
            em.op(POOL, (lambda e: e.collective_compute(
                "AllGather", ALU.bypass, RG_ALL,
                ins=[DR["cnt_st"].h[:, :]], outs=[DR["cnt_all"].h[:, :]])),
                reads=[DR["cnt_st"].r()], writes=[DR["cnt_all"].r()], kind="coll",
                dbg=f"{t}ag_cnt")
            dma(SP, call_sb[:, :], DR["cnt_all"].h, [DR["cnt_all"].r()],
                [call_sb.r()], dbg=f"{t}ld_cnt")
            vop("tensor_copy", callb_sb[:, :], call_sb[:, :], reads=[call_sb.r()],
                writes=[callb_sb.r()], dbg=f"{t}cntb")
            mm(PB[4][0:NSLOT, 0:E], sb_prefm[:, :], callb_sb[:, :],
               [sb_prefm.r(), callb_sb.r()], [PB[4].r(ps=(0, NSLOT), fs=(0, E))],
               start=True, stop=True, dbg=f"{t}offs")
            act(offs_sb[:, :], PB[4][0:NSLOT, 0:E], AF.Copy,
                [PB[4].r(ps=(0, NSLOT), fs=(0, E))], [offs_sb.r()], dbg=f"{t}offse")
            dma(SP, DR["offrow"].h.rearrange("o (s e) -> o s e", s=NSLOT),
                offs_sb[:, :], [offs_sb.r()], [DR["offrow"].r()], dbg=f"{t}st_off")
        def routing_c():
            dma(SP, offr_sb[:, :], DR["offrow"].h, [DR["offrow"].r()],
                [offr_sb.r()], dbg=f"{t}ld_off")
            mm(PB[5][:, 0:NE], sb_onesrow[:, :], offr_sb[0:1, :],
               [sb_onesrow.r(), offr_sb.r()], [PB[5].r(fs=(0, NE))],
               start=True, stop=True, dbg=f"{t}offb")
            vop("tensor_tensor", sumall[:, :], carrys[:, :], PB[5][:, 0:NE],
                ALU.add, reads=[carrys.r(), PB[5].r(fs=(0, NE))],
                writes=[sumall.r()], dbg=f"{t}sumall")
            vop("tensor_tensor", sumall1[:, :], sumall[:, :], O0[:, :], ALU.add,
                reads=[sumall.r(), O0.r()], writes=[sumall1.r()], dbg=f"{t}sumall1")
            for k, Ok, SS in ((0, O0, sumall), (1, O1, sumall1)):
                vop("tensor_tensor", tmpsel[:, :], SS[:, :], Ok[:, :], ALU.mult,
                    reads=[SS.r(), Ok.r()], writes=[tmpsel.r()], dbg=f"{t}sel{k}")
                vop("tensor_reduce",
                    pos01[:, k * NSLOT:(k + 1) * NSLOT],
                    tmpsel.h[:, :].rearrange("p (s e) -> p s e", e=E),
                    axis=AX.X, op=ALU.add,
                    reads=[tmpsel.r()],
                    writes=[pos01.r(fs=(k * NSLOT, (k + 1) * NSLOT))],
                    dbg=f"{t}pos{k}")
            vop("tensor_scalar", tmpsel[:, 0:2 * NSLOT], pos01[:, :], float(Ccap),
                None, ALU.is_lt, reads=[pos01.r()],
                writes=[tmpsel.r(fs=(0, 2 * NSLOT))], dbg=f"{t}win")
            vop("tensor_tensor", ve01[:, :], tmpsel[:, 0:2 * NSLOT], v01[:, :],
                ALU.mult, reads=[tmpsel.r(fs=(0, 2 * NSLOT)), v01.r()],
                writes=[ve01.r()], dbg=f"{t}ve")
            for k, Ok in ((0, O0), (1, O1)):
                for srcb, dstb, nm in ((iota_rk_sb, rank01, "rk"),
                                       (iota_el_sb, el01, "el")):
                    vop("tensor_tensor", pr2[:, :], srcb[:, :], Ok[:, :], ALU.mult,
                        reads=[srcb.r(), Ok.r()], writes=[pr2.r()],
                        dbg=f"{t}{nm}m{k}")
                    vop("tensor_reduce", dstb[:, k * NSLOT:(k + 1) * NSLOT],
                        pr2.h[:, :].rearrange("p (s e) -> p s e", e=E),
                        axis=AX.X, op=ALU.add, reads=[pr2.r()],
                        writes=[dstb.r(fs=(k * NSLOT, (k + 1) * NSLOT))],
                        dbg=f"{t}{nm}r{k}")
            vop("scalar_tensor_tensor", dstf[:, :], i01[:, :], float(Ccap),
                pos01[:, :], ALU.mult, ALU.add,
                reads=[i01.r(), pos01.r()], writes=[dstf.r()], dbg=f"{t}dst0")
            vop("tensor_tensor", dstf[:, :], dstf[:, :], tmpsel[:, 0:2 * NSLOT],
                ALU.mult, reads=[dstf.r(), tmpsel.r(fs=(0, 2 * NSLOT))],
                writes=[dstf.r()], dbg=f"{t}dst1")
            vop("scalar_tensor_tensor", dloc[:, :], rank01[:, :], float(Ccap),
                pos01[:, :], ALU.mult, ALU.add,
                reads=[rank01.r(), pos01.r()], writes=[dloc.r()], dbg=f"{t}dl0")
            vop("tensor_tensor", dloc[:, :], dloc[:, :], tmpsel[:, 0:2 * NSLOT],
                ALU.mult, reads=[dloc.r(), tmpsel.r(fs=(0, 2 * NSLOT))],
                writes=[dloc.r()], dbg=f"{t}dl1")
            vop("tensor_copy", dsti[:, :], dloc[:, :], reads=[dloc.r()],
                writes=[dsti.r()], dbg=f"{t}dsti")
            vop("tensor_scalar", hselw[:, :], pos01[:, :], 1536.0, None,
                ALU.is_lt, reads=[pos01.r()], writes=[hselw.r()], dbg=f"{t}hsel")
            vop("scalar_tensor_tensor", dtmp[:, :], rank01[:, :], 1536.0,
                pos01[:, :], ALU.mult, ALU.add,
                reads=[rank01.r(), pos01.r()], writes=[dtmp.r()], dbg=f"{t}d1a0")
            vop("tensor_tensor", dtmp[:, :], dtmp[:, :], tmpsel[:, 0:2 * NSLOT],
                ALU.mult, reads=[dtmp.r(), tmpsel.r(fs=(0, 2 * NSLOT))],
                writes=[dtmp.r()], dbg=f"{t}d1a1")
            vop("tensor_tensor", dtmp[:, :], dtmp[:, :], hselw[:, :],
                ALU.mult, reads=[dtmp.r(), hselw.r()],
                writes=[dtmp.r()], dbg=f"{t}d1a2")
            vop("tensor_copy", dsti1a[:, :], dtmp[:, :], reads=[dtmp.r()],
                writes=[dsti1a.r()], dbg=f"{t}d1ai")
            vop("scalar_tensor_tensor", dtmp[:, :], rank01[:, :], 1024.0,
                pos01[:, :], ALU.mult, ALU.add,
                reads=[rank01.r(), pos01.r()], writes=[dtmp.r()], dbg=f"{t}d1b0")
            vop("tensor_scalar", dtmp[:, :], dtmp[:, :], -1536.0, 0.0,
                ALU.add, ALU.max, reads=[dtmp.r()], writes=[dtmp.r()],
                dbg=f"{t}d1b1")
            vop("tensor_tensor", dtmp[:, :], dtmp[:, :], tmpsel[:, 0:2 * NSLOT],
                ALU.mult, reads=[dtmp.r(), tmpsel.r(fs=(0, 2 * NSLOT))],
                writes=[dtmp.r()], dbg=f"{t}d1b2")
            vop("tensor_copy", dsti1b[:, :], dtmp[:, :], reads=[dtmp.r()],
                writes=[dsti1b.r()], dbg=f"{t}d1bi")
            vop("tensor_scalar", dtmp[:, :], el01[:, :], -1.0, 1.0,
                ALU.mult, ALU.add, reads=[el01.r()], writes=[dtmp.r()],
                dbg=f"{t}el0s")
            vop("tensor_tensor", ve0w[:, :], ve01[:, :], dtmp[:, :], ALU.mult,
                reads=[ve01.r(), dtmp.r()], writes=[ve0w.r()], dbg=f"{t}ve0w")
            vop("tensor_tensor", ve1w[:, :], ve01[:, :], el01[:, :], ALU.mult,
                reads=[ve01.r(), el01.r()], writes=[ve1w.r()], dbg=f"{t}ve1w")
            vop("tensor_tensor", ve1aw[:, :], ve1w[:, :], hselw[:, :], ALU.mult,
                reads=[ve1w.r(), hselw.r()], writes=[ve1aw.r()], dbg=f"{t}ve1aw")
            vop("tensor_tensor", ve1bw[:, :], ve1w[:, :], ve1aw[:, :],
                ALU.subtract, reads=[ve1w.r(), ve1aw.r()], writes=[ve1bw.r()],
                dbg=f"{t}ve1bw")
            vop("tensor_scalar", tmpsel[:, 0:2 * NSLOT], tmpsel[:, 0:2 * NSLOT],
                -1.0, 1.0, ALU.mult, ALU.add,
                reads=[tmpsel.r(fs=(0, 2 * NSLOT))],
                writes=[tmpsel.r(fs=(0, 2 * NSLOT))], dbg=f"{t}wneg")
            vop("tensor_scalar", tmpsel[:, 0:2 * NSLOT], tmpsel[:, 0:2 * NSLOT],
                bld.bufs["trash_col"][:, 0:1], None, ALU.mult,
                reads=[tmpsel.r(fs=(0, 2 * NSLOT)), bld.bufs["trash_col"].r()],
                writes=[tmpsel.r(fs=(0, 2 * NSLOT))], dbg=f"{t}wtr")
            vop("tensor_tensor", tmpsel[:, 0:2 * NSLOT], tmpsel[:, 0:2 * NSLOT],
                dstf[:, :], ALU.add,
                reads=[tmpsel.r(fs=(0, 2 * NSLOT)), dstf.r()],
                writes=[tmpsel.r(fs=(0, 2 * NSLOT))], dbg=f"{t}dsc")
            vop("tensor_copy", dsti2[:, :], tmpsel[:, 0:2 * NSLOT],
                reads=[tmpsel.r(fs=(0, 2 * NSLOT))], writes=[dsti2.r()],
                dbg=f"{t}dsti2")
        def routing_d():
            nzc = (E * Ccap + 128) // 128
            dma(POOL, DR["idxmap"].h.rearrange("(p c) o -> p c o", p=128),
                zf[:, 0:nzc], [zf.r(fs=(0, nzc))], [DR["idxmap"].r()],
                dbg=f"{t}zidx")
            for k in range(2):
                for s in range(NSLOT):
                    col = k * NSLOT + s
                    em.op(POOL, (lambda e, col=col, s=s: e.indirect_dma_start(
                        DR["idxmap"].h[:, :],
                        IndirectOffsetOnAxis(dsti2[:, col:col + 1], 0),
                        sb_rowvals[:, s:s + 1], None)),
                        reads=[dsti2.r(fs=(col, col + 1)),
                               sb_rowvals.r(fs=(s, s + 1))],
                        writes=[DR["idxmap"].r()], kind="dma", dbg=f"{t}sct{k}_{s}")
            em.op(POOL, (lambda e: e.collective_compute(
                "AllReduce", ALU.add, RG_ALL,
                ins=[DR["idxmap"].h[:, :]], outs=[DR["idxmap_red"].h[:, :]])),
                reads=[DR["idxmap"].r()], writes=[DR["idxmap_red"].r()],
                kind="coll", dbg=f"{t}ar_idx")
        def mlp_chunk(tag, rc, rcw_, lhs_gu, rhs_act, wd_get, out_add):
            ntt_ = rcw_ // 128
            for ft in range(NF):
                pg = PB[ft % 2]
                pu = PB[2 + ft % 2]
                kg = (tag, "g", rc, ft)
                ku = (tag, "u", rc, ft)
                for dt in range(ND):
                    la, lr = lhs_gu(0, dt, ft)
                    ra, rr = rhs_act(dt, rc)
                    mm(pg[:, 0:rcw_], la, ra, [lr, rr], [pg.r(fs=(0, rcw_))],
                       start=(dt == 0), stop=(dt == ND - 1), key=kg,
                       dbg=f"{tag}g{rc}_{ft}_{dt}")
                    la2, lr2 = lhs_gu(1, dt, ft)
                    mm(pu[:, 0:rcw_], la2, ra, [lr2, rr], [pu.r(fs=(0, rcw_))],
                       start=(dt == 0), stop=(dt == ND - 1), key=ku,
                       dbg=f"{tag}u{rc}_{ft}_{dt}")
                gur = gu_chunk.r(fs=(ft * 512, ft * 512 + rcw_))
                act(gu_chunk[:, ft * 512:ft * 512 + rcw_], pg[:, 0:rcw_],
                    AF.Silu, [pg.r(fs=(0, rcw_))], [gur], dbg=f"{tag}sil{rc}_{ft}")
                vop("tensor_tensor", gu_chunk[:, ft * 512:ft * 512 + rcw_],
                    gu_chunk[:, ft * 512:ft * 512 + rcw_], pu[:, 0:rcw_],
                    ALU.mult, reads=[gur, pu.r(fs=(0, rcw_))], writes=[gur],
                    dbg=f"{tag}gu{rc}_{ft}")
            for tt in range(ntt_):
                for dc in range(D // DCW):
                    pd = PB[4 + (tt * (D // DCW) + dc) % 2]
                    kd = (tag, "d", rc, tt, dc)
                    for ft in range(NF):
                        wda, wdr = wd_get(ft, dc)
                        mm(pd[:, 0:DCW],
                           gu_chunk[:, ft * 512 + tt * 128:ft * 512 + (tt + 1) * 128],
                           wda,
                           [gu_chunk.r(fs=(ft * 512 + tt * 128,
                                           ft * 512 + (tt + 1) * 128)), wdr],
                           [pd.r(fs=(0, DCW))], start=(ft == 0),
                           stop=(ft == NF - 1), key=kd,
                           dbg=f"{tag}d{rc}_{tt}_{dc}_{ft}")
                    out_add(rc, tt, dc, pd[:, 0:DCW], pd.r(fs=(0, DCW)))

        swgu_sb = bld.sb(t + "swgu", (128, 4 * ND * 128), BF16)
        swd_sb = bld.sb(t + "swd", (128, NF * D), BF16)
        dma(POOL, swd_sb[:, :], DR["swd_bf"].h.rearrange("(t p) c -> p t c", p=128),
            [DR["swd_bf"].r()], [swd_sb.r()], dbg=f"{t}ld_swd")
        sw_loaded = {}

        def sh_lhs(gi, dt, ft):
            sl = (2 * (ft % 2) + gi) * ND * 128
            if (gi, ft) not in sw_loaded:
                srcb = DR["swg_bf"] if gi == 0 else DR["swu_bf"]
                dma(POOL, swgu_sb[:, sl:sl + ND * 128],
                    srcb.h.rearrange("(t p) c -> p t c", p=128)
                    [:, :, ft * 128:(ft + 1) * 128],
                    [srcb.r(fs=(ft * 128, (ft + 1) * 128))],
                    [swgu_sb.r(fs=(sl, sl + ND * 128))], dbg=f"{t}ldsw{gi}_{ft}")
                sw_loaded[(gi, ft)] = True
            off = sl + dt * 128
            return (swgu_sb[:, off:off + 128], swgu_sb.r(fs=(off, off + 128)))

        def sh_rhs(dt, rc):
            return (xT[:, dt * NT + rc * TCW:dt * NT + (rc + 1) * TCW],
                    xT.r(fs=(dt * NT + rc * TCW, dt * NT + (rc + 1) * TCW)))

        def sh_wd(ft, dc):
            return (swd_sb[:, ft * D + dc * DCW:ft * D + (dc + 1) * DCW],
                    swd_sb.r(fs=(ft * D + dc * DCW, ft * D + (dc + 1) * DCW)))

        def sh_out(rc, tt, dc, pap, prd):
            s = (rc * TCW) // 128 + tt
            fs = (s * D + dc * DCW, s * D + (dc + 1) * DCW)
            vop("tensor_tensor", x_res[:, fs[0]:fs[1]], x_res[:, fs[0]:fs[1]],
                pap, ALU.add, reads=[x_res.r(fs=fs), prd],
                writes=[x_res.r(fs=fs)], dbg=f"{t}shadd{rc}_{tt}_{dc}")

        def routing_bc():
            routing_b()
            routing_c()
        routing_cb = {0: routing_a, 1: routing_bc, 2: routing_d}
        for rc in range(NT // TCW):
            sw_loaded.clear()
            mlp_chunk(t + "sh", rc, TCW, sh_lhs, sh_rhs, sh_wd, sh_out)
            if rc in routing_cb:
                routing_cb[rc]()
        em.op(POOL, (lambda e: e.indirect_dma_start(
            gv[:, :], None, DR["idxmap_red"].h[:, :],
            IndirectOffsetOnAxis(myblk_sb[:, :], 0))),
            reads=[DR["idxmap_red"].r(), myblk_sb.r()], writes=[gv.r()],
            kind="dma", dbg=f"{t}myblk")
        vop("tensor_scalar", gv[:, :], gv[:, :], -1.0, 0.0, ALU.add, ALU.max,
            reads=[gv.r()], writes=[gv.r()], dbg=f"{t}gm1")
        em.op(PE, (lambda e: e.transpose(PB[5][:, 0:NRW], gv[:, :],
                                         sb_identf[0:NRW, 0:NRW])),
              reads=[gv.r(), sb_identf.r()],
              writes=[PB[5].r(fs=(0, NRW))], dbg=f"{t}gtr")
        vop("tensor_copy", gidx[:, :], PB[5][:, 0:NRW],
            reads=[PB[5].r(fs=(0, NRW))], writes=[gidx.r()], dbg=f"{t}gidx")
        eib = bld.sb(t + "eib", (128, 2 * D), BF16)
        for i in range(NRW):
            sl = (i % 2) * D
            em.op(POOL, (lambda e, i=i, sl=sl: e.indirect_dma_start(
                eib[:, sl:sl + D], None,
                DR["xn_all"].h[:, :],
                IndirectOffsetOnAxis(gidx[:, i:i + 1], 0))),
                reads=[DR["xn_all"].r(), gidx.r(fs=(i, i + 1))],
                writes=[eib.r(fs=(sl, sl + D))],
                kind="dma", dbg=f"{t}gei{i}")
            dma(SP, DR["ei_hbm"].h[i * 128:(i + 1) * 128, :], eib[:, sl:sl + D],
                [eib.r(fs=(sl, sl + D))],
                [DR["ei_hbm"].r(ps=(i * 128, (i + 1) * 128))],
                dbg=f"{t}sei{i}")

        em.barrier()
        bld.free(eib, swd_sb, swgu_sb)
        bld.free(tmpsel, sumall1, sumall, carrys, s01b, O1, O0, pr2, pr)
        bld.free(xT)
        wgwu_sb = bld.sb(t + "wgwu", (128, 2 * ND * F), BF16)
        wdx_sb = bld.sb(t + "wdx", (128, NF * D), BF16)
        eiT = bld.sb(t + "eiT", (128, 2 * ND * RCW), BF16)

        def emit_eo_ag(el):
            em.op(POOL, (lambda e, el=el: e.collective_compute(
                "AllGather", ALU.bypass, RG_ALL,
                ins=[DR["eo_mine"].h[el * Ccap:(el + 1) * Ccap, :]],
                outs=[DR[f"eo_all{el}"].h[:, :]])),
                reads=[DR["eo_mine"].r(ps=(el * Ccap, (el + 1) * Ccap))],
                writes=[DR[f"eo_all{el}"].r()], kind="coll",
                dbg=f"{t}ag_eo{el}")
        for el in range(EPC):
            dma(POOL, wgwu_sb[:, 0:ND * F],
                DR["wg_bf"].h[el].rearrange("(t p) c -> p t c", p=128),
                [DR["wg_bf"].r(ps=(el, el + 1))], [wgwu_sb.r(fs=(0, ND * F))],
                dbg=f"{t}ldwg{el}")
            dma(POOL, wgwu_sb[:, ND * F:2 * ND * F],
                DR["wu_bf"].h[el].rearrange("(t p) c -> p t c", p=128),
                [DR["wu_bf"].r(ps=(el, el + 1))],
                [wgwu_sb.r(fs=(ND * F, 2 * ND * F))], dbg=f"{t}ldwu{el}")
            dma(POOL, wdx_sb[:, :],
                DR["wd_bf"].h[el].rearrange("(t p) c -> p t c", p=128),
                [DR["wd_bf"].r(ps=(el, el + 1))], [wdx_sb.r()],
                dbg=f"{t}ldwd{el}")
            if el > 0:
                emit_eo_ag(el - 1)
            def ex_lhs(gi, dt, ft, el=el):
                off = gi * ND * F + dt * F + ft * 128
                return (wgwu_sb[:, off:off + 128],
                        wgwu_sb.r(fs=(off, off + 128)))

            def ex_rhs(dt, rc, el=el):
                sl = (rc % 2) * ND * RCW + dt * RCW
                return (eiT[:, sl:sl + RCW], eiT.r(fs=(sl, sl + RCW)))

            def ex_wd(ft, dc, el=el):
                return (wdx_sb[:, ft * D + dc * DCW:ft * D + (dc + 1) * DCW],
                        wdx_sb.r(fs=(ft * D + dc * DCW, ft * D + (dc + 1) * DCW)))

            def ex_out(rc, tt, dc, pap, prd, el=el):
                r0 = rc * RCW + tt * 128
                act(eo_sb[:, dc * DCW:(dc + 1) * DCW], pap, AF.Copy, [prd],
                    [eo_sb.r(fs=(dc * DCW, (dc + 1) * DCW))],
                    dbg=f"{t}eoe{el}_{rc}_{tt}_{dc}")
                if dc == D // DCW - 1:
                    dma(SP, DR["eo_mine"].h[el * Ccap + r0:el * Ccap + r0 + 128, :],
                        eo_sb[:, 0:D], [eo_sb.r(fs=(0, D))],
                        [DR["eo_mine"].r(ps=(el * Ccap + r0, el * Ccap + r0 + 128))],
                        dbg=f"{t}eost{el}_{rc}_{tt}")

            for rc in range(NRC):
                sl = (rc % 2) * ND * RCW
                for dt in range(ND):
                    dma(SP, eiT[:, sl + dt * RCW:sl + (dt + 1) * RCW],
                        DR["ei_hbm"].h[el * Ccap + rc * RCW:el * Ccap + (rc + 1) * RCW,
                                       dt * 128:(dt + 1) * 128],
                        [DR["ei_hbm"].r(ps=(el * Ccap + rc * RCW,
                                            el * Ccap + (rc + 1) * RCW))],
                        [eiT.r(fs=(sl + dt * RCW, sl + (dt + 1) * RCW))],
                        dbg=f"{t}eit{el}_{rc}_{dt}", transpose=True)
                mlp_chunk(t + f"ex{el}", rc, RCW, ex_lhs, ex_rhs, ex_wd, ex_out)
                if el == EPC - 1 and rc == 2:
                    em.op(POOL, (lambda e: e.collective_compute(
                        "AllGather", ALU.bypass, RG_ALL,
                        ins=[DR["eo_mine"].h[(EPC - 1) * Ccap:(EPC - 1) * Ccap + 1536, :]],
                        outs=[DR["eo_all1a"].h[:, :]])),
                        reads=[DR["eo_mine"].r(ps=((EPC - 1) * Ccap,
                                                   (EPC - 1) * Ccap + 1536))],
                        writes=[DR["eo_all1a"].r()], kind="coll",
                        dbg=f"{t}ag_eo1a")
        em.op(POOL, (lambda e: e.collective_compute(
            "AllGather", ALU.bypass, RG_ALL,
            ins=[DR["eo_mine"].h[(EPC - 1) * Ccap + 1536:EPC * Ccap, :]],
            outs=[DR["eo_all1b"].h[:, :]])),
            reads=[DR["eo_mine"].r(ps=((EPC - 1) * Ccap + 1536, EPC * Ccap))],
            writes=[DR["eo_all1b"].r()], kind="coll", dbg=f"{t}ag_eo1b")
        em.barrier()
        bld.free(eiT, wdx_sb, wgwu_sb)
        for s in range(NSLOT):
            for k, rb_, rb2_, rb3_ in ((0, ret0, ret2, ret4),
                                       (1, ret1, ret3, ret5)):
                col = k * NSLOT + s
                em.op(POOL, (lambda e, col=col, rb_=rb_: e.indirect_dma_start(
                    rb_[:, :], None, DR["eo_all0"].h[:, :],
                    IndirectOffsetOnAxis(dsti[:, col:col + 1], 0))),
                    reads=[DR["eo_all0"].r(), dsti.r(fs=(col, col + 1))],
                    writes=[rb_.r()], kind="dma", dbg=f"{t}ret{k}_{s}")
                em.op(POOL, (lambda e, col=col, rb2_=rb2_: e.indirect_dma_start(
                    rb2_[:, :], None, DR["eo_all1a"].h[:, :],
                    IndirectOffsetOnAxis(dsti1a[:, col:col + 1], 0))),
                    reads=[DR["eo_all1a"].r(), dsti1a.r(fs=(col, col + 1))],
                    writes=[rb2_.r()], kind="dma", dbg=f"{t}reta{k}_{s}")
                em.op(POOL, (lambda e, col=col, rb3_=rb3_: e.indirect_dma_start(
                    rb3_[:, :], None, DR["eo_all1b"].h[:, :],
                    IndirectOffsetOnAxis(dsti1b[:, col:col + 1], 0))),
                    reads=[DR["eo_all1b"].r(), dsti1b.r(fs=(col, col + 1))],
                    writes=[rb3_.r()], kind="dma", dbg=f"{t}retb{k}_{s}")
            xfs = (s * D, (s + 1) * D)
            for nm, rbx, wx, col in (("rv0", ret0, ve0w, s),
                                     ("rva", ret2, ve1aw, s),
                                     ("rvb", ret4, ve1bw, s),
                                     ("rv1", ret1, ve0w, NSLOT + s),
                                     ("rv1a", ret3, ve1aw, NSLOT + s),
                                     ("rv1b", ret5, ve1bw, NSLOT + s)):
                vop("scalar_tensor_tensor", x_res[:, xfs[0]:xfs[1]],
                    rbx[:, :], wx[:, col:col + 1], x_res[:, xfs[0]:xfs[1]],
                    ALU.mult, ALU.add,
                    reads=[rbx.r(), wx.r(fs=(col, col + 1)),
                           x_res.r(fs=xfs)],
                    writes=[x_res.r(fs=xfs)], dbg=f"{t}{nm}_{s}")
        em.barrier()
        bld.free(zf, retf, ret5, ret4, ret3, ret2, ret1, ret0, eo_sb, gu_chunk)

    if "ffn1" in stages:
        emit_ffn(0)

    if "attn_b" in stages:
        x_res_b = HV["x"]
        hd = cfg.hdB
        HPT = 128 // hd
        NHT = cfg.HB // HPT
        VBW = cfg.HB * (hd + 1)
        scale_b = 1.0 / math.sqrt(hd)
        xT = bld.sb("b_xT", (128, ND * NT), BF16)
        xh = bld.sb("b_xh", (128, 2 * D), BF16)
        xh32 = bld.sb("b_xh32", (128, 2 * D), F32)
        sq_scr = bld.sb("b_sq", (128, 2 * D), BF16)
        for s in range(NSLOT):
            rot = s % 2
            layernorm(x_res_b, s, None, None, xh[:, rot * D:(rot + 1) * D],
                      [xh.r(fs=(rot * D, (rot + 1) * D))], rot=rot,
                      dbg=f"b_ln{s}", sq_scr=sq_scr)
            for dt in range(ND):
                po = (dt % 8) * 128
                PTX = PTB[dt % 2]
                transpose(PTX[:, po:po + 128],
                          xh[:, rot * D + dt * 128:rot * D + (dt + 1) * 128],
                          [xh.r(fs=(rot * D + dt * 128, rot * D + (dt + 1) * 128))],
                          [PTX.r(fs=(po, po + 128))], dbg=f"b_xt{s}_{dt}")
                vop("tensor_copy",
                    xT[:, dt * NT + s * 128:dt * NT + (s + 1) * 128],
                    PTX[:, po:po + 128], reads=[PTX.r(fs=(po, po + 128))],
                    writes=[xT.r(fs=(dt * NT + s * 128, dt * NT + (s + 1) * 128))],
                    dbg=f"b_xte{s}_{dt}")
        dma(SP, DR["spill_x"].h.rearrange("(s p) d -> p s d", p=128),
            x_res_b[:, :], [x_res_b.r()], [DR["spill_x"].r()], dbg="b_spx")
        dma(SP, DR["xt_spill"].h.rearrange("(t p) c -> p t c", p=128),
            xT[:, :], [xT.r()], [DR["xt_spill"].r()], dbg="b_spxt")
        em.barrier()
        bld.free(sq_scr, xh32, xh, xT, x_res_b)
        xT2 = bld.sb("b_xT2", (128, ND * NT), BF16)
        dma(SP, xT2[:, :], DR["xt_spill"].h.rearrange("(t p) c -> p t c", p=128),
            [DR["xt_spill"].r()], [xT2.r()], dbg="b_ldxt")
        wT_sb = bld.sb("b_wT", (128, 2 * ND * D), BF16)
        pT_sb = bld.sb("b_pT", (128, 2 * ND * NT), BF16)
        rope_c = bld.sb("b_ropec", (128, NT), BF16)
        rope_s = bld.sb("b_ropes", (128, NT), BF16)
        rope_t = bld.sb("b_ropet", (128, 2 * 512), BF16)
        vstg = bld.sb("b_vstg", (128, 2 * 512), BF16)
        dma(SP, rope_c[:, :], CN["ropecos"].h, [CN["ropecos"].r()],
            [rope_c.r()], dbg="b_ldrc")
        dma(SP, rope_s[:, :], CN["ropesin"].h, [CN["ropesin"].r()],
            [rope_s.r()], dbg="b_ldrs")
        ropeperm_sb = bld.sb("b_rpm", (128, 128), BF16)
        dma(SP, ropeperm_sb[:, :], CN["ropeperm"].h, [CN["ropeperm"].r()],
            [ropeperm_sb.r()], dbg="b_ldrp")

        def load_wT(wname, dbg, wb):
            wo0 = wb * ND * D
            for jt in range(ND):
                dma(SP, wT_sb[:, wo0 + jt * D:wo0 + (jt + 1) * D],
                    DR[wname].h[:, jt * 128:(jt + 1) * 128],
                    [DR[wname].r()], [wT_sb.r(fs=(wo0 + jt * D, wo0 + (jt + 1) * D))],
                    dbg=f"{dbg}{jt}", transpose=True)

        def project_fm(tag, wb, pbx):
            wo0 = wb * ND * D
            po0 = pbx * ND * NT
            for it in range(ND):
                for tc in range(NTCH):
                    pb = PB[(it * NTCH + tc) % 2]
                    key = (tag, it, tc)
                    for jt in range(ND):
                        mm(pb[:, 0:TCH],
                           wT_sb[:, wo0 + jt * D + it * 128:wo0 + jt * D + (it + 1) * 128],
                           xT2[:, jt * NT + tc * TCH:jt * NT + (tc + 1) * TCH],
                           [wT_sb.r(fs=(wo0 + jt * D + it * 128, wo0 + jt * D + (it + 1) * 128)),
                            xT2.r(fs=(jt * NT + tc * TCH, jt * NT + (tc + 1) * TCH))],
                           [pb.r(fs=(0, TCH))], start=(jt == 0),
                           stop=(jt == ND - 1), key=key, dbg=f"{tag}_{it}_{tc}")
                    act(pT_sb[:, po0 + it * NT + tc * TCH:po0 + it * NT + (tc + 1) * TCH],
                        pb[:, 0:TCH], AF.Copy, [pb.r(fs=(0, TCH))],
                        [pT_sb.r(fs=(po0 + it * NT + tc * TCH, po0 + it * NT + (tc + 1) * TCH))],
                        dbg=f"{tag}e_{it}_{tc}")

        def rope_fm(tag, pbx):
            po0 = pbx * ND * NT
            for it in range(ND):
                for tc in range(NT // 512 if NT >= 512 else 1):
                    W512 = min(512, NT)
                    c0 = po0 + it * NT + tc * W512
                    pb = PB[2 + tc % 2]
                    mm(pb[:, 0:W512], ropeperm_sb[:, :],
                       pT_sb[:, c0:c0 + W512],
                       [ropeperm_sb.r(), pT_sb.r(fs=(c0, c0 + W512))],
                       [pb.r(fs=(0, W512))], start=True, stop=True,
                       dbg=f"{tag}r_{it}_{tc}")
                    sl = (tc % 2) * 512
                    vop("tensor_tensor", rope_t[:, sl:sl + W512],
                        pT_sb[:, c0:c0 + W512],
                        rope_c[:, tc * W512:tc * W512 + W512], ALU.mult,
                        reads=[pT_sb.r(fs=(c0, c0 + W512)),
                               rope_c.r(fs=(tc * W512, tc * W512 + W512))],
                        writes=[rope_t.r(fs=(sl, sl + W512))],
                        dbg=f"{tag}c_{it}_{tc}")
                    vop("tensor_tensor", pT_sb[:, c0:c0 + W512],
                        pb[:, 0:W512],
                        rope_s[:, tc * W512:tc * W512 + W512], ALU.mult,
                        reads=[pb.r(fs=(0, W512)),
                               rope_s.r(fs=(tc * W512, tc * W512 + W512))],
                        writes=[pT_sb.r(fs=(c0, c0 + W512))],
                        dbg=f"{tag}s_{it}_{tc}")
                    vop("tensor_tensor", pT_sb[:, c0:c0 + W512],
                        pT_sb[:, c0:c0 + W512], rope_t[:, sl:sl + W512],
                        ALU.add,
                        reads=[pT_sb.r(fs=(c0, c0 + W512)),
                               rope_t.r(fs=(sl, sl + W512))],
                        writes=[pT_sb.r(fs=(c0, c0 + W512))],
                        dbg=f"{tag}a_{it}_{tc}")

        load_wT("wqb_bf", "b_lwq", 0)
        load_wT("wkb_bf", "b_lwk", 1)
        project_fm("b_q", 0, 0)
        rope_fm("b_q", 0)
        dma(SP, DR["qtb_spill"].h.rearrange("(t p) c -> p t c", p=128),
            pT_sb[:, 0:ND * NT], [pT_sb.r(fs=(0, ND * NT))],
            [DR["qtb_spill"].r()], dbg="b_spq")
        project_fm("b_k", 1, 1)
        rope_fm("b_k", 1)
        dma(SP, DR["ktb_st"].h.rearrange("(t p) c -> p t c", p=128),
            pT_sb[:, ND * NT:2 * ND * NT], [pT_sb.r(fs=(ND * NT, 2 * ND * NT))],
            [DR["ktb_st"].r()], dbg="b_spk")
        load_wT("wvb_bf", "b_lwv", 0)
        for s in range(NSLOT):
            for dc in range(D // 512 if D >= 512 else 1):
                W512 = min(512, D)
                pb = PB[4 + dc % 2]
                key = ("b_v", s, dc)
                for jt in range(ND):
                    mm(pb[:, 0:W512],
                       xT2[:, jt * NT + s * 128:jt * NT + (s + 1) * 128],
                       wT_sb[:, jt * D + dc * W512:jt * D + (dc + 1) * W512],
                       [xT2.r(fs=(jt * NT + s * 128, jt * NT + (s + 1) * 128)),
                        wT_sb.r(fs=(jt * D + dc * W512, jt * D + (dc + 1) * W512))],
                       [pb.r(fs=(0, W512))], start=(jt == 0),
                       stop=(jt == ND - 1), key=key, dbg=f"b_v{s}_{dc}")
                sl = (dc % 2) * 512
                act(vstg[:, sl:sl + W512], pb[:, 0:W512], AF.Copy,
                    [pb.r(fs=(0, W512))], [vstg.r(fs=(sl, sl + W512))],
                    dbg=f"b_ve{s}_{dc}")
                dma(SP, DR["vb_st"].h[s * 128:(s + 1) * 128,
                                      dc * W512:(dc + 1) * W512],
                    vstg[:, sl:sl + W512], [vstg.r(fs=(sl, sl + W512))],
                    [DR["vb_st"].r(ps=(s * 128, (s + 1) * 128),
                                   fs=(dc * W512, (dc + 1) * W512))],
                    dbg=f"b_vst{s}_{dc}")
        em.op(POOL, (lambda e: e.collective_compute(
            "AllGather", ALU.bypass, RG_PAIRS,
            ins=[DR["ktb_st"].h[:, :]], outs=[DR["ktb_all"].h[:, :]])),
            reads=[DR["ktb_st"].r()], writes=[DR["ktb_all"].r()], kind="coll",
            dbg="b_ag_k")
        em.op(POOL, (lambda e: e.collective_compute(
            "AllGather", ALU.bypass, RG_PAIRS,
            ins=[DR["vb_st"].h[:, :]], outs=[DR["vb_all"].h[:, :]])),
            reads=[DR["vb_st"].r()], writes=[DR["vb_all"].r()], kind="coll",
            dbg="b_ag_v")
        em.barrier()
        bld.free(ropeperm_sb, vstg, rope_t, rope_s, rope_c, pT_sb, wT_sb, xT2)
        ktb_sb = bld.sb("b_K", (128, ND * 2 * NT), BF16)
        vb_sb = bld.sb("b_V", (128, 2 * NSLOT * VBW), BF16)
        wobT_sb = bld.sb("b_woT", (128, ND * D), BF16)
        qs_sb = bld.sb("b_qs", (128, 2 * ND * 512), BF16)
        Yb = bld.sb("b_Y", (128, 4 * D), BF16)
        ptb2 = [bld.sb(f"b_pt{i}", (128, 512), BF16) for i in range(3)]
        yT2 = bld.sb("b_yT", (128, ND * 128), BF16)
        xadd = bld.sb("b_xadd", (128, 2 * D), F32)
        vint = bld.sb("b_vint", (128, 2 * D), BF16)
        for jt in range(ND):
            dma(SP, wobT_sb[:, jt * D:(jt + 1) * D],
                DR["wob_bf"].h[:, jt * 128:(jt + 1) * 128],
                [DR["wob_bf"].r()], [wobT_sb.r(fs=(jt * D, (jt + 1) * D))],
                dbg=f"b_lwo{jt}", transpose=True)
        for dt in range(ND):
            for rank in range(2):
                dma(SP, ktb_sb[:, dt * 2 * NT + rank * NT:dt * 2 * NT + (rank + 1) * NT],
                    DR["ktb_all"].h[rank * D + dt * 128:rank * D + (dt + 1) * 128, :],
                    [DR["ktb_all"].r(ps=(rank * D + dt * 128,
                                         rank * D + (dt + 1) * 128))],
                    [ktb_sb.r(fs=(dt * 2 * NT + rank * NT,
                                  dt * 2 * NT + (rank + 1) * NT))],
                    dbg=f"b_ldk{dt}_{rank}")
        for jb in range(2 * NSLOT):
            rank, sl_ = jb // NSLOT, jb % NSLOT
            vs = (jb % 2) * D
            dma(SP, vint[:, vs:vs + D],
                DR["vb_all"].h[rank * NT + sl_ * 128:rank * NT + (sl_ + 1) * 128, :],
                [DR["vb_all"].r(ps=(rank * NT + sl_ * 128,
                                    rank * NT + (sl_ + 1) * 128))],
                [vint.r(fs=(vs, vs + D))], dbg=f"b_ldv{jb}")
            vop("tensor_copy",
                vb_sb.h[:, :].rearrange("p (j h c) -> p j h c", j=2 * NSLOT,
                                        h=cfg.HB)[:, jb, :, 0:hd],
                vint.h[:, :].rearrange("p (r h c) -> p r h c", r=2,
                                       h=cfg.HB)[:, jb % 2, :, :],
                reads=[vint.r(fs=(vs, vs + D))],
                writes=[vb_sb.r(fs=(jb * VBW, (jb + 1) * VBW))],
                dbg=f"b_vi{jb}")
            vop("memset",
                vb_sb.h[:, :].rearrange("p (j h c) -> p j h c", j=2 * NSLOT,
                                        h=cfg.HB)[:, jb, :, hd:hd + 1], 1.0,
                writes=[vb_sb.r(fs=(jb * VBW, (jb + 1) * VBW))],
                dbg=f"b_vo{jb}")
        HW2 = hd + 1
        NG = NSLOT // 4
        for t in range(NG):
            qsl = (t % 2) * ND * 512
            dma(SP, qs_sb[:, qsl:qsl + ND * 512],
                DR["qtb_spill"].h[:, t * 512:(t + 1) * 512]
                .rearrange("(t p) c -> p t c", p=128),
                [DR["qtb_spill"].r()],
                [qs_sb.r(fs=(qsl, qsl + ND * 512))], dbg=f"b_lq{t}")
            njt = 8 * t + 8
            for h in range(cfg.HB):
                dt = h // HPT
                base = (h % HPT) * hd
                av = PB[4 + h % 2]
                avk = ("avb", t, h)
                for j in range(njt + 1):
                    if j < njt:
                        jb = (j % 2) * NSLOT + j // 2
                        kcol = dt * 2 * NT + (j % 2) * NT + (j // 2) * 128
                        sc_b = PB[2 + j % 2]
                        jrel = j - 8 * t
                        mm(sc_b[:, 0:512],
                           ktb_sb[base:base + hd, kcol:kcol + 128],
                           qs_sb[base:base + hd, qsl + dt * 512:qsl + (dt + 1) * 512],
                           [ktb_sb.r(ps=(base, base + hd), fs=(kcol, kcol + 128)),
                            qs_sb.r(ps=(base, base + hd),
                                    fs=(qsl + dt * 512, qsl + (dt + 1) * 512))],
                           [sc_b.r(fs=(0, 512))], start=True, stop=(jrel < 0),
                           dbg=f"b_sc{t}_{h}_{j}")
                        if jrel >= 0:
                            r_ = jrel // 2
                            mp = jrel % 2
                            mm(sc_b[:, r_ * 128:(r_ + 1) * 128], sb_nbig[:, :],
                               sb_maskp[:, mp * 128:(mp + 1) * 128],
                               [sb_nbig.r(), sb_maskp.r(fs=(mp * 128, (mp + 1) * 128))],
                               [sc_b.r(fs=(r_ * 128, (r_ + 1) * 128))],
                               start=False, stop=True, dbg=f"b_ma{t}_{h}_{j}")
                        pt = ptb2[j % 3]
                        act(pt[:, :], sc_b[:, 0:512], AF.Exp,
                            [sc_b.r(fs=(0, 512)), sb_expbias.r()], [pt.r()],
                            bias=sb_expbias[:, 0:1], scale=scale_b,
                            dbg=f"b_ex{t}_{h}_{j}")
                    if j >= 1:
                        jj = j - 1
                        jb = (jj % 2) * NSLOT + jj // 2
                        pt = ptb2[jj % 3]
                        for r in range(4):
                            if jj > 8 * t + 2 * r + 1:
                                continue
                            ao = r * HW2
                            mm(av[:, ao:ao + HW2], pt[:, r * 128:(r + 1) * 128],
                               vb_sb[:, jb * VBW + h * HW2:jb * VBW + (h + 1) * HW2],
                               [pt.r(fs=(r * 128, (r + 1) * 128)),
                                vb_sb.r(fs=(jb * VBW + h * HW2,
                                            jb * VBW + (h + 1) * HW2))],
                               [av.r(fs=(ao, ao + HW2))],
                               start=(jj == 0 and r == 0),
                               stop=(jj == njt - 1 and r == 3),
                               key=avk, dbg=f"b_av{t}_{h}_{jj}_{r}")
                for r in range(4):
                    ao = r * HW2
                    rs = (h % 2) * 4 + r
                    rc = scr8[:, rs:rs + 1]
                    rcr = ("scr8", (0, 128, rs, rs + 1))
                    vop("reciprocal", rc, av[:, ao + hd:ao + hd + 1],
                        reads=[av.r(fs=(ao + hd, ao + hd + 1))], writes=[rcr],
                        dbg=f"b_rc{t}_{h}_{r}")
                    act(Yb[:, r * D + h * hd:r * D + (h + 1) * hd],
                        av[:, ao:ao + hd], AF.Copy,
                        [av.r(fs=(ao, ao + hd)), rcr],
                        [Yb.r(fs=(r * D + h * hd, r * D + (h + 1) * hd))],
                        scale=rc, dbg=f"b_ye{t}_{h}_{r}")
            DCW = min(512, D)
            for r in range(4):
                s = 4 * t + r
                for dh_t in range(ND):
                    po = (dh_t % 8) * 128
                    PTX = PTB[dh_t % 2]
                    transpose(PTX[:, po:po + 128],
                              Yb[:, r * D + dh_t * 128:r * D + (dh_t + 1) * 128],
                              [Yb.r(fs=(r * D + dh_t * 128, r * D + (dh_t + 1) * 128))],
                              [PTX.r(fs=(po, po + 128))], dbg=f"b_ytr{s}_{dh_t}")
                    vop("tensor_copy", yT2[:, dh_t * 128:(dh_t + 1) * 128],
                        PTX[:, po:po + 128], reads=[PTX.r(fs=(po, po + 128))],
                        writes=[yT2.r(fs=(dh_t * 128, (dh_t + 1) * 128))],
                        dbg=f"b_ytre{s}_{dh_t}")
                xsl = (s % 2) * D
                dma(SP, xadd[:, xsl:xsl + D],
                    DR["spill_x"].h[s * 128:(s + 1) * 128, :],
                    [DR["spill_x"].r(ps=(s * 128, (s + 1) * 128))],
                    [xadd.r(fs=(xsl, xsl + D))], dbg=f"b_lx{s}")
                for dc in range(D // DCW):
                    pb = PB[dc % 2]
                    key = ("ob", s, dc)
                    for dh_t in range(ND):
                        mm(pb[:, 0:DCW], yT2[:, dh_t * 128:(dh_t + 1) * 128],
                           wobT_sb[:, dh_t * D + dc * DCW:dh_t * D + (dc + 1) * DCW],
                           [yT2.r(fs=(dh_t * 128, (dh_t + 1) * 128)),
                            wobT_sb.r(fs=(dh_t * D + dc * DCW, dh_t * D + (dc + 1) * DCW))],
                           [pb.r(fs=(0, DCW))], start=(dh_t == 0),
                           stop=(dh_t == ND - 1), key=key, dbg=f"b_o{s}_{dc}")
                    vop("tensor_tensor",
                        xadd[:, xsl + dc * DCW:xsl + (dc + 1) * DCW],
                        xadd[:, xsl + dc * DCW:xsl + (dc + 1) * DCW],
                        pb[:, 0:DCW], ALU.add,
                        reads=[xadd.r(fs=(xsl + dc * DCW, xsl + (dc + 1) * DCW)),
                               pb.r(fs=(0, DCW))],
                        writes=[xadd.r(fs=(xsl + dc * DCW, xsl + (dc + 1) * DCW))],
                        dbg=f"b_oa{s}_{dc}")
                dma(SP, DR["spill_x"].h[s * 128:(s + 1) * 128, :],
                    xadd[:, xsl:xsl + D], [xadd.r(fs=(xsl, xsl + D))],
                    [DR["spill_x"].r(ps=(s * 128, (s + 1) * 128))],
                    dbg=f"b_sx{s}")
        em.barrier()
        bld.free(vint, xadd, yT2, *reversed(ptb2))
        bld.free(Yb, qs_sb, wobT_sb, vb_sb, ktb_sb)
        x_res2 = bld.sb("x_res2", (128, NSLOT * D), F32)
        HV["x"] = x_res2
        dma(SP, x_res2[:, :],
            DR["spill_x"].h.rearrange("(s p) d -> p s d", p=128),
            [DR["spill_x"].r()], [x_res2.r()], dbg="b_rex")
        em.barrier()

    if "ffn2" in stages:
        emit_ffn(1)

    x_out = HV["x"]
    dma(SP, out_t.h.rearrange("(s p) d -> p s d", p=128), x_out[:, :],
        [x_out.r()], [out_t.r()], dbg="store_out")
    em.barrier()
    em.flush()
    return nc


def kernel(**inputs):
    cfg = Cfg()
    nc = build_program(cfg)
    in_maps = shard_inputs(cfg, inputs)
    res = run_bass_kernel_spmd(nc, in_maps, core_ids=list(range(cfg.NCORES)))
    return assemble_output(cfg, [r for r in res.results])
